# version 14
# baseline (speedup 1.0000x reference)
import numpy as np

import concourse.bass as bass
import concourse.mybir as mybir
from concourse.bass_utils import run_bass_kernel_spmd

F32 = mybir.dt.float32
BF16 = mybir.dt.bfloat16
I32 = mybir.dt.int32
ALU = mybir.AluOpType
AF = mybir.ActivationFunctionType
AX = mybir.AxisListType

NCORE = 8
DIM = 2048
TOKC = 1024
NH = 16
D = 64
TWOH = 32
DI = 8192
LAM_INIT = 0.8 - 0.6 * float(np.exp(-0.3 * 12))
SCALE = D ** (-0.5)
EPS = 1e-5
QB = 128.0
C_FW = float(np.float32(1.0 / np.sqrt(DI)))
QT = 256


def _perm_qk():
    p = np.zeros(DIM, dtype=np.int64)
    for i in range(TWOH):
        pp, h = i // 16, i % 16
        for d in range(D):
            p[i * D + d] = h * 128 + 2 * d + pp
    return p


def _perm_v():
    p = np.zeros(DIM, dtype=np.int64)
    for e in range(128):
        for h in range(NH):
            p[e * NH + h] = h * 128 + e
    return p


def _perm_wo_rows():
    p = np.zeros(DIM, dtype=np.int64)
    for a in range(NH):
        for e in range(128):
            p[a + 16 * (e & 1) + 32 * (e >> 1)] = a * 128 + e
    return p


def _hadamard(n):
    h = np.array([[1.0]], dtype=np.float32)
    while h.shape[0] < n:
        h = np.block([[h, h], [h, -h]])
    return np.ascontiguousarray(h.astype(np.float32))


class Sem:
    def __init__(self, nc, name):
        self.h = nc.semaphore(name).__enter__()
        self.v = 0


def build_program():
    nc = bass.Bass("TRN2", target_bir_lowering=False)

    def din(name, shape, dt=F32):
        return nc.dram_tensor(name, list(shape), dt, kind="ExternalInput").ap()

    xT = din("xT", [DIM, TOKC])
    wqT = din("wqT", [DIM, DIM])
    wkT = din("wkT", [DIM, DIM])
    wvT = din("wvT", [DIM, DIM])
    woT = din("woT", [DIM, DIM])
    wgT = din("wgT", [DIM, DI])
    wuT = din("wuT", [DIM, DI])
    wdT = din("wdT", [DI, DIM])
    wmod = din("wmod", [DIM, 6 * DIM])
    cvec = din("cvec", [1, DIM])
    bo = din("bo", [1, DIM])
    lq1 = din("lq1", [1, D])
    lq2 = din("lq2", [1, D])
    lk1 = din("lk1", [1, D])
    lk2 = din("lk2", [1, D])
    eye = din("eye", [128, 128])
    had = din("had", [128, 128])
    onesc = din("onesc", [1, 128])
    onesp = din("onesp", [128, 1])

    y = nc.dram_tensor("y", [DIM, TOKC], F32, kind="ExternalOutput").ap()
    modscr = nc.dram_tensor("modscr", [1, 6 * DIM], F32, kind="Internal").ap()
    x2scr = nc.dram_tensor("x2scr", [DIM, TOKC], F32, kind="Internal").ap()
    vscr = nc.dram_tensor("vscr", [128, 8192], F32, kind="Internal").ap()
    qhscr = nc.dram_tensor("qhscr", [128, 8192], BF16, kind="Internal").ap()
    qlscr = nc.dram_tensor("qlscr", [128, 8192], BF16, kind="Internal").ap()
    khscr = nc.dram_tensor("khscr", [128, 8192], BF16, kind="Internal").ap()
    klscr = nc.dram_tensor("klscr", [128, 8192], BF16, kind="Internal").ap()
    tG = nc.dram_tensor("tG", [128, 64 * 2048], BF16, kind="Internal").ap()
    tU = nc.dram_tensor("tU", [128, 64 * 2048], BF16, kind="Internal").ap()
    tD = nc.dram_tensor("tD", [128, 64 * 2048], BF16, kind="Internal").ap()

    def sball(name, shape, dt=F32):
        return nc.sbuf_tensor(name, list(shape), dt).__enter__()

    AR1 = sball("AR1", [128, 16384])
    AR4 = sball("AR4", [128, 12288])
    BA = sball("BA", [128, 36864], BF16)
    ident = sball("ident", [128, 128])
    identb = sball("identb", [128, 128], BF16)
    hadt = sball("hadt", [128, 128])
    onescr = sball("onescr", [1, 128])
    onespc = sball("onespc", [128, 1])
    cfm = sball("cfm", [128, 16])
    bofm = sball("bofm", [128, 16])
    mvec = sball("mvec", [128, 96])

    lam1 = sball("lam1", [1, 16])
    ppt = sball("ppt", [128, 16])
    redp = sball("redp", [128, 200])
    TB2 = sball("TB2", [128, 4096], BF16)
    qt1 = sball("qt1", [128, 512])
    qt2 = sball("qt2", [128, 512])
    qt3 = sball("qt3", [128, 4096], mybir.dt.int16)

    pbk = [nc.psum_tensor(f"pbk{i}", [128, 512], F32).__enter__() for i in range(6)]
    pbkB = [nc.psum_tensor(f"pbkB{i}", [128, 1024], BF16).__enter__() for i in range(2)]

    lrow = AR4[0:1, 0:256]
    rowb = AR1[0:1, 0:6 * DIM]
    rowc = AR4[0:1, 0:6 * DIM]
    srow = AR4[0:1, 6144:7168]
    qrow = AR4[0:1, 7168:8192]
    arow = AR1[0:1, 12288:13312]
    brow = AR1[0:1, 13312:14336]
    a2row = AR1[0:1, 14336:15360]
    b2row = AR1[0:1, 15360:16384]

    sPE = Sem(nc, "sPE")
    sDV = Sem(nc, "sDV")
    sAC = Sem(nc, "sAC")

    class DPool:
        def __init__(self, name, n):
            self.sems = [Sem(nc, f"{name}{i}") for i in range(n)]
            self.i = 0

        def load(self, q, out, in_, waits=(), slow=False):
            s = self.sems[self.i % len(self.sems)]
            self.i += 1
            q.wait_ge(s.h, s.v)
            for h, v in waits:
                q.wait_ge(h, v)
            kw = dict(allow_slow_non_contiguous=True) if slow else {}
            q.dma_start(out=out, in_=in_, **kw).then_inc(s.h, 16)
            s.v += 16
            return (s.h, s.v)

        def toks(self):
            return [(s.h, s.v) for s in self.sems]

    PL = DPool("pl", 8)
    PO = DPool("po", 4)

    def f32v(ar, c0, n):
        return ar[:, c0:c0 + n]

    def bav(c0, n):
        return BA[:, c0:c0 + n]

    def fence():
        vals = ([(s.h, s.v) for s in (sPE, sDV, sAC) if s.v > 0]
                + [(h, v) for (h, v) in PL.toks() + PO.toks() if v > 0])
        with nc.Block() as blk:
            @blk.tensor
            def _(e):
                for h, v in vals:
                    e.wait_ge(h, v)

            @blk.vector
            def _(e):
                for h, v in vals:
                    e.wait_ge(h, v)

            @blk.scalar
            def _(e):
                for h, v in vals:
                    e.wait_ge(h, v)

            @blk.gpsimd
            def _(e):
                for h, v in vals:
                    e.wait_ge(h, v)

            @blk.sync
            def _(e):
                for h, v in vals:
                    e.wait_ge(h, v)

    with nc.Block() as blk:
        @blk.sync
        def _(q):
            for dst, src in [(ident[:], eye[:]), (hadt[:], had[:]),
                             (onescr[:], onesc[:]), (onespc[:], onesp[:]),
                             (lrow[:, 0:64], lq1[:]), (lrow[:, 64:128], lq2[:]),
                             (lrow[:, 128:192], lk1[:]), (lrow[:, 192:256], lk2[:])]:
                PL.load(q, dst, src)
            PL.load(q, cfm[:], cvec.rearrange("o (k p) -> (o p) k", p=128), slow=True)
            PL.load(q, bofm[:], bo.rearrange("o (k p) -> (o p) k", p=128), slow=True)

    fence()
    with nc.Block() as blk:
        @blk.vector
        def _(v):
            v.tensor_copy(identb[:], ident[:])
            v.tensor_tensor(lrow[:, 0:64], lrow[:, 0:64], lrow[:, 128:192], op=ALU.mult)
            v.drain()
            v.tensor_tensor(lrow[:, 64:128], lrow[:, 64:128], lrow[:, 192:256], op=ALU.mult)
            v.drain()
            v.tensor_reduce(lam1[:, 0:1], lrow[:, 0:64], axis=AX.X, op=ALU.add)
            v.drain()
            v.tensor_reduce(lam1[:, 1:2], lrow[:, 64:128], axis=AX.X,
                            op=ALU.add).then_inc(sDV.h, 1); sDV.v += 1

        @blk.scalar
        def _(s):
            s.wait_ge(sDV.h, sDV.v)
            s.activation(lam1[:, 2:4], lam1[:, 0:2], AF.Exp).then_inc(sAC.h, 1); sAC.v += 1

        @blk.vector
        def _(v):
            v.wait_ge(sAC.h, sAC.v)
            v.tensor_tensor(lam1[:, 4:5], lam1[:, 3:4], lam1[:, 2:3], op=ALU.subtract)
            v.drain()
            v.tensor_scalar(lam1[:, 4:5], lam1[:, 4:5], -LAM_INIT, None,
                            op0=ALU.add).then_inc(sDV.h, 1); sDV.v += 1

        @blk.tensor
        def _(t):
            t.wait_ge(sDV.h, sDV.v)
            t.matmul(pbk[3][:, 0:1], onescr[:], lam1[0:1, 4:5], start=True,
                     stop=True).then_inc(sPE.h, 1); sPE.v += 1

        @blk.vector
        def _(v):
            v.wait_ge(sPE.h, sPE.v)
            v.tensor_copy(ppt[:, 0:1], pbk[3][:, 0:1]).then_inc(sDV.h, 1); sDV.v += 1

    with nc.Block() as blk:
        @blk.scalar
        def _(s):
            s.activation(cfm[:], cfm[:], AF.Silu).then_inc(sAC.h, 1); sAC.v += 1

    pe0 = sPE.v
    dv0 = sDV.v
    wm_toks = []
    with nc.Block() as blk:
        @blk.sync
        def _(q):
            for m in range(24):
                waits = [(sPE.h, pe0 + m - 1)] if m >= 2 else []
                base = (m % 2) * 8192
                wm_toks.append(PL.load(
                    q, f32v(AR1, base, 8192).rearrange("p (k c) -> p k c", c=512),
                    wmod[:, m * 512:(m + 1) * 512].rearrange("(k p) c -> p k c", p=128),
                    waits=waits))

        @blk.tensor
        def _(t):
            t.wait_ge(sAC.h, sAC.v)
            for m in range(24):
                t.wait_ge(*wm_toks[m])
                if m >= 2:
                    t.wait_ge(sDV.h, dv0 + m - 1)
                base = (m % 2) * 8192
                bank = pbk[m % 2]
                for k in range(16):
                    mm = t.matmul(bank[0:1, :], cfm[:, k:k + 1], f32v(AR1, base + k * 512, 512),
                                  start=(k == 0), stop=(k == 15))
                mm.then_inc(sPE.h, 1); sPE.v += 1

        @blk.vector
        def _(v):
            for m in range(24):
                v.wait_ge(sPE.h, pe0 + m + 1)
                v.tensor_copy(rowc[:, m * 512:(m + 1) * 512],
                              pbk[m % 2][0:1, :]).then_inc(sDV.h, 1); sDV.v += 1

    fence()
    with nc.Block() as blk:
        @blk.vector
        def _(v):
            v.tensor_copy(rowb[:, 0:DIM], rowc[:, 0:DIM])
            v.tensor_scalar(rowb[:, DIM:2 * DIM], rowc[:, DIM:2 * DIM], 1.0, None, op0=ALU.add)
            v.tensor_copy(rowb[:, 3 * DIM:4 * DIM], rowc[:, 3 * DIM:4 * DIM])
            v.tensor_scalar(rowb[:, 4 * DIM:5 * DIM], rowc[:, 4 * DIM:5 * DIM], 1.0, None, op0=ALU.add)
            last = None
            for o in (2, 5):
                g0 = rowc[:, o * DIM:(o + 1) * DIM]
                gc = rowb[:, o * DIM:(o + 1) * DIM]
                gneg = rowc[:, 0:DIM]
                v.tensor_scalar(gc, g0, 0.0, 2.0, op0=ALU.is_ge, op1=ALU.mult)
                v.drain()
                v.tensor_scalar(gc, gc, -1.0, None, op0=ALU.add)
                v.drain()
                v.tensor_scalar(gneg, g0, -1.0, None, op0=ALU.mult)
                v.drain()
                v.tensor_tensor(g0, g0, gneg, op=ALU.max)
                v.drain()
                v.tensor_scalar(g0, g0, 0.25, None, op0=ALU.max)
                v.drain()
                last = v.tensor_tensor(gc, gc, g0, op=ALU.mult)
                v.drain()
            last.then_inc(sDV.h, 1); sDV.v += 1

        @blk.sync
        def _(q):
            spill = PL.load(q, modscr[:], rowb, waits=[(sDV.h, sDV.v)])
            PL.load(q, mvec[:],
                    modscr.rearrange("o (s k p) -> (o p) (s k)", p=128, k=16),
                    waits=[spill], slow=True)

    fence()
    pe0 = sPE.v
    ac0 = sAC.v
    x_toks = []
    with nc.Block() as blk:
        @blk.sync
        def _(q):
            for ci in range(16):
                waits = [(sPE.h, pe0 + ci - 1)] if ci >= 2 else []
                x_toks.append(PL.load(q, f32v(AR4, (ci % 2) * 1024, 1024),
                                      xT[ci * 128:(ci + 1) * 128, :], waits=waits))

        @blk.scalar
        def _(s):
            for ci in range(16):
                s.wait_ge(*x_toks[ci])
                s.activation(f32v(AR4, 2048 + (ci % 2) * 1024, 1024),
                             f32v(AR4, (ci % 2) * 1024, 1024),
                             AF.Square).then_inc(sAC.h, 1); sAC.v += 1

        @blk.tensor
        def _(t):
            for ci in range(16):
                t.wait_ge(sAC.h, ac0 + ci + 1)
                xc = (ci % 2) * 1024
                for hh in range(2):
                    t.matmul(pbk[2 + hh][0:1, :], onespc[:], f32v(AR4, xc + hh * 512, 512),
                             start=(ci == 0), stop=(ci == 15))
                    mm = t.matmul(pbk[4 + hh][0:1, :], onespc[:],
                                  f32v(AR4, 2048 + xc + hh * 512, 512),
                                  start=(ci == 0), stop=(ci == 15))
                mm.then_inc(sPE.h, 1); sPE.v += 1

    fence()

    def emit_ln_rows(adst_a, adst_b, dst_bc, from_psum):
        with nc.Block() as blk:
            @blk.vector
            def _(v):
                if from_psum:
                    v.tensor_scalar(srow[:, 0:512], pbk[2][0:1, :], 1.0 / DIM, None, op0=ALU.mult)
                    v.tensor_scalar(srow[:, 512:1024], pbk[3][0:1, :], 1.0 / DIM, None, op0=ALU.mult)
                    v.tensor_scalar(qrow[:, 0:512], pbk[4][0:1, :], 1.0 / DIM, None, op0=ALU.mult)
                    v.tensor_scalar(qrow[:, 512:1024], pbk[5][0:1, :], 1.0 / DIM, None, op0=ALU.mult)
                    v.drain()
                v.tensor_tensor(rowb[0:1, 0:1024], srow[:, :], srow[:, :], op=ALU.mult)
                v.drain()
                v.tensor_tensor(qrow[:, :], qrow[:, :], rowb[0:1, 0:1024], op=ALU.subtract)
                v.drain()
                v.tensor_scalar(qrow[:, :], qrow[:, :], EPS, None, op0=ALU.add)
                v.drain()
                v.reciprocal(qrow[:, :], qrow[:, :]).then_inc(sDV.h, 1); sDV.v += 1

            @blk.scalar
            def _(s):
                s.wait_ge(sDV.h, sDV.v)
                s.activation(adst_a[:, :], qrow[:, :], AF.Sqrt).then_inc(sAC.h, 1); sAC.v += 1

            @blk.vector
            def _(v):
                v.wait_ge(sAC.h, sAC.v)
                v.tensor_tensor(adst_b[:, :], srow[:, :], adst_a[:, :], op=ALU.mult)
                v.drain()
                v.tensor_scalar(adst_b[:, :], adst_b[:, :], -1.0, None,
                                op0=ALU.mult).then_inc(sDV.h, 1); sDV.v += 1

            @blk.tensor
            def _(t):
                t.wait_ge(sDV.h, sDV.v)
                for hh in range(2):
                    t.matmul(pbk[hh][:, :], onescr[:], adst_a[:, hh * 512:(hh + 1) * 512],
                             start=True, stop=True)
                    mm = t.matmul(pbk[2 + hh][:, :], onescr[:], adst_b[:, hh * 512:(hh + 1) * 512],
                                  start=True, stop=True)
                mm.then_inc(sPE.h, 1); sPE.v += 1

            @blk.vector
            def _(v):
                v.wait_ge(sPE.h, sPE.v)
                for hh in range(2):
                    v.tensor_copy(f32v(AR4, dst_bc + hh * 512, 512), pbk[hh][:, :])
                    cp = v.tensor_copy(f32v(AR4, dst_bc + 1024 + hh * 512, 512), pbk[2 + hh][:, :])
                cp.then_inc(sDV.h, 1); sDV.v += 1

    emit_ln_rows(arow, brow, 4096, True)
    fence()

    def _p2_chunk_src(i):
        if i < 64:
            rb, cb = i // 4, i % 4
            return wgT[rb * 128:(rb + 1) * 128, cb * 2048:(cb + 1) * 2048]
        if i < 128:
            j = i - 64
            rb, cb = j // 4, j % 4
            return wuT[rb * 128:(rb + 1) * 128, cb * 2048:(cb + 1) * 2048]
        j = i - 128
        return wdT[j * 128:(j + 1) * 128, :]

    p2_state = {"i": 0, "dv": [None, None]}

    def emit_p2_sprinkle(count):
        toks = []
        base = p2_state["i"]
        dv_b2 = sDV.v
        with nc.Block() as blk:
            @blk.sync
            def _(q):
                for n in range(count):
                    i = base + n
                    bi = i % 2
                    if n >= 2:
                        waits = [(sDV.h, dv_b2 + n - 1)]
                    elif p2_state["dv"][bi] is not None:
                        waits = [(sDV.h, p2_state["dv"][bi])]
                    else:
                        waits = []
                    toks.append(PL.load(q, f32v(AR4, 8192 + bi * 2048, 2048),
                                        _p2_chunk_src(i), waits=waits))

            @blk.vector
            def _(v):
                for n in range(count):
                    i = base + n
                    bi = i % 2
                    v.wait_ge(*toks[n])
                    v.tensor_reduce(redp[:, i:i + 1], f32v(AR4, 8192 + bi * 2048, 2048),
                                    axis=AX.X, op=ALU.add,
                                    apply_absolute_value=True).then_inc(sDV.h, 1)
                    sDV.v += 1
                    p2_state["dv"][bi] = sDV.v
        p2_state["i"] += count


    def emit_h1(half):
        dv0 = sDV.v
        h_toks = []
        with nc.Block() as blk:
            @blk.sync
            def _(q):
                for ci in range(16):
                    waits = [(sDV.h, dv0 + ci - 1)] if ci >= 2 else []
                    h_toks.append(PL.load(q, f32v(AR4, (ci % 2) * 512, 512),
                                          xT[ci * 128:(ci + 1) * 128, half * 512:(half + 1) * 512],
                                          waits=waits))

            @blk.vector
            def _(v):
                for ci in range(16):
                    v.wait_ge(*h_toks[ci])
                    xc = f32v(AR4, (ci % 2) * 512, 512)
                    t1 = f32v(AR4, 1024, 512)
                    v.tensor_tensor(t1, xc, f32v(AR4, 4096 + half * 512, 512), op=ALU.mult)
                    v.drain()
                    v.tensor_tensor(t1, t1, f32v(AR4, 5120 + half * 512, 512), op=ALU.add)
                    v.drain()
                    v.tensor_scalar(t1, t1, mvec[:, 16 + ci:17 + ci], mvec[:, ci:ci + 1],
                                    op0=ALU.mult, op1=ALU.add)
                    v.drain()
                    v.tensor_copy(bav(0, 8192)[:, ci * 512:(ci + 1) * 512], t1)
                    v.drain()
                    v.scalar_tensor_tensor(bav(8192, 8192)[:, ci * 512:(ci + 1) * 512],
                                           t1, 1.0, bav(0, 8192)[:, ci * 512:(ci + 1) * 512],
                                           op0=ALU.bypass,
                                           op1=ALU.subtract).then_inc(sDV.h, 1); sDV.v += 1
                    v.drain()

    tms_spills = [None, None]

    def emit_proj(half, wsrc, dst_kind):
        h1H = bav(0, 8192)
        h1L = bav(8192, 8192)
        scr_hi = {'q': qhscr, 'k': khscr}.get(dst_kind)
        scr_lo = {'q': qlscr, 'k': klscr}.get(dst_kind)
        spills = tms_spills
        pe_b, dv_b, pe_prev = sPE.v, sDV.v, sPE.v

        sched = [('S', 0), ('S', 1)]
        for fc in range(16):
            sched.append(('E1', fc)); sched.append(('E2', fc))
            if fc + 2 < 16:
                sched.append(('S', fc + 2))
        split_dv, ev1_dv, ev2_dv = {}, {}, {}
        c = dv_b
        for k, i in sched:
            c += 1
            {'S': split_dv, 'E1': ev1_dv, 'E2': ev2_dv}[k][i] = c
        sDV.v = c
        pe_mm = lambda fc: pe_b + (1 if fc == 0 else 2 * fc)
        pe_tr = lambda fc: pe_b + (2 * fc + 3 if fc < 15 else 32)
        ld_toks = []
        spill_slots = [list(spills)] + [None] * 16

        with nc.Block() as blk:
            @blk.sync
            def _(q):
                def emit_load(fc):
                    bi = fc % 2
                    w32 = f32v(AR1, bi * 2048, 2048)
                    waits = [(sDV.h, split_dv[fc - 2])] if fc >= 2 else []
                    ld_toks.append(PL.load(
                        q, w32.rearrange("p (ci f) -> p ci f", f=128),
                        wsrc[:, fc * 128:(fc + 1) * 128].rearrange("(ci p) f -> p ci f", p=128),
                        waits=waits))

                def emit_spill(fc):
                    if dst_kind == 'v':
                        t0 = PL.load(
                            q,
                            vscr.rearrange("p (g c) -> p g c", c=2048)[:, :, fc * 128:(fc + 1) * 128],
                            f32v(AR1, 4608, 512).rearrange("p (g c) -> p g c", c=128),
                            waits=[(sDV.h, ev2_dv[fc])])
                        spill_slots[fc + 1] = [t0, None]
                    else:
                        t0 = PL.load(
                            q,
                            scr_hi.rearrange("p (g c) -> p g c", c=2048)[:, :, fc * 128:(fc + 1) * 128],
                            bav(25600, 512).rearrange("p (g c) -> p g c", c=128),
                            waits=[(sDV.h, ev2_dv[fc])])
                        t1 = PL.load(
                            q,
                            scr_lo.rearrange("p (g c) -> p g c", c=2048)[:, :, fc * 128:(fc + 1) * 128],
                            bav(26112, 512).rearrange("p (g c) -> p g c", c=128))
                        spill_slots[fc + 1] = [t0, t1]

                emit_load(0)
                emit_load(1)
                for fc in range(16):
                    if fc + 2 < 16:
                        emit_load(fc + 2)
                    emit_spill(fc)

            @blk.vector
            def _(v):
                for kind, fc in sched:
                    bi = fc % 2
                    if kind == 'S':
                        w32 = f32v(AR1, bi * 2048, 2048)
                        wsH = bav(16384 + bi * 4096, 2048)
                        wsL = bav(18432 + bi * 4096, 2048)
                        v.wait_ge(*ld_toks[fc])
                        if fc >= 2:
                            v.wait_ge(sPE.h, pe_mm(fc - 2))
                        v.tensor_copy(wsH, w32)
                        v.drain()
                        v.scalar_tensor_tensor(wsL, w32, 1.0, wsH, op0=ALU.bypass,
                                               op1=ALU.subtract).then_inc(sDV.h, 1)
                        v.drain()
                    elif kind == 'E1':
                        v.wait_ge(sPE.h, pe_mm(fc))
                        bank = pbk[bi]
                        if dst_kind == 'v':
                            for sp in spill_slots[fc]:
                                if sp is not None:
                                    v.wait_ge(*sp)
                            v.tensor_copy(f32v(AR1, 4096, 512), bank[:, :]).then_inc(sDV.h, 1)
                            v.drain()
                        else:
                            v.tensor_copy(bav(24576, 512), bank[:, :])
                            v.drain()
                            v.scalar_tensor_tensor(bav(25088, 512), bank[:, :], 1.0,
                                                   bav(24576, 512), op0=ALU.bypass,
                                                   op1=ALU.subtract).then_inc(sDV.h, 1)
                            v.drain()
                    else:
                        v.wait_ge(sPE.h, pe_tr(fc))
                        if dst_kind == 'v':
                            for sp in spill_slots[fc]:
                                if sp is not None:
                                    v.wait_ge(*sp)
                            for tb in range(4):
                                cp = v.tensor_copy(
                                    f32v(AR1, 4608, 512)[:, tb * 128:(tb + 1) * 128],
                                    pbk[2 + tb % 2][:, (tb // 2) * 128:(tb // 2) * 128 + 128])
                            cp.then_inc(sDV.h, 1)
                            v.drain()
                        else:
                            for sp in spill_slots[fc]:
                                if sp is not None:
                                    v.wait_ge(*sp)
                            v.tensor_copy(bav(25600, 512), pbkB[0][:, 0:512])
                            v.tensor_copy(bav(26112, 512), pbkB[1][:, 0:512]).then_inc(sDV.h, 1)
                            v.drain()

            @blk.tensor
            def _(t):
                def emit_mm(fc):
                    bi = fc % 2
                    wsH = bav(16384 + bi * 4096, 2048)
                    wsL = bav(18432 + bi * 4096, 2048)
                    t.wait_ge(sDV.h, split_dv[fc])
                    if fc >= 2:
                        t.wait_ge(sDV.h, ev1_dv[fc - 2])
                    bank = pbk[bi]
                    first = True
                    for wof, mv in [(wsH, h1H), (wsH, h1L), (wsL, h1H)]:
                        for ci in range(16):
                            mm = t.matmul(bank[:, :], wof[:, ci * 128:(ci + 1) * 128],
                                          mv[:, ci * 512:(ci + 1) * 512],
                                          start=first, stop=(wof is wsL and ci == 15))
                            first = False
                    mm.then_inc(sPE.h, 1)

                def emit_tr(fc):
                    t.wait_ge(sDV.h, ev1_dv[fc])
                    if dst_kind == 'v':
                        for tb in range(4):
                            mm = t.transpose(pbk[2 + tb % 2][:, (tb // 2) * 128:(tb // 2) * 128 + 128],
                                             f32v(AR1, 4096 + tb * 128, 128), ident[:])
                    else:
                        for part, src in [(0, bav(24576, 512)), (1, bav(25088, 512))]:
                            for tb in range(4):
                                mm = t.transpose(pbkB[part][:, tb * 128:(tb + 1) * 128],
                                                 src[:, tb * 128:(tb + 1) * 128], identb[:])
                    mm.then_inc(sPE.h, 1)

                emit_mm(0)
                for fc in range(1, 16):
                    emit_mm(fc)
                    emit_tr(fc - 1)
                emit_tr(15)
        sPE.v = pe_b + 32
        spills[0] = spill_slots[16][0]
        spills[1] = spill_slots[16][1]

    def emit_group(half, g):
        fence()
        qmh, qml = bav(0, 2048), bav(2048, 2048)
        kmh, kml = bav(4096, 2048), bav(6144, 2048)
        qgh, qgl = bav(8192, 2048), bav(10240, 2048)
        kgh, kgl = bav(12288, 2048), bav(14336, 2048)
        vG = f32v(AR4, 0, 2048)
        vm, vm2 = f32v(AR1, 0, 2048), f32v(AR1, 2048, 2048)
        attsb, atts2 = f32v(AR1, 4096, 1024), f32v(AR1, 5120, 1024)
        ddup, dT, dT2 = f32v(AR1, 6144, 1024), f32v(AR1, 7168, 1024), f32v(AR1, 8192, 1024)
        AM = f32v(AR1, 9216, 2048)
        AMsq = f32v(AR1, 4096, 2048)
        rmx, rsm = f32v(AR1, 11264, 32), f32v(AR1, 11296, 32)
        s2a, s2b = f32v(AR1, 11328, 32), f32v(AR1, 11360, 32)

        toks = []
        with nc.Block() as blk:
            @blk.sync
            def _(q):
                toks.append(PL.load(q, vG, vscr[:, g * 2048:(g + 1) * 2048]))
                toks.append(PL.load(q, qgh[:, :], qhscr[:, g * 2048:(g + 1) * 2048]))
                toks.append(PL.load(q, qgl[:, :], qlscr[:, g * 2048:(g + 1) * 2048]))
                toks.append(PL.load(q, kgh[:, :], khscr[:, g * 2048:(g + 1) * 2048]))
                toks.append(PL.load(q, kgl[:, :], klscr[:, g * 2048:(g + 1) * 2048]))

            @blk.vector
            def _(v):
                for tk in toks:
                    v.wait_ge(*tk)
                v.transpose(qmh, qgh)
                v.transpose(qml, qgl)
                v.transpose(kmh, kgh)
                v.transpose(kml, kgl)
                v.transpose(vm, vG)
                v.drain()
                v.stream_shuffle(vm2, vm, [(p + 16) % 32 for p in range(32)]).then_inc(sDV.h, 1)
                sDV.v += 1

        dv_min = sDV.v
        with nc.Block() as blk:
            @blk.tensor
            def _(t):
                t.wait_ge(sDV.h, dv_min)
                for tt in range(128):
                    pq = tt // 32
                    t32 = tt % 32
                    bank = pbk[4] if t32 < 16 else pbk[5]
                    col = (t32 % 16) * 32
                    idx = 0
                    for qm, km in [(qmh, kmh), (qmh, kml), (qml, kmh)]:
                        for dh in range(2):
                            st = qm[32 * pq:32 * pq + 32, :].rearrange(
                                "p (i c) -> p i c", c=64)[:, :, 32 * dh + t32]
                            mv = km[32 * pq:32 * pq + 32, :].rearrange(
                                "p (i c) -> p i c", c=64)[:, :, 32 * dh + t32]
                            mm = t.matmul(bank[32 * pq:32 * pq + 32, col:col + 32], st, mv,
                                          start=(idx == 0), stop=(idx == 5),
                                          tile_position=(32 * pq, 32 * pq))
                            idx += 1
                mm.then_inc(sPE.h, 1); sPE.v += 1

        pe_gram = sPE.v
        with nc.Block() as blk:
            @blk.vector
            def _(v):
                v.wait_ge(sPE.h, pe_gram)
                v.tensor_scalar(attsb[:, 0:512], pbk[4][:, :], SCALE, None, op0=ALU.mult)
                v.tensor_scalar(attsb[:, 512:1024], pbk[5][:, :], SCALE, None, op0=ALU.mult)
                v.drain()
                v.tensor_reduce(rmx, attsb.rearrange("p (s j) -> p s j", j=32),
                                axis=AX.X, op=ALU.max)
                v.drain()
                v.tensor_tensor(atts2.rearrange("p (s j) -> p s j", j=32),
                                attsb.rearrange("p (s j) -> p s j", j=32),
                                rmx.unsqueeze(2).broadcast_to((128, 32, 32)),
                                op=ALU.subtract).then_inc(sDV.h, 1); sDV.v += 1

            @blk.scalar
            def _(s):
                s.wait_ge(sDV.h, sDV.v)
                s.activation(attsb[:, :], atts2[:, :], AF.Exp).then_inc(sAC.h, 1); sAC.v += 1

            @blk.vector
            def _(v):
                v.wait_ge(sAC.h, sAC.v)
                v.tensor_reduce(rsm, attsb.rearrange("p (s j) -> p s j", j=32),
                                axis=AX.X, op=ALU.add)
                v.drain()
                v.reciprocal(rsm, rsm)
                v.drain()
                v.tensor_tensor(atts2.rearrange("p (s j) -> p s j", j=32),
                                attsb.rearrange("p (s j) -> p s j", j=32),
                                rsm.unsqueeze(2).broadcast_to((128, 32, 32)), op=ALU.mult)
                v.drain()
                v.stream_shuffle(attsb, atts2, [(p + 16) % 32 for p in range(32)])
                v.drain()
                for pq in range(4):
                    for dcp in range(2):
                        v.scalar_tensor_tensor(
                            ddup[32 * pq:32 * pq + 16, :].rearrange(
                                "p (s j) -> p s j", j=32)[:, :, dcp * 16:(dcp + 1) * 16],
                            attsb[32 * pq:32 * pq + 16, :].rearrange(
                                "p (s j) -> p s j", j=32)[:, :, 16:32],
                            ppt[32 * pq:32 * pq + 16, 0:1],
                            atts2[32 * pq:32 * pq + 16, :].rearrange(
                                "p (s j) -> p s j", j=32)[:, :, 0:16],
                            op0=ALU.mult, op1=ALU.add)
                        v.drain()
                v.transpose(dT, ddup)
                v.drain()
                v.tensor_copy(dT2.rearrange("p (s j) -> p s j", j=32)[:, :, 16:32],
                              dT.rearrange("p (s j) -> p s j", j=32)[:, :, 0:16]).then_inc(sDV.h, 1)
                sDV.v += 1

        dv_diff = sDV.v
        pe_dv0 = sPE.v
        dv_ev0 = sDV.v
        with nc.Block() as blk:
            @blk.tensor
            def _(t):
                t.wait_ge(sDV.h, dv_diff)
                for batch in range(4):
                    if batch >= 2:
                        t.wait_ge(sDV.h, dv_ev0 + batch - 1)
                    for w in range(8):
                        s = batch * 8 + w
                        for pq in range(4):
                            be = pbk[2] if batch % 2 == 0 else pbk[3]
                            st_e = dT[32 * pq:32 * pq + 16, s * 32:s * 32 + 32]
                            st_o = dT2[32 * pq:32 * pq + 16, s * 32:s * 32 + 32]
                            mv_e = vm[32 * pq:32 * pq + 16, :].rearrange(
                                "p (u c) -> p u c", c=32)[:, :, s]
                            mv_o = vm2[32 * pq:32 * pq + 16, :].rearrange(
                                "p (u c) -> p u c", c=32)[:, :, s]
                            t.matmul(be[32 * pq:32 * pq + 32, w * 64:w * 64 + 64], st_e, mv_e,
                                     start=True, stop=False, tile_position=(32 * pq, 32 * pq))
                            mm = t.matmul(be[32 * pq:32 * pq + 32, w * 64:w * 64 + 64], st_o, mv_o,
                                          start=False, stop=True, tile_position=(32 * pq, 32 * pq))
                    mm.then_inc(sPE.h, 1); sPE.v += 1

            @blk.vector
            def _(v):
                for batch in range(4):
                    v.wait_ge(sPE.h, pe_dv0 + batch + 1)
                    be = pbk[2] if batch % 2 == 0 else pbk[3]
                    v.tensor_copy(
                        AM.rearrange("p (u c) -> p u c", c=32)[:, :, batch * 8:batch * 8 + 8],
                        be[:, :].rearrange("p (w u) -> p u w", u=64)).then_inc(sDV.h, 1)
                    sDV.v += 1

        fence()
        with nc.Block() as blk:
            @blk.scalar
            def _(s):
                s.activation(AMsq, AM, AF.Square).then_inc(sAC.h, 1); sAC.v += 1

            @blk.vector
            def _(v):
                v.wait_ge(sAC.h, sAC.v)
                v.tensor_reduce(s2a, AMsq.rearrange("p (u c) -> p c u", c=32),
                                axis=AX.X, op=ALU.add)
                v.drain()
                v.stream_shuffle(s2b, s2a, [(p + 16) % 32 for p in range(32)])
                v.drain()
                v.tensor_tensor(s2a, s2a, s2b, op=ALU.add)
                v.drain()
                v.tensor_scalar(s2a, s2a, 1.0 / 128.0, EPS, op0=ALU.mult, op1=ALU.add)
                v.drain()
                v.reciprocal(s2a, s2a).then_inc(sDV.h, 1); sDV.v += 1

            @blk.scalar
            def _(s):
                s.wait_ge(sDV.h, sDV.v)
                s.activation(s2a, s2a, AF.Sqrt,
                             scale=(1.0 - LAM_INIT) ** 2).then_inc(sAC.h, 1); sAC.v += 1

            @blk.vector
            def _(v):
                v.wait_ge(sAC.h, sAC.v)
                v.tensor_tensor(AM.rearrange("p (u c) -> p u c", c=32),
                                AM.rearrange("p (u c) -> p u c", c=32),
                                s2a.unsqueeze(1).broadcast_to((128, 64, 32)), op=ALU.mult)
                v.drain()
                v.tensor_copy(bav(16384, 2048), AM)
                v.drain()
                v.scalar_tensor_tensor(bav(18432, 2048), AM, 1.0, bav(16384, 2048),
                                       op0=ALU.bypass, op1=ALU.subtract)
                v.drain()
                v.transpose(bav(16384, 2048), bav(16384, 2048))
                v.drain()
                v.transpose(bav(18432, 2048), bav(18432, 2048)).then_inc(sDV.h, 1); sDV.v += 1

            dv_t32 = sDV.v
            @blk.tensor
            def _(t):
                t.wait_ge(sDV.h, dv_t32)
                i = 0
                for part, c0 in [(0, 16384), (1, 18432)]:
                    for fq in range(4):
                        if i >= 2:
                            t.wait_ge(sDV.h, dv_t32 + i - 1)
                        bank = pbkB[fq % 2]
                        for sub in range(4):
                            fc = fq * 4 + sub
                            mm = t.transpose(bank[:, sub * 128:(sub + 1) * 128],
                                             bav(c0, 2048)[:, fc * 128:(fc + 1) * 128],
                                             identb[:])
                        mm.then_inc(sPE.h, 1); sPE.v += 1
                        i += 1

            @blk.vector
            def _(v):
                cnt = sPE.v - 8
                for part, arof in [(0, 20480), (1, 28672)]:
                    for fq in range(4):
                        cnt += 1
                        v.wait_ge(sPE.h, cnt)
                        bank = pbkB[fq % 2]
                        dst = bav(arof, 8192)
                        v.tensor_copy(
                            dst.rearrange("p (fc t) -> p fc t", t=512)[:, fq * 4:fq * 4 + 4,
                                                                       g * 128:(g + 1) * 128],
                            bank[:, 0:512].rearrange("p (fc t) -> p fc t", t=128)).then_inc(sDV.h, 1)
                        sDV.v += 1

    def emit_wo(half):
        AFMH = bav(20480, 8192)
        AFML = bav(28672, 8192)
        pe_b, dv_b, ac_b = sPE.v, sDV.v, sAC.v

        sched = [('S', 0), ('S', 1)]
        for fc in range(16):
            sched.append(('R', fc))
            if fc + 2 < 16:
                sched.append(('S', fc + 2))
        split_dv, res_dv = {}, {}
        c = dv_b
        for k, i in sched:
            c += 1
            {'S': split_dv, 'R': res_dv}[k][i] = c
        sDV.v = c
        pe_mm = lambda fc: pe_b + (1 if fc == 0 else 2 * fc)
        pe_st = lambda fc: pe_b + (2 * fc + 3 if fc < 15 else 32)
        ac_bias = lambda fc: ac_b + 2 * fc + 1
        ac_sq = lambda fc: ac_b + 2 * fc + 2
        sAC.v = ac_b + 32

        def wsbuf(bi):
            if bi == 0:
                return bav(16384, 2048), bav(18432, 2048)
            return bav(0, 2048), bav(2048, 2048)

        wld, xld, spl = [], [], [None] * 17
        with nc.Block() as blk:
            @blk.sync
            def _(q):
                def emit_load(fc):
                    bi = fc % 2
                    w32 = f32v(AR1, bi * 2048, 2048)
                    waits = [(sDV.h, split_dv[fc - 2])] if fc >= 2 else []
                    wld.append(PL.load(
                        q, w32.rearrange("p (ci f) -> p ci f", f=128),
                        woT[:, fc * 128:(fc + 1) * 128].rearrange("(ci p) f -> p ci f", p=128),
                        waits=waits))
                    xw = [(sDV.h, res_dv[fc - 2])] if fc >= 2 else []
                    xld.append(PL.load(
                        q, f32v(AR4, bi * 512, 512),
                        xT[fc * 128:(fc + 1) * 128, half * 512:(half + 1) * 512],
                        waits=xw))

                emit_load(0)
                emit_load(1)
                for fc in range(16):
                    if fc + 2 < 16:
                        emit_load(fc + 2)
                    spl[fc + 1] = PL.load(
                        q, x2scr[fc * 128:(fc + 1) * 128, half * 512:(half + 1) * 512],
                        f32v(AR4, 2048, 512), waits=[(sPE.h, pe_st(fc))])

            @blk.vector
            def _(v):
                for kind, fc in sched:
                    bi = fc % 2
                    if kind == 'S':
                        w32 = f32v(AR1, bi * 2048, 2048)
                        wsH, wsL = wsbuf(bi)
                        v.wait_ge(*wld[fc])
                        if fc >= 2:
                            v.wait_ge(sPE.h, pe_mm(fc - 2))
                        v.tensor_copy(wsH, w32)
                        v.drain()
                        v.scalar_tensor_tensor(wsL, w32, 1.0, wsH, op0=ALU.bypass,
                                               op1=ALU.subtract).then_inc(sDV.h, 1)
                        v.drain()
                    else:
                        v.wait_ge(sAC.h, ac_bias(fc))
                        if spl[fc] is not None:
                            v.wait_ge(*spl[fc])
                        v.scalar_tensor_tensor(f32v(AR4, 2048, 512), f32v(AR4, 1024, 512),
                                               mvec[:, 32 + fc:33 + fc], f32v(AR4, bi * 512, 512),
                                               op0=ALU.mult, op1=ALU.add).then_inc(sDV.h, 1)
                        v.drain()

            @blk.scalar
            def _(s):
                for fc in range(16):
                    bi = fc % 2
                    s.wait_ge(sPE.h, pe_mm(fc))
                    s.wait_ge(*xld[fc])
                    if fc >= 1:
                        s.wait_ge(sDV.h, res_dv[fc - 1])
                    s.activation(f32v(AR4, 1024, 512), pbk[bi][:, :], AF.Identity,
                                 bias=bofm[:, fc:fc + 1], scale=1.0).then_inc(sAC.h, 1)
                    s.wait_ge(sDV.h, res_dv[fc])
                    s.activation(f32v(AR4, 3072, 512), f32v(AR4, 2048, 512),
                                 AF.Square).then_inc(sAC.h, 1)

            @blk.tensor
            def _(t):
                def emit_mm(fc):
                    bi = fc % 2
                    wsH, wsL = wsbuf(bi)
                    t.wait_ge(sDV.h, split_dv[fc])
                    if fc >= 2:
                        t.wait_ge(sAC.h, ac_bias(fc - 2))
                    bank = pbk[bi]
                    first = True
                    for wof, mv in [(wsH, AFMH), (wsH, AFML), (wsL, AFMH)]:
                        for ci in range(16):
                            mm = t.matmul(bank[:, :], wof[:, ci * 128:(ci + 1) * 128],
                                          mv[:, ci * 512:(ci + 1) * 512],
                                          start=first, stop=(wof is wsL and ci == 15))
                            first = False
                    mm.then_inc(sPE.h, 1)

                def emit_st(fc):
                    t.wait_ge(sAC.h, ac_sq(fc))
                    t.matmul(pbk[4][0:1, :], onespc[:], f32v(AR4, 2048, 512),
                             start=(fc == 0), stop=(fc == 15))
                    t.matmul(pbk[5][0:1, :], onespc[:], f32v(AR4, 3072, 512),
                             start=(fc == 0), stop=(fc == 15)).then_inc(sPE.h, 1)

                emit_mm(0)
                for fc in range(1, 16):
                    emit_mm(fc)
                    emit_st(fc - 1)
                emit_st(15)
        sPE.v = pe_b + 32
        fence()
        with nc.Block() as blk:
            @blk.vector
            def _(v):
                v.tensor_scalar(srow[:, half * 512:(half + 1) * 512], pbk[4][0:1, :],
                                1.0 / DIM, None, op0=ALU.mult)
                v.tensor_scalar(qrow[:, half * 512:(half + 1) * 512], pbk[5][0:1, :],
                                1.0 / DIM, None, op0=ALU.mult).then_inc(sDV.h, 1); sDV.v += 1

    for half in range(2):
        emit_h1(half)
        fence()
        emit_proj(half, wqT, 'q')
        emit_p2_sprinkle(24)
        emit_proj(half, wkT, 'k')
        emit_p2_sprinkle(24)
        emit_proj(half, wvT, 'v')
        emit_p2_sprinkle(24)
        fence()
        with nc.Block() as blk:
            @blk.vector
            def _(v):
                v.memset(f32v(AR1, 6144, 1024), 0.0)
                v.memset(f32v(AR1, 8192, 1024), 0.0).then_inc(sDV.h, 1); sDV.v += 1
        for g in range(4):
            emit_group(half, g)
        fence()
        emit_wo(half)
        emit_p2_sprinkle(24)
        fence()

    emit_ln_rows(a2row, b2row, 4096, False)
    fence()

    with nc.Block() as blk:
        @blk.vector
        def _(v):
            v.tensor_reduce(redp[:, 192:195],
                            redp[:, 0:192].rearrange("p (g k) -> p g k", k=64),
                            axis=AX.X, op=ALU.add).then_inc(sDV.h, 1); sDV.v += 1

        @blk.tensor
        def _(t):
            t.wait_ge(sDV.h, sDV.v)
            for j in range(3):
                mm = t.matmul(pbk[4][0:1, j:j + 1], onespc[:], redp[:, 192 + j:193 + j],
                              start=True, stop=True)
            mm.then_inc(sPE.h, 1); sPE.v += 1

        @blk.vector
        def _(v):
            v.wait_ge(sPE.h, sPE.v)
            v.tensor_scalar(lam1[:, 5:8], pbk[4][0:1, 0:3], 1.0 / (DIM * DI), EPS,
                            op0=ALU.mult, op1=ALU.max)
            v.drain()
            v.tensor_scalar(lam1[:, 8:11], lam1[:, 5:8], 0.5, None, op0=ALU.mult)
            v.drain()
            v.tensor_scalar(lam1[:, 11:14], lam1[:, 8:11], -1.0, None,
                            op0=ALU.mult).then_inc(sDV.h, 1); sDV.v += 1

        @blk.tensor
        def _(t):
            t.wait_ge(sDV.h, sDV.v)
            t.matmul(pbk[3][:, 0:9], onescr[:], lam1[0:1, 5:14], start=True,
                     stop=True).then_inc(sPE.h, 1); sPE.v += 1

        @blk.vector
        def _(v):
            v.wait_ge(sPE.h, sPE.v)
            v.tensor_copy(ppt[:, 1:10], pbk[3][:, 0:9]).then_inc(sDV.h, 1); sDV.v += 1
    fence()

    def utile(k):
        return f32v(AR1, k * QT, QT)

    def zqv(k):
        return bav(20480 + k * QT, QT)

    h2q_all = bav(0, 16384)
    u_all = f32v(AR1, 0, 16384)

    def emit_pmax_bcast(srcdst, width, extra_waits=()):
        nb = width // 128
        with nc.Block() as blk:
            @blk.tensor
            def _(t):
                t.wait_ge(sDV.h, sDV.v)
                for tb in range(nb):
                    mm = t.transpose(pbk[3][:, tb * 128:(tb + 1) * 128],
                                     srcdst[:, tb * 128:(tb + 1) * 128], ident[:])
                mm.then_inc(sPE.h, 1); sPE.v += 1

            @blk.vector
            def _(v):
                v.wait_ge(sPE.h, sPE.v)
                for h, vv in extra_waits:
                    v.wait_ge(h, vv)
                v.tensor_copy(qt1[:, 0:width], pbk[3][:, 0:width])
                v.drain()
                v.tensor_reduce(qt2[:, 0:nb],
                                qt1[:, 0:width].rearrange("p (tb c) -> p tb c", c=128),
                                axis=AX.X, op=ALU.max).then_inc(sDV.h, 1); sDV.v += 1

            @blk.tensor
            def _(t):
                t.wait_ge(sDV.h, sDV.v)
                for tb in range(nb):
                    mm = t.transpose(pbk[4][0:1, tb * 128:(tb + 1) * 128],
                                     qt2[:, tb:tb + 1], ident[:])
                mm.then_inc(sPE.h, 1); sPE.v += 1

            @blk.vector
            def _(v):
                v.wait_ge(sPE.h, sPE.v)
                v.tensor_copy(qt1[0:1, 0:width], pbk[4][0:1, 0:width]).then_inc(sDV.h, 1)
                sDV.v += 1

            @blk.tensor
            def _(t):
                t.wait_ge(sDV.h, sDV.v)
                t.matmul(pbk[3][:, 0:width], onescr[:], qt1[0:1, 0:width], start=True,
                         stop=True).then_inc(sPE.h, 1); sPE.v += 1

            @blk.vector
            def _(v):
                v.wait_ge(sPE.h, sPE.v)
                v.tensor_copy(srcdst, pbk[3][:, 0:width]).then_inc(sDV.h, 1); sDV.v += 1

    def emit_quant_slabs(src_slab, seff_row, seg, nslab, dst_of, extra_waits=()):
        SW = 4096
        slab_dv = []
        with nc.Block() as blk:
            @blk.vector
            def _(v):
                v.wait_ge(sDV.h, sDV.v)
                for h, vv in extra_waits:
                    v.wait_ge(h, vv)
                for s in range(nslab):
                    U = src_slab(s)
                    B = f32v(AR4, 8192, SW)
                    J = qt3[:, 0:SW]
                    nrep = SW // seg
                    v.tensor_tensor(U.rearrange("p (r t) -> p r t", t=seg),
                                    U.rearrange("p (r t) -> p r t", t=seg),
                                    seff_row.unsqueeze(1).broadcast_to((128, nrep, seg)),
                                    op=ALU.mult)
                    v.drain()
                    v.tensor_scalar(U, U, 127.0, 384.5, op0=ALU.min, op1=ALU.add)
                    v.drain()
                    v.tensor_copy(J, U)
                    v.drain()
                    v.tensor_scalar(B, J, -385.0, None, op0=ALU.add)
                    v.drain()
                    v.tensor_tensor(U, U, B, op=ALU.subtract)
                    v.drain()
                    v.scalar_tensor_tensor(B, U, 386.0, B, op0=ALU.is_ge, op1=ALU.add)
                    v.drain()
                    v.scalar_tensor_tensor(bav(dst_of + s * SW, SW), U, 385.0, B,
                                           op0=ALU.is_ge, op1=ALU.add).then_inc(sDV.h, 1)
                    sDV.v += 1
                    slab_dv.append(sDV.v)
                    v.drain()
        return slab_dv

    def emit_h2q():
        dv0 = sDV.v
        h_toks = []
        with nc.Block() as blk:
            @blk.sync
            def _(q):
                for ci in range(16):
                    waits = [(sDV.h, dv0 + ci - 1)] if ci >= 2 else []
                    h_toks.append(PL.load(
                        q, f32v(AR4, (ci % 2) * 1024, 1024),
                        x2scr[ci * 128:(ci + 1) * 128, :], waits=waits))

            @blk.vector
            def _(v):
                for ci in range(16):
                    v.wait_ge(*h_toks[ci])
                    xc = f32v(AR4, (ci % 2) * 1024, 1024)
                    t1 = f32v(AR4, 8192, 1024)
                    v.tensor_tensor(t1, xc, f32v(AR4, 4096, 1024), op=ALU.mult)
                    v.drain()
                    v.tensor_tensor(t1, t1, f32v(AR4, 5120, 1024), op=ALU.add)
                    v.drain()
                    v.tensor_scalar(t1, t1, mvec[:, 64 + ci:65 + ci], mvec[:, 48 + ci:49 + ci],
                                    op0=ALU.mult, op1=ALU.add)
                    v.drain()
                    v.tensor_copy(f32v(AR1, ci * 1024, 1024), t1).then_inc(sDV.h, 1); sDV.v += 1
                    v.drain()

        fence()
        with nc.Block() as blk:
            @blk.vector
            def _(v):
                v.tensor_reduce(f32v(AR4, 2048, 1024),
                                u_all.rearrange("p (ci t) -> p t ci", t=1024),
                                axis=AX.X, op=ALU.max,
                                apply_absolute_value=True).then_inc(sDV.h, 1); sDV.v += 1
        emit_pmax_bcast(f32v(AR4, 2048, 512), 512)
        emit_pmax_bcast(f32v(AR4, 2560, 512), 512)
        with nc.Block() as blk:
            @blk.vector
            def _(v):
                v.wait_ge(sDV.h, sDV.v)
                v.tensor_scalar(f32v(AR4, 2048, 1024), f32v(AR4, 2048, 1024), EPS, None,
                                op0=ALU.max)
                v.drain()
                v.reciprocal(f32v(AR4, 3072, 1024), f32v(AR4, 2048, 1024))
                v.drain()
                v.tensor_scalar(f32v(AR4, 3072, 1024), f32v(AR4, 3072, 1024), QB, None,
                                op0=ALU.mult)
                v.drain()
                v.tensor_scalar(f32v(AR4, 0, 1024), f32v(AR4, 2048, 1024), ppt[:, 1:2],
                                1.0 / QB, op0=ALU.mult, op1=ALU.mult)
                v.tensor_scalar(f32v(AR4, 1024, 1024), f32v(AR4, 2048, 1024), ppt[:, 2:3],
                                1.0 / QB, op0=ALU.mult, op1=ALU.mult).then_inc(sDV.h, 1)
                sDV.v += 1
        emit_quant_slabs(lambda s: f32v(AR1, s * 4096, 4096), f32v(AR4, 3072, 1024),
                         1024, 4, 0)

    def emit_tern(w32, dst_bf, kindcol, waits):
        with nc.Block() as blk:
            @blk.vector
            def _(v):
                for h, vv in waits:
                    v.wait_ge(h, vv)
                v.tensor_scalar(dst_bf, w32, ppt[:, 7 + kindcol:8 + kindcol], None, op0=ALU.is_gt)
                v.drain()
                v.scalar_tensor_tensor(dst_bf, w32, ppt[:, 4 + kindcol:5 + kindcol], dst_bf,
                                       op0=ALU.is_gt, op1=ALU.add)
                v.drain()
                v.tensor_scalar(dst_bf, dst_bf, -1.0, None, op0=ALU.add).then_inc(sDV.h, 1)
                sDV.v += 1
                v.drain()

    def emit_ffn_quarter(q4, prev):
        pe_b, dv_b, ac_b = sPE.v, sDV.v, sAC.v

        def tern_of(j):
            return 16384 if j % 2 == 0 else 18432

        def tbuf4(j):
            r = j % 4
            if r < 2:
                return bav(16384 + r * 2048, 2048)
            return TB2[:, (r - 2) * 2048:(r - 2) * 2048 + 2048]

        sched = []
        if q4 == 0:
            sched += [('T', 0), ('T', 1), ('T', 2), ('T', 3)]
        for p in range(32):
            if q4 == 0:
                for jj in range(4 * p + 4, min(4 * p + 8, 128)):
                    sched.append(('T', jj))
            sched += [('E1', p), ('E3', p)]
        tern_dv, ev1_dv, ev3_dv = {}, {}, {}
        c = dv_b
        for k, i in sched:
            c += 1
            {'T': tern_dv, 'E1': ev1_dv, 'E3': ev3_dv}[k][i] = c
        sDV.v = c
        sAC.v = ac_b + 32
        ld_toks = []
        spl_toks = {}
        with nc.Block() as blk:
            @blk.sync
            def _(q):
                for j in range(128):
                    oc, which = j // 2, j % 2
                    wsrcT = wgT if which == 0 else wuT
                    scr = tG if which == 0 else tU
                    if q4 == 0:
                        waits = [(sDV.h, tern_dv[j - 2])] if j >= 2 else []
                        ld_toks.append(PL.load(
                            q, f32v(AR4, 8192 + which * 2048, 2048).rearrange(
                                "p (ci f) -> p ci f", f=128),
                            wsrcT[:, oc * 128:(oc + 1) * 128].rearrange(
                                "(ci p) f -> p ci f", p=128),
                            waits=waits))
                        if j >= 1:
                            jj = j - 1
                            sc2 = tG if jj % 2 == 0 else tU
                            spl_toks[jj] = PL.load(
                                q, sc2[:, (jj // 2) * 2048:(jj // 2 + 1) * 2048],
                                bav(tern_of(jj), 2048), waits=[(sDV.h, tern_dv[jj])])
                    else:
                        if j >= 4:
                            waits = [(sPE.h, pe_b + j - 3)]
                        elif prev is not None:
                            nb_p = prev['nbuf']
                            waits = []
                            if j < nb_p:
                                waits.append((sPE.h, prev['down_pe_end'] - nb_p + j + 1))
                            if j < 2 and prev['down_spl'][j] is not None:
                                waits.append(prev['down_spl'][j])
                        else:
                            waits = []
                        ld_toks.append(PL.load(
                            q, tbuf4(j),
                            scr[:, oc * 2048:(oc + 1) * 2048], waits=waits))
                if q4 == 0:
                    jj = 127
                    spl_toks[jj] = PL.load(
                        q, tU[:, (jj // 2) * 2048:(jj // 2 + 1) * 2048],
                        bav(tern_of(jj), 2048), waits=[(sDV.h, tern_dv[jj])])

            @blk.vector
            def _(v):
                for kind, i in sched:
                    if kind == 'T':
                        j = i
                        wof = f32v(AR4, 8192 + (j % 2) * 2048, 2048)
                        dst = bav(tern_of(j), 2048)
                        kindcol = 0 if j % 2 == 0 else 1
                        v.wait_ge(*ld_toks[j])
                        if j >= 2:
                            v.wait_ge(sPE.h, pe_b + j - 1)
                            v.wait_ge(*spl_toks[j - 2])
                        v.tensor_scalar(dst, wof, ppt[:, 7 + kindcol:8 + kindcol], None,
                                        op0=ALU.is_gt)
                        v.drain()
                        v.scalar_tensor_tensor(dst, wof, ppt[:, 4 + kindcol:5 + kindcol],
                                               dst, op0=ALU.is_gt, op1=ALU.add)
                        v.drain()
                        v.tensor_scalar(dst, dst, -1.0, None, op0=ALU.add).then_inc(sDV.h, 1)
                        v.drain()
                    elif kind == 'E1':
                        p = i
                        gb_ = pbk[2 * (p % 2)]
                        v.wait_ge(sPE.h, pe_b + 4 * p + 4)
                        v.tensor_tensor(qt1[:, 0:512].rearrange("p (r t) -> p r t", t=QT),
                                        gb_[:, 0:512].rearrange("p (r t) -> p r t", t=QT),
                                        f32v(AR4, q4 * QT, QT).unsqueeze(1).broadcast_to(
                                            (128, 2, QT)),
                                        op=ALU.mult).then_inc(sDV.h, 1)
                        v.drain()
                    else:
                        p = i
                        ub_ = pbk[2 * (p % 2) + 1]
                        v.wait_ge(sAC.h, ac_b + p + 1)
                        v.tensor_tensor(f32v(AR1, 2 * p * QT, 512), qt1[:, 0:512],
                                        ub_[:, 0:512], op=ALU.mult).then_inc(sDV.h, 1)
                        v.drain()

            @blk.scalar
            def _(s):
                for p in range(32):
                    s.wait_ge(sDV.h, ev1_dv[p])
                    s.activation(qt1[:, 0:512], qt1[:, 0:512], AF.Silu).then_inc(sAC.h, 1)

            @blk.tensor
            def _(t):
                for j in range(128):
                    oc, which = j // 2, j % 2
                    p, slot = oc // 2, oc % 2
                    if q4 == 0:
                        t.wait_ge(sDV.h, tern_dv[j])
                    else:
                        t.wait_ge(*ld_toks[j])
                    if p >= 2 and slot == 0 and which == 0:
                        t.wait_ge(sDV.h, ev3_dv[p - 2])
                    if p == 0 and slot == 0 and which == 0 and prev is not None:
                        t.wait_ge(sDV.h, prev['down_ev15'])
                    bank = pbk[2 * (p % 2) + which]
                    tsrc = bav(tern_of(j), 2048) if q4 == 0 else tbuf4(j)
                    for ci in range(16):
                        mm = t.matmul(bank[:, slot * QT:slot * QT + QT],
                                      tsrc[:, ci * 128:(ci + 1) * 128],
                                      h2q_all[:, ci * 1024 + q4 * QT:ci * 1024 + q4 * QT + QT],
                                      start=(ci == 0), stop=(ci == 15))
                    mm.then_inc(sPE.h, 1)
        sPE.v = pe_b + 128

        pe0_ = sPE.v
        dv0_ = sDV.v
        with nc.Block() as blk:
            @blk.tensor
            def _(t):
                for k in range(64):
                    pair = k // 2
                    t.wait_ge(sDV.h, ev3_dv[k // 2])
                    if pair >= 2 and k % 2 == 0:
                        t.wait_ge(sDV.h, dv0_ + pair - 1)
                    bank = pbk[4 + pair % 2]
                    t.matmul(bank[:, (k % 2) * QT:(k % 2) * QT + QT], hadt[:], utile(k),
                             start=True, stop=True).then_inc(sPE.h, 1); sPE.v += 1

            @blk.vector
            def _(v):
                for pair in range(32):
                    v.wait_ge(sPE.h, pe0_ + 2 * pair + 2)
                    bank = pbk[4 + pair % 2]
                    v.tensor_copy(f32v(AR1, pair * 512, 512), bank[:, 0:512]).then_inc(sDV.h, 1)
                    sDV.v += 1

        with nc.Block() as blk:
            @blk.vector
            def _(v):
                for s in range(6):
                    d = 1 << s
                    rest = d * QT
                    av = u_all.rearrange("p (blk r2) -> p blk r2", r2=2 * rest)[:, :, 0:rest]
                    bv = u_all.rearrange("p (blk r2) -> p blk r2", r2=2 * rest)[:, :, rest:2 * rest]
                    v.tensor_tensor(av, av, bv, op=ALU.add)
                    v.drain()
                    last = v.scalar_tensor_tensor(bv, bv, -2.0, av, op0=ALU.mult, op1=ALU.add)
                    v.drain()
                last.then_inc(sDV.h, 1); sDV.v += 1

        with nc.Block() as blk:
            @blk.vector
            def _(v):
                v.tensor_reduce(f32v(AR4, 2048, QT),
                                u_all.rearrange("p (k t) -> p t k", t=QT),
                                axis=AX.X, op=ALU.max,
                                apply_absolute_value=True).then_inc(sDV.h, 1); sDV.v += 1

        emit_pmax_bcast(f32v(AR4, 2048, QT), QT,
                        extra_waits=([prev['y15']] if prev is not None else []))
        with nc.Block() as blk:
            @blk.vector
            def _(v):
                v.wait_ge(sDV.h, sDV.v)
                v.tensor_tensor(f32v(AR4, 2048, QT), f32v(AR4, 2048, QT),
                                f32v(AR4, 1024 + q4 * QT, QT), op=ALU.mult)
                v.drain()
                v.tensor_scalar(f32v(AR4, 2048, QT), f32v(AR4, 2048, QT), C_FW, EPS,
                                op0=ALU.mult, op1=ALU.max)
                v.drain()
                v.reciprocal(f32v(AR4, 2560, QT), f32v(AR4, 2048, QT))
                v.drain()
                v.tensor_tensor(f32v(AR4, 2560, QT), f32v(AR4, 2560, QT),
                                f32v(AR4, 1024 + q4 * QT, QT), op=ALU.mult)
                v.drain()
                v.tensor_scalar(f32v(AR4, 2560, QT), f32v(AR4, 2560, QT), QB * C_FW, None,
                                op0=ALU.mult)
                v.tensor_scalar(f32v(AR4, 2816, QT), f32v(AR4, 2048, QT), ppt[:, 3:4],
                                1.0 / QB, op0=ALU.mult, op1=ALU.mult).then_inc(sDV.h, 1)
                sDV.v += 1
        zq_waits = [(sPE.h, prev['down_pe_end'])] if prev is not None else []
        slab_dv = emit_quant_slabs(lambda s: f32v(AR1, s * 4096, 4096),
                                   f32v(AR4, 2560, QT), QT, 4, 20480,
                                   extra_waits=zq_waits)

        gup_tern_dv, gup_spl, gup_pe_end = tern_dv, spl_toks, pe_b + 128
        pe_b, dv_b = sPE.v, sDV.v
        sched = []
        for fo in range(16):
            if q4 == 0:
                sched += [('T', 4 * fo + s) for s in range(4)]
            sched += [('EV', fo)]
        tern_dv, ev_dv = {}, {}
        c = dv_b
        for k, i in sched:
            c += 1
            {'T': tern_dv, 'EV': ev_dv}[k][i] = c
        sDV.v = c
        ld_toks = []
        spl_toks = {}
        x2_toks = [None] * 16
        y_toks = [None] * 17
        with nc.Block() as blk:
            @blk.sync
            def _(q):
                for fo in range(16):
                    if fo >= 2:
                        xw = [(sDV.h, ev_dv[fo - 2])]
                    elif prev is not None:
                        xw = [(sDV.h, prev['down_ev14'] if fo == 0 else prev['down_ev15'])]
                    else:
                        xw = []
                    x2_toks[fo] = PL.load(
                        q, f32v(AR4, 3328 + (fo % 2) * 256, QT),
                        x2scr[fo * 128:(fo + 1) * 128, q4 * QT:(q4 + 1) * QT], waits=xw)
                    for sub in range(4):
                        j = 4 * fo + sub
                        if q4 == 0:
                            waits = ([(sDV.h, tern_dv[j - 2])] if j >= 2
                                     else [(sDV.h, gup_tern_dv[126 + j]),
                                           (sDV.h, slab_dv[3])])
                            ld_toks.append(PL.load(
                                q, f32v(AR4, 8192 + (j % 2) * 2048, 2048).rearrange(
                                    "p (ci f) -> p ci f", f=128),
                                wdT[sub * 2048:(sub + 1) * 2048,
                                    fo * 128:(fo + 1) * 128].rearrange(
                                        "(ci p) f -> p ci f", p=128),
                                waits=waits))
                            if j >= 1:
                                jj = j - 1
                                spl_toks[jj] = PL.load(
                                    q, tD[:, jj * 2048:(jj + 1) * 2048],
                                    bav(tern_of(jj), 2048), waits=[(sDV.h, tern_dv[jj])])
                        else:
                            waits = ([(sPE.h, pe_b + j - 3)] if j >= 4
                                     else [(sPE.h, gup_pe_end - 4 + j + 1)])
                            ld_toks.append(PL.load(
                                q, tbuf4(j),
                                tD[:, j * 2048:(j + 1) * 2048], waits=waits))
                    if fo >= 1:
                        y_toks[fo - 1] = PO.load(
                            q, y[(fo - 1) * 128:fo * 128, q4 * QT:(q4 + 1) * QT],
                            qt2[:, 0:QT], waits=[(sDV.h, ev_dv[fo - 1])])
                if q4 == 0:
                    jj = 63
                    spl_toks[jj] = PL.load(
                        q, tD[:, jj * 2048:(jj + 1) * 2048],
                        bav(tern_of(jj), 2048), waits=[(sDV.h, tern_dv[jj])])
                y_toks[15] = PO.load(
                    q, y[15 * 128:16 * 128, q4 * QT:(q4 + 1) * QT],
                    qt2[:, 0:QT], waits=[(sDV.h, ev_dv[15])])

            @blk.vector
            def _(v):
                for kind, i in sched:
                    if kind == 'T':
                        j = i
                        wof = f32v(AR4, 8192 + (j % 2) * 2048, 2048)
                        dst = bav(tern_of(j), 2048)
                        v.wait_ge(*ld_toks[j])
                        if j >= 2:
                            v.wait_ge(sPE.h, pe_b + j - 1)
                            v.wait_ge(*spl_toks[j - 2])
                        else:
                            v.wait_ge(sPE.h, gup_pe_end - 1 + j)
                            v.wait_ge(*gup_spl[126 + j])
                        v.tensor_scalar(dst, wof, ppt[:, 9:10], None, op0=ALU.is_gt)
                        v.drain()
                        v.scalar_tensor_tensor(dst, wof, ppt[:, 6:7], dst,
                                               op0=ALU.is_gt, op1=ALU.add)
                        v.drain()
                        v.tensor_scalar(dst, dst, -1.0, None, op0=ALU.add).then_inc(sDV.h, 1)
                        v.drain()
                    else:
                        fo = i
                        v.wait_ge(sPE.h, pe_b + 4 * fo + 4)
                        v.wait_ge(*x2_toks[fo])
                        if fo >= 1 and y_toks[fo - 1] is not None:
                            v.wait_ge(*y_toks[fo - 1])
                        elif fo == 0 and prev is not None:
                            v.wait_ge(*prev['y15'])
                        v.tensor_tensor(qt1[:, 0:QT], pbk[fo % 2][:, 0:QT],
                                        f32v(AR4, 2816, QT), op=ALU.mult)
                        v.drain()
                        v.scalar_tensor_tensor(qt2[:, 0:QT], qt1[:, 0:QT],
                                               mvec[:, 80 + fo:81 + fo],
                                               f32v(AR4, 3328 + (fo % 2) * 256, QT),
                                               op0=ALU.mult, op1=ALU.add).then_inc(sDV.h, 1)
                        v.drain()

            @blk.tensor
            def _(t):
                for j in range(64):
                    fo, sub = j // 4, j % 4
                    if q4 == 0:
                        t.wait_ge(sDV.h, tern_dv[j])
                    else:
                        t.wait_ge(*ld_toks[j])
                    t.wait_ge(sDV.h, slab_dv[sub])
                    if sub == 0 and fo >= 2:
                        t.wait_ge(sDV.h, ev_dv[fo - 2])
                    tsrc = bav(tern_of(j), 2048) if q4 == 0 else tbuf4(j)
                    for ci in range(16):
                        k = sub * 16 + ci
                        mm = t.matmul(pbk[fo % 2][:, 0:QT],
                                      tsrc[:, ci * 128:(ci + 1) * 128],
                                      zqv(k), start=(k == 0), stop=(k == 63))
                    mm.then_inc(sPE.h, 1)
        sPE.v = pe_b + 64
        return {'down_ev14': ev_dv[14], 'down_ev15': ev_dv[15],
                'down_pe_end': pe_b + 64, 'y15': y_toks[15],
                'down_spl': (spl_toks.get(62), spl_toks.get(63)),
                'nbuf': 2 if q4 == 0 else 4}

    emit_h2q()
    fence()
    prev = None
    for q4 in range(4):
        prev = emit_ffn_quarter(q4, prev)
    fence()

    with nc.Block() as blk:
        @blk.sync
        def _(q):
            for h, v in PO.toks():
                q.wait_ge(h, v)

    return nc


def kernel(**inputs):
    x = np.asarray(inputs["L0_attn_x"])
    Wq = np.asarray(inputs["L0_attn_Wq"])
    Wk = np.asarray(inputs["L0_attn_Wk"])
    Wv = np.asarray(inputs["L0_attn_Wv"])
    Wo = np.asarray(inputs["L0_attn_Wo"])
    bo = np.asarray(inputs["L0_attn_bo"])
    lq1 = np.asarray(inputs["L0_attn_lq1"])
    lq2 = np.asarray(inputs["L0_attn_lq2"])
    lk1 = np.asarray(inputs["L0_attn_lk1"])
    lk2 = np.asarray(inputs["L0_attn_lk2"])
    wg = np.asarray(inputs["L0_ffn_w_gate"])
    wu = np.asarray(inputs["L0_ffn_w_up"])
    wd = np.asarray(inputs["L0_ffn_w_down"])
    c = np.asarray(inputs["L0_c"])
    wmod = np.asarray(inputs["L0_wmod"])

    pqk = _perm_qk()
    pv = _perm_v()
    pwo = _perm_wo_rows()

    wqT = np.ascontiguousarray(Wq.T[:, pqk])
    wkT = np.ascontiguousarray(Wk.T[:, pqk])
    wvT = np.ascontiguousarray(Wv.T[:, pv])
    woT = np.ascontiguousarray(Wo.T[pwo, :])
    wgT = np.ascontiguousarray(wg.T)
    wuT = np.ascontiguousarray(wu.T)
    wdT = np.ascontiguousarray(wd.T)

    shared = dict(
        wqT=wqT, wkT=wkT, wvT=wvT, woT=woT, wgT=wgT, wuT=wuT, wdT=wdT,
        wmod=np.ascontiguousarray(wmod),
        cvec=np.ascontiguousarray(c.reshape(1, DIM)),
        bo=np.ascontiguousarray(bo.reshape(1, DIM)),
        lq1=np.ascontiguousarray(lq1.reshape(1, D)),
        lq2=np.ascontiguousarray(lq2.reshape(1, D)),
        lk1=np.ascontiguousarray(lk1.reshape(1, D)),
        lk2=np.ascontiguousarray(lk2.reshape(1, D)),
        eye=np.eye(128, dtype=np.float32),
        had=_hadamard(128),
        onesc=np.ones((1, 128), dtype=np.float32),
        onesp=np.ones((128, 1), dtype=np.float32),
    )

    xt = x[0].T
    in_maps = []
    for i in range(NCORE):
        m = dict(shared)
        m["xT"] = np.ascontiguousarray(xt[:, i * TOKC:(i + 1) * TOKC])
        in_maps.append(m)

    nc = build_program()
    res = run_bass_kernel_spmd(nc, in_maps, core_ids=list(range(NCORE)))
    out = np.concatenate([np.asarray(r["y"]).T for r in res.results], axis=0)
    return np.ascontiguousarray(out.reshape(1, 8192, DIM))


# version 15
# speedup vs baseline: 1.0198x; 1.0198x over previous
import numpy as np

import concourse.bass as bass
import concourse.mybir as mybir
from concourse.bass_utils import run_bass_kernel_spmd

F32 = mybir.dt.float32
BF16 = mybir.dt.bfloat16
I32 = mybir.dt.int32
ALU = mybir.AluOpType
AF = mybir.ActivationFunctionType
AX = mybir.AxisListType

NCORE = 8
DIM = 2048
TOKC = 1024
NH = 16
D = 64
TWOH = 32
DI = 8192
LAM_INIT = 0.8 - 0.6 * float(np.exp(-0.3 * 12))
SCALE = D ** (-0.5)
EPS = 1e-5
QB = 128.0
C_FW = float(np.float32(1.0 / np.sqrt(DI)))
QT = 256


def _perm_qk():
    p = np.zeros(DIM, dtype=np.int64)
    for i in range(TWOH):
        pp, h = i // 16, i % 16
        for d in range(D):
            p[i * D + d] = h * 128 + 2 * d + pp
    return p


def _perm_v():
    p = np.zeros(DIM, dtype=np.int64)
    for e in range(128):
        for h in range(NH):
            p[e * NH + h] = h * 128 + e
    return p


def _perm_wo_rows():
    p = np.zeros(DIM, dtype=np.int64)
    for a in range(NH):
        for e in range(128):
            p[a + 16 * (e & 1) + 32 * (e >> 1)] = a * 128 + e
    return p


def _hadamard(n):
    h = np.array([[1.0]], dtype=np.float32)
    while h.shape[0] < n:
        h = np.block([[h, h], [h, -h]])
    return np.ascontiguousarray(h.astype(np.float32))


class Sem:
    def __init__(self, nc, name):
        self.h = nc.semaphore(name).__enter__()
        self.v = 0


def build_program():
    nc = bass.Bass("TRN2", target_bir_lowering=False)

    def din(name, shape, dt=F32):
        return nc.dram_tensor(name, list(shape), dt, kind="ExternalInput").ap()

    xT = din("xT", [DIM, TOKC])
    wqT = din("wqT", [DIM, DIM])
    wkT = din("wkT", [DIM, DIM])
    wvT = din("wvT", [DIM, DIM])
    woT = din("woT", [DIM, DIM])
    wgT = din("wgT", [DIM, DI])
    wuT = din("wuT", [DIM, DI])
    wdT = din("wdT", [DI, DIM])
    wmod = din("wmod", [DIM, 6 * DIM])
    cvec = din("cvec", [1, DIM])
    bo = din("bo", [1, DIM])
    lq1 = din("lq1", [1, D])
    lq2 = din("lq2", [1, D])
    lk1 = din("lk1", [1, D])
    lk2 = din("lk2", [1, D])
    eye = din("eye", [128, 128])
    had = din("had", [128, 128])
    onesc = din("onesc", [1, 128])
    onesp = din("onesp", [128, 1])

    y = nc.dram_tensor("y", [DIM, TOKC], F32, kind="ExternalOutput").ap()
    modscr = nc.dram_tensor("modscr", [1, 6 * DIM], F32, kind="Internal").ap()
    x2scr = nc.dram_tensor("x2scr", [DIM, TOKC], F32, kind="Internal").ap()
    vscr = nc.dram_tensor("vscr", [128, 8192], F32, kind="Internal").ap()
    qhscr = nc.dram_tensor("qhscr", [128, 8192], BF16, kind="Internal").ap()
    qlscr = nc.dram_tensor("qlscr", [128, 8192], BF16, kind="Internal").ap()
    khscr = nc.dram_tensor("khscr", [128, 8192], BF16, kind="Internal").ap()
    klscr = nc.dram_tensor("klscr", [128, 8192], BF16, kind="Internal").ap()
    tG = nc.dram_tensor("tG", [128, 64 * 2048], BF16, kind="Internal").ap()
    tU = nc.dram_tensor("tU", [128, 64 * 2048], BF16, kind="Internal").ap()
    tD = nc.dram_tensor("tD", [128, 64 * 2048], BF16, kind="Internal").ap()

    def sball(name, shape, dt=F32):
        return nc.sbuf_tensor(name, list(shape), dt).__enter__()

    AR1 = sball("AR1", [128, 16384])
    AR4 = sball("AR4", [128, 12288])
    BA = sball("BA", [128, 36864], BF16)
    ident = sball("ident", [128, 128])
    identb = sball("identb", [128, 128], BF16)
    hadt = sball("hadt", [128, 128])
    onescr = sball("onescr", [1, 128])
    onespc = sball("onespc", [128, 1])
    cfm = sball("cfm", [128, 16])
    bofm = sball("bofm", [128, 16])
    mvec = sball("mvec", [128, 96])

    lam1 = sball("lam1", [1, 16])
    ppt = sball("ppt", [128, 16])
    redp = sball("redp", [128, 200])
    TB2 = sball("TB2", [128, 4096], BF16)
    qt1 = sball("qt1", [128, 512])
    qt2 = sball("qt2", [128, 512])
    qt3 = sball("qt3", [128, 2048], I32)

    pbk = [nc.psum_tensor(f"pbk{i}", [128, 512], F32).__enter__() for i in range(6)]
    pbkB = [nc.psum_tensor(f"pbkB{i}", [128, 1024], BF16).__enter__() for i in range(2)]

    lrow = AR4[0:1, 0:256]
    rowb = AR1[0:1, 0:6 * DIM]
    rowc = AR4[0:1, 0:6 * DIM]
    srow = AR4[0:1, 6144:7168]
    qrow = AR4[0:1, 7168:8192]
    arow = AR1[0:1, 12288:13312]
    brow = AR1[0:1, 13312:14336]
    a2row = AR1[0:1, 14336:15360]
    b2row = AR1[0:1, 15360:16384]

    sPE = Sem(nc, "sPE")
    sDV = Sem(nc, "sDV")
    sAC = Sem(nc, "sAC")

    class DPool:
        def __init__(self, name, n):
            self.sems = [Sem(nc, f"{name}{i}") for i in range(n)]
            self.i = 0

        def load(self, q, out, in_, waits=(), slow=False):
            s = self.sems[self.i % len(self.sems)]
            self.i += 1
            q.wait_ge(s.h, s.v)
            for h, v in waits:
                q.wait_ge(h, v)
            kw = dict(allow_slow_non_contiguous=True) if slow else {}
            q.dma_start(out=out, in_=in_, **kw).then_inc(s.h, 16)
            s.v += 16
            return (s.h, s.v)

        def toks(self):
            return [(s.h, s.v) for s in self.sems]

    PL = DPool("pl", 8)
    PO = DPool("po", 4)

    def f32v(ar, c0, n):
        return ar[:, c0:c0 + n]

    def bav(c0, n):
        return BA[:, c0:c0 + n]

    def fence():
        vals = ([(s.h, s.v) for s in (sPE, sDV, sAC) if s.v > 0]
                + [(h, v) for (h, v) in PL.toks() + PO.toks() if v > 0])
        with nc.Block() as blk:
            @blk.tensor
            def _(e):
                for h, v in vals:
                    e.wait_ge(h, v)

            @blk.vector
            def _(e):
                for h, v in vals:
                    e.wait_ge(h, v)

            @blk.scalar
            def _(e):
                for h, v in vals:
                    e.wait_ge(h, v)

            @blk.gpsimd
            def _(e):
                for h, v in vals:
                    e.wait_ge(h, v)

            @blk.sync
            def _(e):
                for h, v in vals:
                    e.wait_ge(h, v)

    with nc.Block() as blk:
        @blk.sync
        def _(q):
            for dst, src in [(ident[:], eye[:]), (hadt[:], had[:]),
                             (onescr[:], onesc[:]), (onespc[:], onesp[:]),
                             (lrow[:, 0:64], lq1[:]), (lrow[:, 64:128], lq2[:]),
                             (lrow[:, 128:192], lk1[:]), (lrow[:, 192:256], lk2[:])]:
                PL.load(q, dst, src)
            PL.load(q, cfm[:], cvec.rearrange("o (k p) -> (o p) k", p=128), slow=True)
            PL.load(q, bofm[:], bo.rearrange("o (k p) -> (o p) k", p=128), slow=True)

    fence()
    with nc.Block() as blk:
        @blk.vector
        def _(v):
            v.tensor_copy(identb[:], ident[:])
            v.tensor_tensor(lrow[:, 0:64], lrow[:, 0:64], lrow[:, 128:192], op=ALU.mult)
            v.drain()
            v.tensor_tensor(lrow[:, 64:128], lrow[:, 64:128], lrow[:, 192:256], op=ALU.mult)
            v.drain()
            v.tensor_reduce(lam1[:, 0:1], lrow[:, 0:64], axis=AX.X, op=ALU.add)
            v.drain()
            v.tensor_reduce(lam1[:, 1:2], lrow[:, 64:128], axis=AX.X,
                            op=ALU.add).then_inc(sDV.h, 1); sDV.v += 1

        @blk.scalar
        def _(s):
            s.wait_ge(sDV.h, sDV.v)
            s.activation(lam1[:, 2:4], lam1[:, 0:2], AF.Exp).then_inc(sAC.h, 1); sAC.v += 1

        @blk.vector
        def _(v):
            v.wait_ge(sAC.h, sAC.v)
            v.tensor_tensor(lam1[:, 4:5], lam1[:, 3:4], lam1[:, 2:3], op=ALU.subtract)
            v.drain()
            v.tensor_scalar(lam1[:, 4:5], lam1[:, 4:5], -LAM_INIT, None,
                            op0=ALU.add).then_inc(sDV.h, 1); sDV.v += 1

        @blk.tensor
        def _(t):
            t.wait_ge(sDV.h, sDV.v)
            t.matmul(pbk[3][:, 0:1], onescr[:], lam1[0:1, 4:5], start=True,
                     stop=True).then_inc(sPE.h, 1); sPE.v += 1

        @blk.vector
        def _(v):
            v.wait_ge(sPE.h, sPE.v)
            v.tensor_copy(ppt[:, 0:1], pbk[3][:, 0:1]).then_inc(sDV.h, 1); sDV.v += 1

    with nc.Block() as blk:
        @blk.scalar
        def _(s):
            s.activation(cfm[:], cfm[:], AF.Silu).then_inc(sAC.h, 1); sAC.v += 1

    pe0 = sPE.v
    dv0 = sDV.v
    wm_toks = []
    with nc.Block() as blk:
        @blk.sync
        def _(q):
            for m in range(24):
                waits = [(sPE.h, pe0 + m - 1)] if m >= 2 else []
                base = (m % 2) * 8192
                wm_toks.append(PL.load(
                    q, f32v(AR1, base, 8192).rearrange("p (k c) -> p k c", c=512),
                    wmod[:, m * 512:(m + 1) * 512].rearrange("(k p) c -> p k c", p=128),
                    waits=waits))

        @blk.tensor
        def _(t):
            t.wait_ge(sAC.h, sAC.v)
            for m in range(24):
                t.wait_ge(*wm_toks[m])
                if m >= 2:
                    t.wait_ge(sDV.h, dv0 + m - 1)
                base = (m % 2) * 8192
                bank = pbk[m % 2]
                for k in range(16):
                    mm = t.matmul(bank[0:1, :], cfm[:, k:k + 1], f32v(AR1, base + k * 512, 512),
                                  start=(k == 0), stop=(k == 15))
                mm.then_inc(sPE.h, 1); sPE.v += 1

        @blk.vector
        def _(v):
            for m in range(24):
                v.wait_ge(sPE.h, pe0 + m + 1)
                v.tensor_copy(rowc[:, m * 512:(m + 1) * 512],
                              pbk[m % 2][0:1, :]).then_inc(sDV.h, 1); sDV.v += 1

    fence()
    with nc.Block() as blk:
        @blk.vector
        def _(v):
            v.tensor_copy(rowb[:, 0:DIM], rowc[:, 0:DIM])
            v.tensor_scalar(rowb[:, DIM:2 * DIM], rowc[:, DIM:2 * DIM], 1.0, None, op0=ALU.add)
            v.tensor_copy(rowb[:, 3 * DIM:4 * DIM], rowc[:, 3 * DIM:4 * DIM])
            v.tensor_scalar(rowb[:, 4 * DIM:5 * DIM], rowc[:, 4 * DIM:5 * DIM], 1.0, None, op0=ALU.add)
            last = None
            for o in (2, 5):
                g0 = rowc[:, o * DIM:(o + 1) * DIM]
                gc = rowb[:, o * DIM:(o + 1) * DIM]
                gneg = rowc[:, 0:DIM]
                v.tensor_scalar(gc, g0, 0.0, 2.0, op0=ALU.is_ge, op1=ALU.mult)
                v.drain()
                v.tensor_scalar(gc, gc, -1.0, None, op0=ALU.add)
                v.drain()
                v.tensor_scalar(gneg, g0, -1.0, None, op0=ALU.mult)
                v.drain()
                v.tensor_tensor(g0, g0, gneg, op=ALU.max)
                v.drain()
                v.tensor_scalar(g0, g0, 0.25, None, op0=ALU.max)
                v.drain()
                last = v.tensor_tensor(gc, gc, g0, op=ALU.mult)
                v.drain()
            last.then_inc(sDV.h, 1); sDV.v += 1

        @blk.sync
        def _(q):
            spill = PL.load(q, modscr[:], rowb, waits=[(sDV.h, sDV.v)])
            PL.load(q, mvec[:],
                    modscr.rearrange("o (s k p) -> (o p) (s k)", p=128, k=16),
                    waits=[spill], slow=True)

    fence()
    pe0 = sPE.v
    ac0 = sAC.v
    x_toks = []
    with nc.Block() as blk:
        @blk.sync
        def _(q):
            for ci in range(16):
                waits = [(sPE.h, pe0 + ci - 1)] if ci >= 2 else []
                x_toks.append(PL.load(q, f32v(AR4, (ci % 2) * 1024, 1024),
                                      xT[ci * 128:(ci + 1) * 128, :], waits=waits))

        @blk.scalar
        def _(s):
            for ci in range(16):
                s.wait_ge(*x_toks[ci])
                s.activation(f32v(AR4, 2048 + (ci % 2) * 1024, 1024),
                             f32v(AR4, (ci % 2) * 1024, 1024),
                             AF.Square).then_inc(sAC.h, 1); sAC.v += 1

        @blk.tensor
        def _(t):
            for ci in range(16):
                t.wait_ge(sAC.h, ac0 + ci + 1)
                xc = (ci % 2) * 1024
                for hh in range(2):
                    t.matmul(pbk[2 + hh][0:1, :], onespc[:], f32v(AR4, xc + hh * 512, 512),
                             start=(ci == 0), stop=(ci == 15))
                    mm = t.matmul(pbk[4 + hh][0:1, :], onespc[:],
                                  f32v(AR4, 2048 + xc + hh * 512, 512),
                                  start=(ci == 0), stop=(ci == 15))
                mm.then_inc(sPE.h, 1); sPE.v += 1

    fence()

    def emit_ln_rows(adst_a, adst_b, dst_bc, from_psum):
        with nc.Block() as blk:
            @blk.vector
            def _(v):
                if from_psum:
                    v.tensor_scalar(srow[:, 0:512], pbk[2][0:1, :], 1.0 / DIM, None, op0=ALU.mult)
                    v.tensor_scalar(srow[:, 512:1024], pbk[3][0:1, :], 1.0 / DIM, None, op0=ALU.mult)
                    v.tensor_scalar(qrow[:, 0:512], pbk[4][0:1, :], 1.0 / DIM, None, op0=ALU.mult)
                    v.tensor_scalar(qrow[:, 512:1024], pbk[5][0:1, :], 1.0 / DIM, None, op0=ALU.mult)
                    v.drain()
                v.tensor_tensor(rowb[0:1, 0:1024], srow[:, :], srow[:, :], op=ALU.mult)
                v.drain()
                v.tensor_tensor(qrow[:, :], qrow[:, :], rowb[0:1, 0:1024], op=ALU.subtract)
                v.drain()
                v.tensor_scalar(qrow[:, :], qrow[:, :], EPS, None, op0=ALU.add)
                v.drain()
                v.reciprocal(qrow[:, :], qrow[:, :]).then_inc(sDV.h, 1); sDV.v += 1

            @blk.scalar
            def _(s):
                s.wait_ge(sDV.h, sDV.v)
                s.activation(adst_a[:, :], qrow[:, :], AF.Sqrt).then_inc(sAC.h, 1); sAC.v += 1

            @blk.vector
            def _(v):
                v.wait_ge(sAC.h, sAC.v)
                v.tensor_tensor(adst_b[:, :], srow[:, :], adst_a[:, :], op=ALU.mult)
                v.drain()
                v.tensor_scalar(adst_b[:, :], adst_b[:, :], -1.0, None,
                                op0=ALU.mult).then_inc(sDV.h, 1); sDV.v += 1

            @blk.tensor
            def _(t):
                t.wait_ge(sDV.h, sDV.v)
                for hh in range(2):
                    t.matmul(pbk[hh][:, :], onescr[:], adst_a[:, hh * 512:(hh + 1) * 512],
                             start=True, stop=True)
                    mm = t.matmul(pbk[2 + hh][:, :], onescr[:], adst_b[:, hh * 512:(hh + 1) * 512],
                                  start=True, stop=True)
                mm.then_inc(sPE.h, 1); sPE.v += 1

            @blk.vector
            def _(v):
                v.wait_ge(sPE.h, sPE.v)
                for hh in range(2):
                    v.tensor_copy(f32v(AR4, dst_bc + hh * 512, 512), pbk[hh][:, :])
                    cp = v.tensor_copy(f32v(AR4, dst_bc + 1024 + hh * 512, 512), pbk[2 + hh][:, :])
                cp.then_inc(sDV.h, 1); sDV.v += 1

    emit_ln_rows(arow, brow, 4096, True)
    fence()

    def _p2_chunk_src(i):
        if i < 64:
            rb, cb = i // 4, i % 4
            return wgT[rb * 128:(rb + 1) * 128, cb * 2048:(cb + 1) * 2048]
        if i < 128:
            j = i - 64
            rb, cb = j // 4, j % 4
            return wuT[rb * 128:(rb + 1) * 128, cb * 2048:(cb + 1) * 2048]
        j = i - 128
        return wdT[j * 128:(j + 1) * 128, :]

    p2_state = {"i": 0, "dv": [None, None]}

    def emit_p2_sprinkle(count):
        toks = []
        base = p2_state["i"]
        dv_b2 = sDV.v
        with nc.Block() as blk:
            @blk.sync
            def _(q):
                for n in range(count):
                    i = base + n
                    bi = i % 2
                    if n >= 2:
                        waits = [(sDV.h, dv_b2 + n - 1)]
                    elif p2_state["dv"][bi] is not None:
                        waits = [(sDV.h, p2_state["dv"][bi])]
                    else:
                        waits = []
                    toks.append(PL.load(q, f32v(AR4, 8192 + bi * 2048, 2048),
                                        _p2_chunk_src(i), waits=waits))

            @blk.vector
            def _(v):
                for n in range(count):
                    i = base + n
                    bi = i % 2
                    v.wait_ge(*toks[n])
                    v.tensor_reduce(redp[:, i:i + 1], f32v(AR4, 8192 + bi * 2048, 2048),
                                    axis=AX.X, op=ALU.add,
                                    apply_absolute_value=True).then_inc(sDV.h, 1)
                    sDV.v += 1
                    p2_state["dv"][bi] = sDV.v
        p2_state["i"] += count


    def emit_h1(half):
        dv0 = sDV.v
        h_toks = []
        with nc.Block() as blk:
            @blk.sync
            def _(q):
                for ci in range(16):
                    waits = [(sDV.h, dv0 + ci - 1)] if ci >= 2 else []
                    h_toks.append(PL.load(q, f32v(AR4, (ci % 2) * 512, 512),
                                          xT[ci * 128:(ci + 1) * 128, half * 512:(half + 1) * 512],
                                          waits=waits))

            @blk.vector
            def _(v):
                for ci in range(16):
                    v.wait_ge(*h_toks[ci])
                    xc = f32v(AR4, (ci % 2) * 512, 512)
                    t1 = f32v(AR4, 1024, 512)
                    v.tensor_tensor(t1, xc, f32v(AR4, 4096 + half * 512, 512), op=ALU.mult)
                    v.drain()
                    v.tensor_tensor(t1, t1, f32v(AR4, 5120 + half * 512, 512), op=ALU.add)
                    v.drain()
                    v.tensor_scalar(t1, t1, mvec[:, 16 + ci:17 + ci], mvec[:, ci:ci + 1],
                                    op0=ALU.mult, op1=ALU.add)
                    v.drain()
                    v.tensor_copy(bav(0, 8192)[:, ci * 512:(ci + 1) * 512], t1)
                    v.drain()
                    v.scalar_tensor_tensor(bav(8192, 8192)[:, ci * 512:(ci + 1) * 512],
                                           t1, 1.0, bav(0, 8192)[:, ci * 512:(ci + 1) * 512],
                                           op0=ALU.bypass,
                                           op1=ALU.subtract).then_inc(sDV.h, 1); sDV.v += 1
                    v.drain()

    tms_spills = [None, None]

    def emit_proj(half, wsrc, dst_kind):
        h1H = bav(0, 8192)
        h1L = bav(8192, 8192)
        scr_hi = {'q': qhscr, 'k': khscr}.get(dst_kind)
        scr_lo = {'q': qlscr, 'k': klscr}.get(dst_kind)
        spills = tms_spills
        pe_b, dv_b, pe_prev = sPE.v, sDV.v, sPE.v

        sched = [('S', 0), ('S', 1)]
        for fc in range(16):
            sched.append(('E1', fc)); sched.append(('E2', fc))
            if fc + 2 < 16:
                sched.append(('S', fc + 2))
        split_dv, ev1_dv, ev2_dv = {}, {}, {}
        c = dv_b
        for k, i in sched:
            c += 1
            {'S': split_dv, 'E1': ev1_dv, 'E2': ev2_dv}[k][i] = c
        sDV.v = c
        pe_mm = lambda fc: pe_b + (1 if fc == 0 else 2 * fc)
        pe_tr = lambda fc: pe_b + (2 * fc + 3 if fc < 15 else 32)
        ld_toks = []
        spill_slots = [list(spills)] + [None] * 16

        with nc.Block() as blk:
            @blk.sync
            def _(q):
                def emit_load(fc):
                    bi = fc % 2
                    w32 = f32v(AR1, bi * 2048, 2048)
                    waits = [(sDV.h, split_dv[fc - 2])] if fc >= 2 else []
                    ld_toks.append(PL.load(
                        q, w32.rearrange("p (ci f) -> p ci f", f=128),
                        wsrc[:, fc * 128:(fc + 1) * 128].rearrange("(ci p) f -> p ci f", p=128),
                        waits=waits))

                def emit_spill(fc):
                    if dst_kind == 'v':
                        t0 = PL.load(
                            q,
                            vscr.rearrange("p (g c) -> p g c", c=2048)[:, :, fc * 128:(fc + 1) * 128],
                            f32v(AR1, 4608, 512).rearrange("p (g c) -> p g c", c=128),
                            waits=[(sDV.h, ev2_dv[fc])])
                        spill_slots[fc + 1] = [t0, None]
                    else:
                        t0 = PL.load(
                            q,
                            scr_hi.rearrange("p (g c) -> p g c", c=2048)[:, :, fc * 128:(fc + 1) * 128],
                            bav(25600, 512).rearrange("p (g c) -> p g c", c=128),
                            waits=[(sDV.h, ev2_dv[fc])])
                        t1 = PL.load(
                            q,
                            scr_lo.rearrange("p (g c) -> p g c", c=2048)[:, :, fc * 128:(fc + 1) * 128],
                            bav(26112, 512).rearrange("p (g c) -> p g c", c=128))
                        spill_slots[fc + 1] = [t0, t1]

                emit_load(0)
                emit_load(1)
                for fc in range(16):
                    if fc + 2 < 16:
                        emit_load(fc + 2)
                    emit_spill(fc)

            @blk.vector
            def _(v):
                for kind, fc in sched:
                    bi = fc % 2
                    if kind == 'S':
                        w32 = f32v(AR1, bi * 2048, 2048)
                        wsH = bav(16384 + bi * 4096, 2048)
                        wsL = bav(18432 + bi * 4096, 2048)
                        v.wait_ge(*ld_toks[fc])
                        if fc >= 2:
                            v.wait_ge(sPE.h, pe_mm(fc - 2))
                        v.tensor_copy(wsH, w32)
                        v.drain()
                        v.scalar_tensor_tensor(wsL, w32, 1.0, wsH, op0=ALU.bypass,
                                               op1=ALU.subtract).then_inc(sDV.h, 1)
                        v.drain()
                    elif kind == 'E1':
                        v.wait_ge(sPE.h, pe_mm(fc))
                        bank = pbk[bi]
                        if dst_kind == 'v':
                            for sp in spill_slots[fc]:
                                if sp is not None:
                                    v.wait_ge(*sp)
                            v.tensor_copy(f32v(AR1, 4096, 512), bank[:, :]).then_inc(sDV.h, 1)
                            v.drain()
                        else:
                            v.tensor_copy(bav(24576, 512), bank[:, :])
                            v.drain()
                            v.scalar_tensor_tensor(bav(25088, 512), bank[:, :], 1.0,
                                                   bav(24576, 512), op0=ALU.bypass,
                                                   op1=ALU.subtract).then_inc(sDV.h, 1)
                            v.drain()
                    else:
                        v.wait_ge(sPE.h, pe_tr(fc))
                        if dst_kind == 'v':
                            for sp in spill_slots[fc]:
                                if sp is not None:
                                    v.wait_ge(*sp)
                            for tb in range(4):
                                cp = v.tensor_copy(
                                    f32v(AR1, 4608, 512)[:, tb * 128:(tb + 1) * 128],
                                    pbk[2 + tb % 2][:, (tb // 2) * 128:(tb // 2) * 128 + 128])
                            cp.then_inc(sDV.h, 1)
                            v.drain()
                        else:
                            for sp in spill_slots[fc]:
                                if sp is not None:
                                    v.wait_ge(*sp)
                            v.tensor_copy(bav(25600, 512), pbkB[0][:, 0:512])
                            v.tensor_copy(bav(26112, 512), pbkB[1][:, 0:512]).then_inc(sDV.h, 1)
                            v.drain()

            @blk.tensor
            def _(t):
                def emit_mm(fc):
                    bi = fc % 2
                    wsH = bav(16384 + bi * 4096, 2048)
                    wsL = bav(18432 + bi * 4096, 2048)
                    t.wait_ge(sDV.h, split_dv[fc])
                    if fc >= 2:
                        t.wait_ge(sDV.h, ev1_dv[fc - 2])
                    bank = pbk[bi]
                    first = True
                    for wof, mv in [(wsH, h1H), (wsH, h1L), (wsL, h1H)]:
                        for ci in range(16):
                            mm = t.matmul(bank[:, :], wof[:, ci * 128:(ci + 1) * 128],
                                          mv[:, ci * 512:(ci + 1) * 512],
                                          start=first, stop=(wof is wsL and ci == 15))
                            first = False
                    mm.then_inc(sPE.h, 1)

                def emit_tr(fc):
                    t.wait_ge(sDV.h, ev1_dv[fc])
                    if dst_kind == 'v':
                        for tb in range(4):
                            mm = t.transpose(pbk[2 + tb % 2][:, (tb // 2) * 128:(tb // 2) * 128 + 128],
                                             f32v(AR1, 4096 + tb * 128, 128), ident[:])
                    else:
                        for part, src in [(0, bav(24576, 512)), (1, bav(25088, 512))]:
                            for tb in range(4):
                                mm = t.transpose(pbkB[part][:, tb * 128:(tb + 1) * 128],
                                                 src[:, tb * 128:(tb + 1) * 128], identb[:])
                    mm.then_inc(sPE.h, 1)

                emit_mm(0)
                for fc in range(1, 16):
                    emit_mm(fc)
                    emit_tr(fc - 1)
                emit_tr(15)
        sPE.v = pe_b + 32
        spills[0] = spill_slots[16][0]
        spills[1] = spill_slots[16][1]

    grp_state = {'tr_dv': None, 'diffv_pe': None}

    def emit_group(half, g):
        qmh, qml = bav(0, 2048), bav(2048, 2048)
        kmh, kml = bav(4096, 2048), bav(6144, 2048)
        qgh, qgl = bav(8192, 2048), bav(10240, 2048)
        kgh, kgl = bav(12288, 2048), bav(14336, 2048)
        vG = f32v(AR4, 0, 2048)
        vm, vm2 = f32v(AR1, 0, 2048), f32v(AR1, 2048, 2048)
        attsb, atts2 = f32v(AR1, 4096, 1024), f32v(AR1, 5120, 1024)
        ddup, dT, dT2 = f32v(AR1, 6144, 1024), f32v(AR1, 7168, 1024), f32v(AR1, 8192, 1024)
        AM = f32v(AR1, 9216, 2048)
        AMsq = f32v(AR1, 4096, 2048)
        rmx, rsm = f32v(AR1, 11264, 32), f32v(AR1, 11296, 32)
        s2a, s2b = f32v(AR1, 11328, 32), f32v(AR1, 11360, 32)

        toks = []
        with nc.Block() as blk:
            @blk.sync
            def _(q):
                w0 = [(sDV.h, grp_state['tr_dv'])] if grp_state['tr_dv'] is not None else []
                toks.append(PL.load(q, vG, vscr[:, g * 2048:(g + 1) * 2048], waits=w0))
                toks.append(PL.load(q, qgh[:, :], qhscr[:, g * 2048:(g + 1) * 2048]))
                toks.append(PL.load(q, qgl[:, :], qlscr[:, g * 2048:(g + 1) * 2048]))
                toks.append(PL.load(q, kgh[:, :], khscr[:, g * 2048:(g + 1) * 2048]))
                toks.append(PL.load(q, kgl[:, :], klscr[:, g * 2048:(g + 1) * 2048]))

            @blk.vector
            def _(v):
                for tk in toks:
                    v.wait_ge(*tk)
                if grp_state['diffv_pe'] is not None:
                    v.wait_ge(sPE.h, grp_state['diffv_pe'])
                v.transpose(qmh, qgh)
                v.transpose(qml, qgl)
                v.transpose(kmh, kgh)
                v.transpose(kml, kgl)
                v.transpose(vm, vG)
                v.drain()
                v.stream_shuffle(vm2, vm, [(p + 16) % 32 for p in range(32)]).then_inc(sDV.h, 1)
                sDV.v += 1
                grp_state['tr_dv'] = sDV.v

        dv_min = sDV.v
        with nc.Block() as blk:
            @blk.tensor
            def _(t):
                t.wait_ge(sDV.h, dv_min)
                for tt in range(128):
                    pq = tt // 32
                    t32 = tt % 32
                    bank = pbk[4] if t32 < 16 else pbk[5]
                    col = (t32 % 16) * 32
                    idx = 0
                    for qm, km in [(qmh, kmh), (qmh, kml), (qml, kmh)]:
                        for dh in range(2):
                            st = qm[32 * pq:32 * pq + 32, :].rearrange(
                                "p (i c) -> p i c", c=64)[:, :, 32 * dh + t32]
                            mv = km[32 * pq:32 * pq + 32, :].rearrange(
                                "p (i c) -> p i c", c=64)[:, :, 32 * dh + t32]
                            mm = t.matmul(bank[32 * pq:32 * pq + 32, col:col + 32], st, mv,
                                          start=(idx == 0), stop=(idx == 5),
                                          tile_position=(32 * pq, 32 * pq))
                            idx += 1
                mm.then_inc(sPE.h, 1); sPE.v += 1

        pe_gram = sPE.v
        with nc.Block() as blk:
            @blk.vector
            def _(v):
                v.wait_ge(sPE.h, pe_gram)
                v.tensor_scalar(attsb[:, 0:512], pbk[4][:, :], SCALE, None, op0=ALU.mult)
                v.tensor_scalar(attsb[:, 512:1024], pbk[5][:, :], SCALE, None, op0=ALU.mult)
                v.drain()
                v.tensor_reduce(rmx, attsb.rearrange("p (s j) -> p s j", j=32),
                                axis=AX.X, op=ALU.max)
                v.drain()
                v.tensor_tensor(atts2.rearrange("p (s j) -> p s j", j=32),
                                attsb.rearrange("p (s j) -> p s j", j=32),
                                rmx.unsqueeze(2).broadcast_to((128, 32, 32)),
                                op=ALU.subtract).then_inc(sDV.h, 1); sDV.v += 1

            @blk.scalar
            def _(s):
                s.wait_ge(sDV.h, sDV.v)
                s.activation(attsb[:, :], atts2[:, :], AF.Exp).then_inc(sAC.h, 1); sAC.v += 1

            @blk.vector
            def _(v):
                v.wait_ge(sAC.h, sAC.v)
                v.tensor_reduce(rsm, attsb.rearrange("p (s j) -> p s j", j=32),
                                axis=AX.X, op=ALU.add)
                v.drain()
                v.reciprocal(rsm, rsm)
                v.drain()
                v.tensor_tensor(atts2.rearrange("p (s j) -> p s j", j=32),
                                attsb.rearrange("p (s j) -> p s j", j=32),
                                rsm.unsqueeze(2).broadcast_to((128, 32, 32)), op=ALU.mult)
                v.drain()
                v.stream_shuffle(attsb, atts2, [(p + 16) % 32 for p in range(32)])
                v.drain()
                for pq in range(4):
                    for dcp in range(2):
                        v.scalar_tensor_tensor(
                            ddup[32 * pq:32 * pq + 16, :].rearrange(
                                "p (s j) -> p s j", j=32)[:, :, dcp * 16:(dcp + 1) * 16],
                            attsb[32 * pq:32 * pq + 16, :].rearrange(
                                "p (s j) -> p s j", j=32)[:, :, 16:32],
                            ppt[32 * pq:32 * pq + 16, 0:1],
                            atts2[32 * pq:32 * pq + 16, :].rearrange(
                                "p (s j) -> p s j", j=32)[:, :, 0:16],
                            op0=ALU.mult, op1=ALU.add)
                        v.drain()
                v.transpose(dT, ddup)
                v.drain()
                v.tensor_copy(dT2.rearrange("p (s j) -> p s j", j=32)[:, :, 16:32],
                              dT.rearrange("p (s j) -> p s j", j=32)[:, :, 0:16]).then_inc(sDV.h, 1)
                sDV.v += 1

        dv_diff = sDV.v
        pe_dv0 = sPE.v
        dv_ev0 = sDV.v
        with nc.Block() as blk:
            @blk.tensor
            def _(t):
                t.wait_ge(sDV.h, dv_diff)
                for batch in range(4):
                    if batch >= 2:
                        t.wait_ge(sDV.h, dv_ev0 + batch - 1)
                    for w in range(8):
                        s = batch * 8 + w
                        for pq in range(4):
                            be = pbk[2] if batch % 2 == 0 else pbk[3]
                            st_e = dT[32 * pq:32 * pq + 16, s * 32:s * 32 + 32]
                            st_o = dT2[32 * pq:32 * pq + 16, s * 32:s * 32 + 32]
                            mv_e = vm[32 * pq:32 * pq + 16, :].rearrange(
                                "p (u c) -> p u c", c=32)[:, :, s]
                            mv_o = vm2[32 * pq:32 * pq + 16, :].rearrange(
                                "p (u c) -> p u c", c=32)[:, :, s]
                            t.matmul(be[32 * pq:32 * pq + 32, w * 64:w * 64 + 64], st_e, mv_e,
                                     start=True, stop=False, tile_position=(32 * pq, 32 * pq))
                            mm = t.matmul(be[32 * pq:32 * pq + 32, w * 64:w * 64 + 64], st_o, mv_o,
                                          start=False, stop=True, tile_position=(32 * pq, 32 * pq))
                    mm.then_inc(sPE.h, 1); sPE.v += 1
                grp_state['diffv_pe'] = sPE.v

            @blk.vector
            def _(v):
                for batch in range(4):
                    v.wait_ge(sPE.h, pe_dv0 + batch + 1)
                    be = pbk[2] if batch % 2 == 0 else pbk[3]
                    v.tensor_copy(
                        AM.rearrange("p (u c) -> p u c", c=32)[:, :, batch * 8:batch * 8 + 8],
                        be[:, :].rearrange("p (w u) -> p u w", u=64)).then_inc(sDV.h, 1)
                    sDV.v += 1

        with nc.Block() as blk:
            @blk.scalar
            def _(s):
                s.wait_ge(sDV.h, sDV.v)
                s.activation(AMsq, AM, AF.Square).then_inc(sAC.h, 1); sAC.v += 1

            @blk.vector
            def _(v):
                v.wait_ge(sAC.h, sAC.v)
                v.tensor_reduce(s2a, AMsq.rearrange("p (u c) -> p c u", c=32),
                                axis=AX.X, op=ALU.add)
                v.drain()
                v.stream_shuffle(s2b, s2a, [(p + 16) % 32 for p in range(32)])
                v.drain()
                v.tensor_tensor(s2a, s2a, s2b, op=ALU.add)
                v.drain()
                v.tensor_scalar(s2a, s2a, 1.0 / 128.0, EPS, op0=ALU.mult, op1=ALU.add)
                v.drain()
                v.reciprocal(s2a, s2a).then_inc(sDV.h, 1); sDV.v += 1

            @blk.scalar
            def _(s):
                s.wait_ge(sDV.h, sDV.v)
                s.activation(s2a, s2a, AF.Sqrt,
                             scale=(1.0 - LAM_INIT) ** 2).then_inc(sAC.h, 1); sAC.v += 1

            @blk.vector
            def _(v):
                v.wait_ge(sAC.h, sAC.v)
                v.tensor_tensor(AM.rearrange("p (u c) -> p u c", c=32),
                                AM.rearrange("p (u c) -> p u c", c=32),
                                s2a.unsqueeze(1).broadcast_to((128, 64, 32)), op=ALU.mult)
                v.drain()
                v.tensor_copy(bav(16384, 2048), AM)
                v.drain()
                v.scalar_tensor_tensor(bav(18432, 2048), AM, 1.0, bav(16384, 2048),
                                       op0=ALU.bypass, op1=ALU.subtract)
                v.drain()
                v.transpose(bav(16384, 2048), bav(16384, 2048))
                v.drain()
                v.transpose(bav(18432, 2048), bav(18432, 2048)).then_inc(sDV.h, 1); sDV.v += 1

            dv_t32 = sDV.v
            @blk.tensor
            def _(t):
                t.wait_ge(sDV.h, dv_t32)
                i = 0
                for part, c0 in [(0, 16384), (1, 18432)]:
                    for fq in range(4):
                        if i >= 2:
                            t.wait_ge(sDV.h, dv_t32 + i - 1)
                        bank = pbkB[fq % 2]
                        for sub in range(4):
                            fc = fq * 4 + sub
                            mm = t.transpose(bank[:, sub * 128:(sub + 1) * 128],
                                             bav(c0, 2048)[:, fc * 128:(fc + 1) * 128],
                                             identb[:])
                        mm.then_inc(sPE.h, 1); sPE.v += 1
                        i += 1

            @blk.vector
            def _(v):
                cnt = sPE.v - 8
                for part, arof in [(0, 20480), (1, 28672)]:
                    for fq in range(4):
                        cnt += 1
                        v.wait_ge(sPE.h, cnt)
                        bank = pbkB[fq % 2]
                        dst = bav(arof, 8192)
                        v.tensor_copy(
                            dst.rearrange("p (fc t) -> p fc t", t=512)[:, fq * 4:fq * 4 + 4,
                                                                       g * 128:(g + 1) * 128],
                            bank[:, 0:512].rearrange("p (fc t) -> p fc t", t=128)).then_inc(sDV.h, 1)
                        sDV.v += 1

    def emit_wo(half):
        AFMH = bav(20480, 8192)
        AFML = bav(28672, 8192)
        pe_b, dv_b, ac_b = sPE.v, sDV.v, sAC.v

        sched = [('S', 0), ('S', 1)]
        for fc in range(16):
            sched.append(('R', fc))
            if fc + 2 < 16:
                sched.append(('S', fc + 2))
        split_dv, res_dv = {}, {}
        c = dv_b
        for k, i in sched:
            c += 1
            {'S': split_dv, 'R': res_dv}[k][i] = c
        sDV.v = c
        pe_mm = lambda fc: pe_b + (1 if fc == 0 else 2 * fc)
        pe_st = lambda fc: pe_b + (2 * fc + 3 if fc < 15 else 32)
        ac_bias = lambda fc: ac_b + 2 * fc + 1
        ac_sq = lambda fc: ac_b + 2 * fc + 2
        sAC.v = ac_b + 32

        def wsbuf(bi):
            if bi == 0:
                return bav(16384, 2048), bav(18432, 2048)
            return bav(0, 2048), bav(2048, 2048)

        wld, xld, spl = [], [], [None] * 17
        with nc.Block() as blk:
            @blk.sync
            def _(q):
                def emit_load(fc):
                    bi = fc % 2
                    w32 = f32v(AR1, bi * 2048, 2048)
                    waits = [(sDV.h, split_dv[fc - 2])] if fc >= 2 else []
                    wld.append(PL.load(
                        q, w32.rearrange("p (ci f) -> p ci f", f=128),
                        woT[:, fc * 128:(fc + 1) * 128].rearrange("(ci p) f -> p ci f", p=128),
                        waits=waits))
                    xw = [(sDV.h, res_dv[fc - 2])] if fc >= 2 else []
                    xld.append(PL.load(
                        q, f32v(AR4, bi * 512, 512),
                        xT[fc * 128:(fc + 1) * 128, half * 512:(half + 1) * 512],
                        waits=xw))

                emit_load(0)
                emit_load(1)
                for fc in range(16):
                    if fc + 2 < 16:
                        emit_load(fc + 2)
                    spl[fc + 1] = PL.load(
                        q, x2scr[fc * 128:(fc + 1) * 128, half * 512:(half + 1) * 512],
                        f32v(AR4, 2048, 512), waits=[(sPE.h, pe_st(fc))])

            @blk.vector
            def _(v):
                for kind, fc in sched:
                    bi = fc % 2
                    if kind == 'S':
                        w32 = f32v(AR1, bi * 2048, 2048)
                        wsH, wsL = wsbuf(bi)
                        v.wait_ge(*wld[fc])
                        if fc >= 2:
                            v.wait_ge(sPE.h, pe_mm(fc - 2))
                        v.tensor_copy(wsH, w32)
                        v.drain()
                        v.scalar_tensor_tensor(wsL, w32, 1.0, wsH, op0=ALU.bypass,
                                               op1=ALU.subtract).then_inc(sDV.h, 1)
                        v.drain()
                    else:
                        v.wait_ge(sAC.h, ac_bias(fc))
                        if spl[fc] is not None:
                            v.wait_ge(*spl[fc])
                        v.scalar_tensor_tensor(f32v(AR4, 2048, 512), f32v(AR4, 1024, 512),
                                               mvec[:, 32 + fc:33 + fc], f32v(AR4, bi * 512, 512),
                                               op0=ALU.mult, op1=ALU.add).then_inc(sDV.h, 1)
                        v.drain()

            @blk.scalar
            def _(s):
                for fc in range(16):
                    bi = fc % 2
                    s.wait_ge(sPE.h, pe_mm(fc))
                    s.wait_ge(*xld[fc])
                    if fc >= 1:
                        s.wait_ge(sDV.h, res_dv[fc - 1])
                    s.activation(f32v(AR4, 1024, 512), pbk[bi][:, :], AF.Identity,
                                 bias=bofm[:, fc:fc + 1], scale=1.0).then_inc(sAC.h, 1)
                    s.wait_ge(sDV.h, res_dv[fc])
                    s.activation(f32v(AR4, 3072, 512), f32v(AR4, 2048, 512),
                                 AF.Square).then_inc(sAC.h, 1)

            @blk.tensor
            def _(t):
                def emit_mm(fc):
                    bi = fc % 2
                    wsH, wsL = wsbuf(bi)
                    t.wait_ge(sDV.h, split_dv[fc])
                    if fc >= 2:
                        t.wait_ge(sAC.h, ac_bias(fc - 2))
                    bank = pbk[bi]
                    first = True
                    for wof, mv in [(wsH, AFMH), (wsH, AFML), (wsL, AFMH)]:
                        for ci in range(16):
                            mm = t.matmul(bank[:, :], wof[:, ci * 128:(ci + 1) * 128],
                                          mv[:, ci * 512:(ci + 1) * 512],
                                          start=first, stop=(wof is wsL and ci == 15))
                            first = False
                    mm.then_inc(sPE.h, 1)

                def emit_st(fc):
                    t.wait_ge(sAC.h, ac_sq(fc))
                    t.matmul(pbk[4][0:1, :], onespc[:], f32v(AR4, 2048, 512),
                             start=(fc == 0), stop=(fc == 15))
                    t.matmul(pbk[5][0:1, :], onespc[:], f32v(AR4, 3072, 512),
                             start=(fc == 0), stop=(fc == 15)).then_inc(sPE.h, 1)

                emit_mm(0)
                for fc in range(1, 16):
                    emit_mm(fc)
                    emit_st(fc - 1)
                emit_st(15)
        sPE.v = pe_b + 32
        fence()
        with nc.Block() as blk:
            @blk.vector
            def _(v):
                v.tensor_scalar(srow[:, half * 512:(half + 1) * 512], pbk[4][0:1, :],
                                1.0 / DIM, None, op0=ALU.mult)
                v.tensor_scalar(qrow[:, half * 512:(half + 1) * 512], pbk[5][0:1, :],
                                1.0 / DIM, None, op0=ALU.mult).then_inc(sDV.h, 1); sDV.v += 1

    for half in range(2):
        emit_h1(half)
        fence()
        emit_proj(half, wqT, 'q')
        emit_p2_sprinkle(24)
        emit_proj(half, wkT, 'k')
        emit_p2_sprinkle(24)
        emit_proj(half, wvT, 'v')
        emit_p2_sprinkle(24)
        fence()
        with nc.Block() as blk:
            @blk.vector
            def _(v):
                v.memset(f32v(AR1, 6144, 1024), 0.0)
                v.memset(f32v(AR1, 8192, 1024), 0.0).then_inc(sDV.h, 1); sDV.v += 1
        for g in range(4):
            emit_group(half, g)
        fence()
        emit_wo(half)
        emit_p2_sprinkle(24)
        fence()

    emit_ln_rows(a2row, b2row, 4096, False)
    fence()

    with nc.Block() as blk:
        @blk.vector
        def _(v):
            v.tensor_reduce(redp[:, 192:195],
                            redp[:, 0:192].rearrange("p (g k) -> p g k", k=64),
                            axis=AX.X, op=ALU.add).then_inc(sDV.h, 1); sDV.v += 1

        @blk.tensor
        def _(t):
            t.wait_ge(sDV.h, sDV.v)
            for j in range(3):
                mm = t.matmul(pbk[4][0:1, j:j + 1], onespc[:], redp[:, 192 + j:193 + j],
                              start=True, stop=True)
            mm.then_inc(sPE.h, 1); sPE.v += 1

        @blk.vector
        def _(v):
            v.wait_ge(sPE.h, sPE.v)
            v.tensor_scalar(lam1[:, 5:8], pbk[4][0:1, 0:3], 1.0 / (DIM * DI), EPS,
                            op0=ALU.mult, op1=ALU.max)
            v.drain()
            v.tensor_scalar(lam1[:, 8:11], lam1[:, 5:8], 0.5, None, op0=ALU.mult)
            v.drain()
            v.tensor_scalar(lam1[:, 11:14], lam1[:, 8:11], -1.0, None,
                            op0=ALU.mult).then_inc(sDV.h, 1); sDV.v += 1

        @blk.tensor
        def _(t):
            t.wait_ge(sDV.h, sDV.v)
            t.matmul(pbk[3][:, 0:9], onescr[:], lam1[0:1, 5:14], start=True,
                     stop=True).then_inc(sPE.h, 1); sPE.v += 1

        @blk.vector
        def _(v):
            v.wait_ge(sPE.h, sPE.v)
            v.tensor_copy(ppt[:, 1:10], pbk[3][:, 0:9]).then_inc(sDV.h, 1); sDV.v += 1
    fence()

    def utile(k):
        return f32v(AR1, k * QT, QT)

    def zqv(k):
        return bav(20480 + k * QT, QT)

    h2q_all = bav(0, 16384)
    u_all = f32v(AR1, 0, 16384)

    def emit_pmax_bcast(srcdst, width, extra_waits=()):
        nb = width // 128
        with nc.Block() as blk:
            @blk.tensor
            def _(t):
                t.wait_ge(sDV.h, sDV.v)
                for tb in range(nb):
                    mm = t.transpose(pbk[3][:, tb * 128:(tb + 1) * 128],
                                     srcdst[:, tb * 128:(tb + 1) * 128], ident[:])
                mm.then_inc(sPE.h, 1); sPE.v += 1

            @blk.vector
            def _(v):
                v.wait_ge(sPE.h, sPE.v)
                for h, vv in extra_waits:
                    v.wait_ge(h, vv)
                v.tensor_copy(qt1[:, 0:width], pbk[3][:, 0:width])
                v.drain()
                v.tensor_reduce(qt2[:, 0:nb],
                                qt1[:, 0:width].rearrange("p (tb c) -> p tb c", c=128),
                                axis=AX.X, op=ALU.max).then_inc(sDV.h, 1); sDV.v += 1

            @blk.tensor
            def _(t):
                t.wait_ge(sDV.h, sDV.v)
                for tb in range(nb):
                    mm = t.transpose(pbk[4][0:1, tb * 128:(tb + 1) * 128],
                                     qt2[:, tb:tb + 1], ident[:])
                mm.then_inc(sPE.h, 1); sPE.v += 1

            @blk.vector
            def _(v):
                v.wait_ge(sPE.h, sPE.v)
                v.tensor_copy(qt1[0:1, 0:width], pbk[4][0:1, 0:width]).then_inc(sDV.h, 1)
                sDV.v += 1

            @blk.tensor
            def _(t):
                t.wait_ge(sDV.h, sDV.v)
                t.matmul(pbk[3][:, 0:width], onescr[:], qt1[0:1, 0:width], start=True,
                         stop=True).then_inc(sPE.h, 1); sPE.v += 1

            @blk.vector
            def _(v):
                v.wait_ge(sPE.h, sPE.v)
                v.tensor_copy(srcdst, pbk[3][:, 0:width]).then_inc(sDV.h, 1); sDV.v += 1

    def emit_quant_slabs(src_slab, seff_row, seg, nslab, dst_of, extra_waits=()):
        SW = 2048
        slab_dv = []
        with nc.Block() as blk:
            @blk.vector
            def _(v):
                v.wait_ge(sDV.h, sDV.v)
                for h, vv in extra_waits:
                    v.wait_ge(h, vv)
                for s in range(nslab):
                    U = src_slab(s)
                    B = f32v(AR4, 8192, SW)
                    J = qt3[:, 0:SW]
                    nrep = SW // seg
                    v.tensor_tensor(U.rearrange("p (r t) -> p r t", t=seg),
                                    U.rearrange("p (r t) -> p r t", t=seg),
                                    seff_row.unsqueeze(1).broadcast_to((128, nrep, seg)),
                                    op=ALU.mult)
                    v.drain()
                    v.tensor_scalar(U, U, 127.0, 384.5, op0=ALU.min, op1=ALU.add)
                    v.drain()
                    v.tensor_copy(J, U)
                    v.drain()
                    v.tensor_scalar(B, J, -385.0, None, op0=ALU.add)
                    v.drain()
                    v.tensor_tensor(U, U, B, op=ALU.subtract)
                    v.drain()
                    v.scalar_tensor_tensor(B, U, 386.0, B, op0=ALU.is_ge, op1=ALU.add)
                    v.drain()
                    v.scalar_tensor_tensor(bav(dst_of + s * SW, SW), U, 385.0, B,
                                           op0=ALU.is_ge, op1=ALU.add).then_inc(sDV.h, 1)
                    sDV.v += 1
                    slab_dv.append(sDV.v)
                    v.drain()
        return slab_dv

    def emit_h2q():
        dv0 = sDV.v
        h_toks = []
        with nc.Block() as blk:
            @blk.sync
            def _(q):
                for ci in range(16):
                    waits = [(sDV.h, dv0 + ci - 1)] if ci >= 2 else []
                    h_toks.append(PL.load(
                        q, f32v(AR4, (ci % 2) * 1024, 1024),
                        x2scr[ci * 128:(ci + 1) * 128, :], waits=waits))

            @blk.vector
            def _(v):
                for ci in range(16):
                    v.wait_ge(*h_toks[ci])
                    xc = f32v(AR4, (ci % 2) * 1024, 1024)
                    t1 = f32v(AR4, 8192, 1024)
                    v.tensor_tensor(t1, xc, f32v(AR4, 4096, 1024), op=ALU.mult)
                    v.drain()
                    v.tensor_tensor(t1, t1, f32v(AR4, 5120, 1024), op=ALU.add)
                    v.drain()
                    v.tensor_scalar(t1, t1, mvec[:, 64 + ci:65 + ci], mvec[:, 48 + ci:49 + ci],
                                    op0=ALU.mult, op1=ALU.add)
                    v.drain()
                    v.tensor_copy(f32v(AR1, ci * 1024, 1024), t1).then_inc(sDV.h, 1); sDV.v += 1
                    v.drain()

        fence()
        with nc.Block() as blk:
            @blk.vector
            def _(v):
                v.tensor_reduce(f32v(AR4, 2048, 1024),
                                u_all.rearrange("p (ci t) -> p t ci", t=1024),
                                axis=AX.X, op=ALU.max,
                                apply_absolute_value=True).then_inc(sDV.h, 1); sDV.v += 1
        emit_pmax_bcast(f32v(AR4, 2048, 512), 512)
        emit_pmax_bcast(f32v(AR4, 2560, 512), 512)
        with nc.Block() as blk:
            @blk.vector
            def _(v):
                v.wait_ge(sDV.h, sDV.v)
                v.tensor_scalar(f32v(AR4, 2048, 1024), f32v(AR4, 2048, 1024), EPS, None,
                                op0=ALU.max)
                v.drain()
                v.reciprocal(f32v(AR4, 3072, 1024), f32v(AR4, 2048, 1024))
                v.drain()
                v.tensor_scalar(f32v(AR4, 3072, 1024), f32v(AR4, 3072, 1024), QB, None,
                                op0=ALU.mult)
                v.drain()
                v.tensor_scalar(f32v(AR4, 0, 1024), f32v(AR4, 2048, 1024), ppt[:, 1:2],
                                1.0 / QB, op0=ALU.mult, op1=ALU.mult)
                v.tensor_scalar(f32v(AR4, 1024, 1024), f32v(AR4, 2048, 1024), ppt[:, 2:3],
                                1.0 / QB, op0=ALU.mult, op1=ALU.mult).then_inc(sDV.h, 1)
                sDV.v += 1
        emit_quant_slabs(lambda s: f32v(AR1, s * 2048, 2048), f32v(AR4, 3072, 1024),
                         1024, 8, 0)

    def emit_tern(w32, dst_bf, kindcol, waits):
        with nc.Block() as blk:
            @blk.vector
            def _(v):
                for h, vv in waits:
                    v.wait_ge(h, vv)
                v.tensor_scalar(dst_bf, w32, ppt[:, 7 + kindcol:8 + kindcol], None, op0=ALU.is_gt)
                v.drain()
                v.scalar_tensor_tensor(dst_bf, w32, ppt[:, 4 + kindcol:5 + kindcol], dst_bf,
                                       op0=ALU.is_gt, op1=ALU.add)
                v.drain()
                v.tensor_scalar(dst_bf, dst_bf, -1.0, None, op0=ALU.add).then_inc(sDV.h, 1)
                sDV.v += 1
                v.drain()

    def emit_ffn_quarter(q4, prev):
        pe_b, dv_b, ac_b = sPE.v, sDV.v, sAC.v

        def tern_of(j):
            return 16384 if j % 2 == 0 else 18432

        def tbuf4(j):
            r = j % 4
            if r < 2:
                return bav(16384 + r * 2048, 2048)
            return TB2[:, (r - 2) * 2048:(r - 2) * 2048 + 2048]

        sched = []
        if q4 == 0:
            sched += [('T', 0), ('T', 1), ('T', 2), ('T', 3)]
        for p in range(32):
            if q4 == 0:
                for jj in range(4 * p + 4, min(4 * p + 8, 128)):
                    sched.append(('T', jj))
            sched += [('E1', p), ('E3', p)]
        tern_dv, ev1_dv, ev3_dv = {}, {}, {}
        c = dv_b
        for k, i in sched:
            c += 1
            {'T': tern_dv, 'E1': ev1_dv, 'E3': ev3_dv}[k][i] = c
        sDV.v = c
        sAC.v = ac_b + 32
        ld_toks = []
        spl_toks = {}
        with nc.Block() as blk:
            @blk.sync
            def _(q):
                for j in range(128):
                    oc, which = j // 2, j % 2
                    wsrcT = wgT if which == 0 else wuT
                    scr = tG if which == 0 else tU
                    if q4 == 0:
                        waits = [(sDV.h, tern_dv[j - 2])] if j >= 2 else []
                        ld_toks.append(PL.load(
                            q, f32v(AR4, 8192 + which * 2048, 2048).rearrange(
                                "p (ci f) -> p ci f", f=128),
                            wsrcT[:, oc * 128:(oc + 1) * 128].rearrange(
                                "(ci p) f -> p ci f", p=128),
                            waits=waits))
                        if j >= 1:
                            jj = j - 1
                            sc2 = tG if jj % 2 == 0 else tU
                            spl_toks[jj] = PL.load(
                                q, sc2[:, (jj // 2) * 2048:(jj // 2 + 1) * 2048],
                                bav(tern_of(jj), 2048), waits=[(sDV.h, tern_dv[jj])])
                    else:
                        if j >= 4:
                            waits = [(sPE.h, pe_b + j - 3)]
                        elif prev is not None:
                            nb_p = prev['nbuf']
                            waits = []
                            if j < nb_p:
                                waits.append((sPE.h, prev['down_pe_end'] - nb_p + j + 1))
                            if j < 2 and prev['down_spl'][j] is not None:
                                waits.append(prev['down_spl'][j])
                        else:
                            waits = []
                        ld_toks.append(PL.load(
                            q, tbuf4(j),
                            scr[:, oc * 2048:(oc + 1) * 2048], waits=waits))
                if q4 == 0:
                    jj = 127
                    spl_toks[jj] = PL.load(
                        q, tU[:, (jj // 2) * 2048:(jj // 2 + 1) * 2048],
                        bav(tern_of(jj), 2048), waits=[(sDV.h, tern_dv[jj])])

            @blk.vector
            def _(v):
                for kind, i in sched:
                    if kind == 'T':
                        j = i
                        wof = f32v(AR4, 8192 + (j % 2) * 2048, 2048)
                        dst = bav(tern_of(j), 2048)
                        kindcol = 0 if j % 2 == 0 else 1
                        v.wait_ge(*ld_toks[j])
                        if j >= 2:
                            v.wait_ge(sPE.h, pe_b + j - 1)
                            v.wait_ge(*spl_toks[j - 2])
                        v.tensor_scalar(dst, wof, ppt[:, 7 + kindcol:8 + kindcol], None,
                                        op0=ALU.is_gt)
                        v.drain()
                        v.scalar_tensor_tensor(dst, wof, ppt[:, 4 + kindcol:5 + kindcol],
                                               dst, op0=ALU.is_gt, op1=ALU.add)
                        v.drain()
                        v.tensor_scalar(dst, dst, -1.0, None, op0=ALU.add).then_inc(sDV.h, 1)
                        v.drain()
                    elif kind == 'E1':
                        p = i
                        gb_ = pbk[2 * (p % 2)]
                        v.wait_ge(sPE.h, pe_b + 4 * p + 4)
                        v.tensor_tensor(qt1[:, 0:512].rearrange("p (r t) -> p r t", t=QT),
                                        gb_[:, 0:512].rearrange("p (r t) -> p r t", t=QT),
                                        f32v(AR4, q4 * QT, QT).unsqueeze(1).broadcast_to(
                                            (128, 2, QT)),
                                        op=ALU.mult).then_inc(sDV.h, 1)
                        v.drain()
                    else:
                        p = i
                        ub_ = pbk[2 * (p % 2) + 1]
                        v.wait_ge(sAC.h, ac_b + p + 1)
                        v.tensor_tensor(f32v(AR1, 2 * p * QT, 512), qt1[:, 0:512],
                                        ub_[:, 0:512], op=ALU.mult).then_inc(sDV.h, 1)
                        v.drain()

            @blk.scalar
            def _(s):
                for p in range(32):
                    s.wait_ge(sDV.h, ev1_dv[p])
                    s.activation(qt1[:, 0:512], qt1[:, 0:512], AF.Silu).then_inc(sAC.h, 1)

            @blk.tensor
            def _(t):
                for j in range(128):
                    oc, which = j // 2, j % 2
                    p, slot = oc // 2, oc % 2
                    if q4 == 0:
                        t.wait_ge(sDV.h, tern_dv[j])
                    else:
                        t.wait_ge(*ld_toks[j])
                    if p >= 2 and slot == 0 and which == 0:
                        t.wait_ge(sDV.h, ev3_dv[p - 2])
                    if p == 0 and slot == 0 and which == 0 and prev is not None:
                        t.wait_ge(sDV.h, prev['down_ev15'])
                    bank = pbk[2 * (p % 2) + which]
                    tsrc = bav(tern_of(j), 2048) if q4 == 0 else tbuf4(j)
                    for ci in range(16):
                        mm = t.matmul(bank[:, slot * QT:slot * QT + QT],
                                      tsrc[:, ci * 128:(ci + 1) * 128],
                                      h2q_all[:, ci * 1024 + q4 * QT:ci * 1024 + q4 * QT + QT],
                                      start=(ci == 0), stop=(ci == 15))
                    mm.then_inc(sPE.h, 1)
        sPE.v = pe_b + 128

        pe0_ = sPE.v
        dv0_ = sDV.v
        with nc.Block() as blk:
            @blk.tensor
            def _(t):
                for k in range(64):
                    pair = k // 2
                    t.wait_ge(sDV.h, ev3_dv[k // 2])
                    if pair >= 2 and k % 2 == 0:
                        t.wait_ge(sDV.h, dv0_ + pair - 1)
                    bank = pbk[4 + pair % 2]
                    t.matmul(bank[:, (k % 2) * QT:(k % 2) * QT + QT], hadt[:], utile(k),
                             start=True, stop=True).then_inc(sPE.h, 1); sPE.v += 1

            @blk.vector
            def _(v):
                for pair in range(32):
                    v.wait_ge(sPE.h, pe0_ + 2 * pair + 2)
                    bank = pbk[4 + pair % 2]
                    v.tensor_copy(f32v(AR1, pair * 512, 512), bank[:, 0:512]).then_inc(sDV.h, 1)
                    sDV.v += 1

        with nc.Block() as blk:
            @blk.vector
            def _(v):
                for s in range(6):
                    d = 1 << s
                    rest = d * QT
                    av = u_all.rearrange("p (blk r2) -> p blk r2", r2=2 * rest)[:, :, 0:rest]
                    bv = u_all.rearrange("p (blk r2) -> p blk r2", r2=2 * rest)[:, :, rest:2 * rest]
                    v.tensor_tensor(av, av, bv, op=ALU.add)
                    v.drain()
                    last = v.scalar_tensor_tensor(bv, bv, -2.0, av, op0=ALU.mult, op1=ALU.add)
                    v.drain()
                last.then_inc(sDV.h, 1); sDV.v += 1

        with nc.Block() as blk:
            @blk.vector
            def _(v):
                v.tensor_reduce(f32v(AR4, 2048, QT),
                                u_all.rearrange("p (k t) -> p t k", t=QT),
                                axis=AX.X, op=ALU.max,
                                apply_absolute_value=True).then_inc(sDV.h, 1); sDV.v += 1

        emit_pmax_bcast(f32v(AR4, 2048, QT), QT,
                        extra_waits=([prev['y15']] if prev is not None else []))
        with nc.Block() as blk:
            @blk.vector
            def _(v):
                v.wait_ge(sDV.h, sDV.v)
                v.tensor_tensor(f32v(AR4, 2048, QT), f32v(AR4, 2048, QT),
                                f32v(AR4, 1024 + q4 * QT, QT), op=ALU.mult)
                v.drain()
                v.tensor_scalar(f32v(AR4, 2048, QT), f32v(AR4, 2048, QT), C_FW, EPS,
                                op0=ALU.mult, op1=ALU.max)
                v.drain()
                v.reciprocal(f32v(AR4, 2560, QT), f32v(AR4, 2048, QT))
                v.drain()
                v.tensor_tensor(f32v(AR4, 2560, QT), f32v(AR4, 2560, QT),
                                f32v(AR4, 1024 + q4 * QT, QT), op=ALU.mult)
                v.drain()
                v.tensor_scalar(f32v(AR4, 2560, QT), f32v(AR4, 2560, QT), QB * C_FW, None,
                                op0=ALU.mult)
                v.tensor_scalar(f32v(AR4, 2816, QT), f32v(AR4, 2048, QT), ppt[:, 3:4],
                                1.0 / QB, op0=ALU.mult, op1=ALU.mult).then_inc(sDV.h, 1)
                sDV.v += 1
        zq_waits = [(sPE.h, prev['down_pe_end'])] if prev is not None else []
        slab_dv = emit_quant_slabs(lambda s: f32v(AR1, s * 2048, 2048),
                                   f32v(AR4, 2560, QT), QT, 8, 20480,
                                   extra_waits=zq_waits)

        gup_tern_dv, gup_spl, gup_pe_end = tern_dv, spl_toks, pe_b + 128
        pe_b, dv_b = sPE.v, sDV.v
        sched = []
        for fo in range(16):
            if q4 == 0:
                sched += [('T', 4 * fo + s) for s in range(4)]
            sched += [('EV', fo)]
        tern_dv, ev_dv = {}, {}
        c = dv_b
        for k, i in sched:
            c += 1
            {'T': tern_dv, 'EV': ev_dv}[k][i] = c
        sDV.v = c
        ld_toks = []
        spl_toks = {}
        x2_toks = [None] * 16
        y_toks = [None] * 17
        with nc.Block() as blk:
            @blk.sync
            def _(q):
                for fo in range(16):
                    if fo >= 2:
                        xw = [(sDV.h, ev_dv[fo - 2])]
                    elif prev is not None:
                        xw = [(sDV.h, prev['down_ev14'] if fo == 0 else prev['down_ev15'])]
                    else:
                        xw = []
                    x2_toks[fo] = PL.load(
                        q, f32v(AR4, 3328 + (fo % 2) * 256, QT),
                        x2scr[fo * 128:(fo + 1) * 128, q4 * QT:(q4 + 1) * QT], waits=xw)
                    for sub in range(4):
                        j = 4 * fo + sub
                        if q4 == 0:
                            waits = ([(sDV.h, tern_dv[j - 2])] if j >= 2
                                     else [(sDV.h, gup_tern_dv[126 + j]),
                                           (sDV.h, slab_dv[7])])
                            ld_toks.append(PL.load(
                                q, f32v(AR4, 8192 + (j % 2) * 2048, 2048).rearrange(
                                    "p (ci f) -> p ci f", f=128),
                                wdT[sub * 2048:(sub + 1) * 2048,
                                    fo * 128:(fo + 1) * 128].rearrange(
                                        "(ci p) f -> p ci f", p=128),
                                waits=waits))
                            if j >= 1:
                                jj = j - 1
                                spl_toks[jj] = PL.load(
                                    q, tD[:, jj * 2048:(jj + 1) * 2048],
                                    bav(tern_of(jj), 2048), waits=[(sDV.h, tern_dv[jj])])
                        else:
                            waits = ([(sPE.h, pe_b + j - 3)] if j >= 4
                                     else [(sPE.h, gup_pe_end - 4 + j + 1)])
                            ld_toks.append(PL.load(
                                q, tbuf4(j),
                                tD[:, j * 2048:(j + 1) * 2048], waits=waits))
                    if fo >= 1:
                        y_toks[fo - 1] = PO.load(
                            q, y[(fo - 1) * 128:fo * 128, q4 * QT:(q4 + 1) * QT],
                            qt2[:, 0:QT], waits=[(sDV.h, ev_dv[fo - 1])])
                if q4 == 0:
                    jj = 63
                    spl_toks[jj] = PL.load(
                        q, tD[:, jj * 2048:(jj + 1) * 2048],
                        bav(tern_of(jj), 2048), waits=[(sDV.h, tern_dv[jj])])
                y_toks[15] = PO.load(
                    q, y[15 * 128:16 * 128, q4 * QT:(q4 + 1) * QT],
                    qt2[:, 0:QT], waits=[(sDV.h, ev_dv[15])])

            @blk.vector
            def _(v):
                for kind, i in sched:
                    if kind == 'T':
                        j = i
                        wof = f32v(AR4, 8192 + (j % 2) * 2048, 2048)
                        dst = bav(tern_of(j), 2048)
                        v.wait_ge(*ld_toks[j])
                        if j >= 2:
                            v.wait_ge(sPE.h, pe_b + j - 1)
                            v.wait_ge(*spl_toks[j - 2])
                        else:
                            v.wait_ge(sPE.h, gup_pe_end - 1 + j)
                            v.wait_ge(*gup_spl[126 + j])
                        v.tensor_scalar(dst, wof, ppt[:, 9:10], None, op0=ALU.is_gt)
                        v.drain()
                        v.scalar_tensor_tensor(dst, wof, ppt[:, 6:7], dst,
                                               op0=ALU.is_gt, op1=ALU.add)
                        v.drain()
                        v.tensor_scalar(dst, dst, -1.0, None, op0=ALU.add).then_inc(sDV.h, 1)
                        v.drain()
                    else:
                        fo = i
                        v.wait_ge(sPE.h, pe_b + 4 * fo + 4)
                        v.wait_ge(*x2_toks[fo])
                        if fo >= 1 and y_toks[fo - 1] is not None:
                            v.wait_ge(*y_toks[fo - 1])
                        elif fo == 0 and prev is not None:
                            v.wait_ge(*prev['y15'])
                        v.tensor_tensor(qt1[:, 0:QT], pbk[fo % 2][:, 0:QT],
                                        f32v(AR4, 2816, QT), op=ALU.mult)
                        v.drain()
                        v.scalar_tensor_tensor(qt2[:, 0:QT], qt1[:, 0:QT],
                                               mvec[:, 80 + fo:81 + fo],
                                               f32v(AR4, 3328 + (fo % 2) * 256, QT),
                                               op0=ALU.mult, op1=ALU.add).then_inc(sDV.h, 1)
                        v.drain()

            @blk.tensor
            def _(t):
                for j in range(64):
                    fo, sub = j // 4, j % 4
                    if q4 == 0:
                        t.wait_ge(sDV.h, tern_dv[j])
                    else:
                        t.wait_ge(*ld_toks[j])
                    t.wait_ge(sDV.h, slab_dv[2 * sub + 1])
                    if sub == 0 and fo >= 2:
                        t.wait_ge(sDV.h, ev_dv[fo - 2])
                    tsrc = bav(tern_of(j), 2048) if q4 == 0 else tbuf4(j)
                    for ci in range(16):
                        k = sub * 16 + ci
                        mm = t.matmul(pbk[fo % 2][:, 0:QT],
                                      tsrc[:, ci * 128:(ci + 1) * 128],
                                      zqv(k), start=(k == 0), stop=(k == 63))
                    mm.then_inc(sPE.h, 1)
        sPE.v = pe_b + 64
        return {'down_ev14': ev_dv[14], 'down_ev15': ev_dv[15],
                'down_pe_end': pe_b + 64, 'y15': y_toks[15],
                'down_spl': (spl_toks.get(62), spl_toks.get(63)),
                'nbuf': 2 if q4 == 0 else 4}

    emit_h2q()
    fence()
    prev = None
    for q4 in range(4):
        prev = emit_ffn_quarter(q4, prev)
    fence()

    with nc.Block() as blk:
        @blk.sync
        def _(q):
            for h, v in PO.toks():
                q.wait_ge(h, v)

    return nc


def kernel(**inputs):
    x = np.asarray(inputs["L0_attn_x"])
    Wq = np.asarray(inputs["L0_attn_Wq"])
    Wk = np.asarray(inputs["L0_attn_Wk"])
    Wv = np.asarray(inputs["L0_attn_Wv"])
    Wo = np.asarray(inputs["L0_attn_Wo"])
    bo = np.asarray(inputs["L0_attn_bo"])
    lq1 = np.asarray(inputs["L0_attn_lq1"])
    lq2 = np.asarray(inputs["L0_attn_lq2"])
    lk1 = np.asarray(inputs["L0_attn_lk1"])
    lk2 = np.asarray(inputs["L0_attn_lk2"])
    wg = np.asarray(inputs["L0_ffn_w_gate"])
    wu = np.asarray(inputs["L0_ffn_w_up"])
    wd = np.asarray(inputs["L0_ffn_w_down"])
    c = np.asarray(inputs["L0_c"])
    wmod = np.asarray(inputs["L0_wmod"])

    pqk = _perm_qk()
    pv = _perm_v()
    pwo = _perm_wo_rows()

    wqT = np.ascontiguousarray(Wq.T[:, pqk])
    wkT = np.ascontiguousarray(Wk.T[:, pqk])
    wvT = np.ascontiguousarray(Wv.T[:, pv])
    woT = np.ascontiguousarray(Wo.T[pwo, :])
    wgT = np.ascontiguousarray(wg.T)
    wuT = np.ascontiguousarray(wu.T)
    wdT = np.ascontiguousarray(wd.T)

    shared = dict(
        wqT=wqT, wkT=wkT, wvT=wvT, woT=woT, wgT=wgT, wuT=wuT, wdT=wdT,
        wmod=np.ascontiguousarray(wmod),
        cvec=np.ascontiguousarray(c.reshape(1, DIM)),
        bo=np.ascontiguousarray(bo.reshape(1, DIM)),
        lq1=np.ascontiguousarray(lq1.reshape(1, D)),
        lq2=np.ascontiguousarray(lq2.reshape(1, D)),
        lk1=np.ascontiguousarray(lk1.reshape(1, D)),
        lk2=np.ascontiguousarray(lk2.reshape(1, D)),
        eye=np.eye(128, dtype=np.float32),
        had=_hadamard(128),
        onesc=np.ones((1, 128), dtype=np.float32),
        onesp=np.ones((128, 1), dtype=np.float32),
    )

    xt = x[0].T
    in_maps = []
    for i in range(NCORE):
        m = dict(shared)
        m["xT"] = np.ascontiguousarray(xt[:, i * TOKC:(i + 1) * TOKC])
        in_maps.append(m)

    nc = build_program()
    res = run_bass_kernel_spmd(nc, in_maps, core_ids=list(range(NCORE)))
    out = np.concatenate([np.asarray(r["y"]).T for r in res.results], axis=0)
    return np.ascontiguousarray(out.reshape(1, 8192, DIM))


# version 16 (final)
# speedup vs baseline: 1.0310x; 1.0110x over previous
import numpy as np

import concourse.bass as bass
import concourse.mybir as mybir
from concourse.bass_utils import run_bass_kernel_spmd

F32 = mybir.dt.float32
BF16 = mybir.dt.bfloat16
I32 = mybir.dt.int32
ALU = mybir.AluOpType
AF = mybir.ActivationFunctionType
AX = mybir.AxisListType

NCORE = 8
DIM = 2048
TOKC = 1024
NH = 16
D = 64
TWOH = 32
DI = 8192
LAM_INIT = 0.8 - 0.6 * float(np.exp(-0.3 * 12))
SCALE = D ** (-0.5)
EPS = 1e-5
QB = 128.0
C_FW = float(np.float32(1.0 / np.sqrt(DI)))
QT = 256


def _perm_qk():
    p = np.zeros(DIM, dtype=np.int64)
    for i in range(TWOH):
        pp, h = i // 16, i % 16
        for d in range(D):
            p[i * D + d] = h * 128 + 2 * d + pp
    return p


def _perm_v():
    p = np.zeros(DIM, dtype=np.int64)
    for e in range(128):
        for h in range(NH):
            p[e * NH + h] = h * 128 + e
    return p


def _perm_wo_rows():
    p = np.zeros(DIM, dtype=np.int64)
    for a in range(NH):
        for e in range(128):
            p[a + 16 * (e & 1) + 32 * (e >> 1)] = a * 128 + e
    return p


def _hadamard(n):
    h = np.array([[1.0]], dtype=np.float32)
    while h.shape[0] < n:
        h = np.block([[h, h], [h, -h]])
    return np.ascontiguousarray(h.astype(np.float32))


class Sem:
    def __init__(self, nc, name):
        self.h = nc.semaphore(name).__enter__()
        self.v = 0


def build_program():
    nc = bass.Bass("TRN2", target_bir_lowering=False)

    def din(name, shape, dt=F32):
        return nc.dram_tensor(name, list(shape), dt, kind="ExternalInput").ap()

    xT = din("xT", [DIM, TOKC])
    wqT = din("wqT", [DIM, DIM])
    wkT = din("wkT", [DIM, DIM])
    wvT = din("wvT", [DIM, DIM])
    woT = din("woT", [DIM, DIM])
    wgT = din("wgT", [DIM, DI])
    wuT = din("wuT", [DIM, DI])
    wdT = din("wdT", [DI, DIM])
    wmod = din("wmod", [DIM, 6 * DIM])
    cvec = din("cvec", [1, DIM])
    bo = din("bo", [1, DIM])
    lq1 = din("lq1", [1, D])
    lq2 = din("lq2", [1, D])
    lk1 = din("lk1", [1, D])
    lk2 = din("lk2", [1, D])
    eye = din("eye", [128, 128])
    had = din("had", [128, 128])
    onesc = din("onesc", [1, 128])
    onesp = din("onesp", [128, 1])

    y = nc.dram_tensor("y", [DIM, TOKC], F32, kind="ExternalOutput").ap()
    modscr = nc.dram_tensor("modscr", [1, 6 * DIM], F32, kind="Internal").ap()
    x2scr = nc.dram_tensor("x2scr", [DIM, TOKC], F32, kind="Internal").ap()
    vscr = nc.dram_tensor("vscr", [128, 8192], F32, kind="Internal").ap()
    qhscr = nc.dram_tensor("qhscr", [128, 8192], BF16, kind="Internal").ap()
    qlscr = nc.dram_tensor("qlscr", [128, 8192], BF16, kind="Internal").ap()
    khscr = nc.dram_tensor("khscr", [128, 8192], BF16, kind="Internal").ap()
    klscr = nc.dram_tensor("klscr", [128, 8192], BF16, kind="Internal").ap()
    I8 = mybir.dt.int8
    tG = nc.dram_tensor("tG", [128, 64 * 2048], I8, kind="Internal").ap()
    tU = nc.dram_tensor("tU", [128, 64 * 2048], I8, kind="Internal").ap()
    tD = nc.dram_tensor("tD", [128, 64 * 2048], I8, kind="Internal").ap()

    def sball(name, shape, dt=F32):
        return nc.sbuf_tensor(name, list(shape), dt).__enter__()

    AR1 = sball("AR1", [128, 16384])
    AR4 = sball("AR4", [128, 12288])
    BA = sball("BA", [128, 36864], BF16)
    ident = sball("ident", [128, 128])
    identb = sball("identb", [128, 128], BF16)
    hadt = sball("hadt", [128, 128])
    onescr = sball("onescr", [1, 128])
    onespc = sball("onespc", [128, 1])
    cfm = sball("cfm", [128, 16])
    bofm = sball("bofm", [128, 16])
    mvec = sball("mvec", [128, 96])

    lam1 = sball("lam1", [1, 16])
    ppt = sball("ppt", [128, 16])
    redp = sball("redp", [128, 200])
    TB2 = sball("TB2", [128, 4096], BF16)
    qt1 = sball("qt1", [128, 512])
    qt2 = sball("qt2", [128, 512])
    qt3 = sball("qt3", [128, 2048], I32)

    pbk = [nc.psum_tensor(f"pbk{i}", [128, 512], F32).__enter__() for i in range(6)]
    pbkB = [nc.psum_tensor(f"pbkB{i}", [128, 1024], BF16).__enter__() for i in range(2)]

    lrow = AR4[0:1, 0:256]
    rowb = AR1[0:1, 0:6 * DIM]
    rowc = AR4[0:1, 0:6 * DIM]
    srow = AR4[0:1, 6144:7168]
    qrow = AR4[0:1, 7168:8192]
    arow = AR1[0:1, 12288:13312]
    brow = AR1[0:1, 13312:14336]
    a2row = AR1[0:1, 14336:15360]
    b2row = AR1[0:1, 15360:16384]

    sPE = Sem(nc, "sPE")
    sDV = Sem(nc, "sDV")
    sAC = Sem(nc, "sAC")

    class DPool:
        def __init__(self, name, n):
            self.sems = [Sem(nc, f"{name}{i}") for i in range(n)]
            self.i = 0

        def load(self, q, out, in_, waits=(), slow=False):
            s = self.sems[self.i % len(self.sems)]
            self.i += 1
            q.wait_ge(s.h, s.v)
            for h, v in waits:
                q.wait_ge(h, v)
            kw = dict(allow_slow_non_contiguous=True) if slow else {}
            q.dma_start(out=out, in_=in_, **kw).then_inc(s.h, 16)
            s.v += 16
            return (s.h, s.v)

        def toks(self):
            return [(s.h, s.v) for s in self.sems]

    PL = DPool("pl", 8)
    PO = DPool("po", 4)
    PG = DPool("pg", 6)

    def f32v(ar, c0, n):
        return ar[:, c0:c0 + n]

    def bav(c0, n):
        return BA[:, c0:c0 + n]

    def fence():
        vals = ([(s.h, s.v) for s in (sPE, sDV, sAC) if s.v > 0]
                + [(h, v) for (h, v) in PL.toks() + PO.toks() + PG.toks() if v > 0])
        with nc.Block() as blk:
            @blk.tensor
            def _(e):
                for h, v in vals:
                    e.wait_ge(h, v)

            @blk.vector
            def _(e):
                for h, v in vals:
                    e.wait_ge(h, v)

            @blk.scalar
            def _(e):
                for h, v in vals:
                    e.wait_ge(h, v)

            @blk.gpsimd
            def _(e):
                for h, v in vals:
                    e.wait_ge(h, v)

            @blk.sync
            def _(e):
                for h, v in vals:
                    e.wait_ge(h, v)

    with nc.Block() as blk:
        @blk.sync
        def _(q):
            for dst, src in [(ident[:], eye[:]), (hadt[:], had[:]),
                             (onescr[:], onesc[:]), (onespc[:], onesp[:]),
                             (lrow[:, 0:64], lq1[:]), (lrow[:, 64:128], lq2[:]),
                             (lrow[:, 128:192], lk1[:]), (lrow[:, 192:256], lk2[:])]:
                PL.load(q, dst, src)
            PL.load(q, cfm[:], cvec.rearrange("o (k p) -> (o p) k", p=128), slow=True)
            PL.load(q, bofm[:], bo.rearrange("o (k p) -> (o p) k", p=128), slow=True)

    fence()
    with nc.Block() as blk:
        @blk.vector
        def _(v):
            v.tensor_copy(identb[:], ident[:])
            v.tensor_tensor(lrow[:, 0:64], lrow[:, 0:64], lrow[:, 128:192], op=ALU.mult)
            v.drain()
            v.tensor_tensor(lrow[:, 64:128], lrow[:, 64:128], lrow[:, 192:256], op=ALU.mult)
            v.drain()
            v.tensor_reduce(lam1[:, 0:1], lrow[:, 0:64], axis=AX.X, op=ALU.add)
            v.drain()
            v.tensor_reduce(lam1[:, 1:2], lrow[:, 64:128], axis=AX.X,
                            op=ALU.add).then_inc(sDV.h, 1); sDV.v += 1

        @blk.scalar
        def _(s):
            s.wait_ge(sDV.h, sDV.v)
            s.activation(lam1[:, 2:4], lam1[:, 0:2], AF.Exp).then_inc(sAC.h, 1); sAC.v += 1

        @blk.vector
        def _(v):
            v.wait_ge(sAC.h, sAC.v)
            v.tensor_tensor(lam1[:, 4:5], lam1[:, 3:4], lam1[:, 2:3], op=ALU.subtract)
            v.drain()
            v.tensor_scalar(lam1[:, 4:5], lam1[:, 4:5], -LAM_INIT, None,
                            op0=ALU.add).then_inc(sDV.h, 1); sDV.v += 1

        @blk.tensor
        def _(t):
            t.wait_ge(sDV.h, sDV.v)
            t.matmul(pbk[3][:, 0:1], onescr[:], lam1[0:1, 4:5], start=True,
                     stop=True).then_inc(sPE.h, 1); sPE.v += 1

        @blk.vector
        def _(v):
            v.wait_ge(sPE.h, sPE.v)
            v.tensor_copy(ppt[:, 0:1], pbk[3][:, 0:1]).then_inc(sDV.h, 1); sDV.v += 1

    with nc.Block() as blk:
        @blk.scalar
        def _(s):
            s.activation(cfm[:], cfm[:], AF.Silu).then_inc(sAC.h, 1); sAC.v += 1

    pe0 = sPE.v
    dv0 = sDV.v
    wm_toks = []
    with nc.Block() as blk:
        @blk.sync
        def _(q):
            for m in range(24):
                waits = [(sPE.h, pe0 + m - 1)] if m >= 2 else []
                base = (m % 2) * 8192
                wm_toks.append(PL.load(
                    q, f32v(AR1, base, 8192).rearrange("p (k c) -> p k c", c=512),
                    wmod[:, m * 512:(m + 1) * 512].rearrange("(k p) c -> p k c", p=128),
                    waits=waits))

        @blk.tensor
        def _(t):
            t.wait_ge(sAC.h, sAC.v)
            for m in range(24):
                t.wait_ge(*wm_toks[m])
                if m >= 2:
                    t.wait_ge(sDV.h, dv0 + m - 1)
                base = (m % 2) * 8192
                bank = pbk[m % 2]
                for k in range(16):
                    mm = t.matmul(bank[0:1, :], cfm[:, k:k + 1], f32v(AR1, base + k * 512, 512),
                                  start=(k == 0), stop=(k == 15))
                mm.then_inc(sPE.h, 1); sPE.v += 1

        @blk.vector
        def _(v):
            for m in range(24):
                v.wait_ge(sPE.h, pe0 + m + 1)
                v.tensor_copy(rowc[:, m * 512:(m + 1) * 512],
                              pbk[m % 2][0:1, :]).then_inc(sDV.h, 1); sDV.v += 1

    fence()
    with nc.Block() as blk:
        @blk.vector
        def _(v):
            v.tensor_copy(rowb[:, 0:DIM], rowc[:, 0:DIM])
            v.tensor_scalar(rowb[:, DIM:2 * DIM], rowc[:, DIM:2 * DIM], 1.0, None, op0=ALU.add)
            v.tensor_copy(rowb[:, 3 * DIM:4 * DIM], rowc[:, 3 * DIM:4 * DIM])
            v.tensor_scalar(rowb[:, 4 * DIM:5 * DIM], rowc[:, 4 * DIM:5 * DIM], 1.0, None, op0=ALU.add)
            last = None
            for o in (2, 5):
                g0 = rowc[:, o * DIM:(o + 1) * DIM]
                gc = rowb[:, o * DIM:(o + 1) * DIM]
                gneg = rowc[:, 0:DIM]
                v.tensor_scalar(gc, g0, 0.0, 2.0, op0=ALU.is_ge, op1=ALU.mult)
                v.drain()
                v.tensor_scalar(gc, gc, -1.0, None, op0=ALU.add)
                v.drain()
                v.tensor_scalar(gneg, g0, -1.0, None, op0=ALU.mult)
                v.drain()
                v.tensor_tensor(g0, g0, gneg, op=ALU.max)
                v.drain()
                v.tensor_scalar(g0, g0, 0.25, None, op0=ALU.max)
                v.drain()
                last = v.tensor_tensor(gc, gc, g0, op=ALU.mult)
                v.drain()
            last.then_inc(sDV.h, 1); sDV.v += 1

        @blk.sync
        def _(q):
            spill = PL.load(q, modscr[:], rowb, waits=[(sDV.h, sDV.v)])
            PL.load(q, mvec[:],
                    modscr.rearrange("o (s k p) -> (o p) (s k)", p=128, k=16),
                    waits=[spill], slow=True)

    fence()
    pe0 = sPE.v
    ac0 = sAC.v
    x_toks = []
    with nc.Block() as blk:
        @blk.sync
        def _(q):
            for ci in range(16):
                waits = [(sPE.h, pe0 + ci - 1)] if ci >= 2 else []
                x_toks.append(PL.load(q, f32v(AR4, (ci % 2) * 1024, 1024),
                                      xT[ci * 128:(ci + 1) * 128, :], waits=waits))

        @blk.scalar
        def _(s):
            for ci in range(16):
                s.wait_ge(*x_toks[ci])
                s.activation(f32v(AR4, 2048 + (ci % 2) * 1024, 1024),
                             f32v(AR4, (ci % 2) * 1024, 1024),
                             AF.Square).then_inc(sAC.h, 1); sAC.v += 1

        @blk.tensor
        def _(t):
            for ci in range(16):
                t.wait_ge(sAC.h, ac0 + ci + 1)
                xc = (ci % 2) * 1024
                for hh in range(2):
                    t.matmul(pbk[2 + hh][0:1, :], onespc[:], f32v(AR4, xc + hh * 512, 512),
                             start=(ci == 0), stop=(ci == 15))
                    mm = t.matmul(pbk[4 + hh][0:1, :], onespc[:],
                                  f32v(AR4, 2048 + xc + hh * 512, 512),
                                  start=(ci == 0), stop=(ci == 15))
                mm.then_inc(sPE.h, 1); sPE.v += 1

    fence()

    def emit_ln_rows(adst_a, adst_b, dst_bc, from_psum):
        with nc.Block() as blk:
            @blk.vector
            def _(v):
                if from_psum:
                    v.tensor_scalar(srow[:, 0:512], pbk[2][0:1, :], 1.0 / DIM, None, op0=ALU.mult)
                    v.tensor_scalar(srow[:, 512:1024], pbk[3][0:1, :], 1.0 / DIM, None, op0=ALU.mult)
                    v.tensor_scalar(qrow[:, 0:512], pbk[4][0:1, :], 1.0 / DIM, None, op0=ALU.mult)
                    v.tensor_scalar(qrow[:, 512:1024], pbk[5][0:1, :], 1.0 / DIM, None, op0=ALU.mult)
                    v.drain()
                v.tensor_tensor(rowb[0:1, 0:1024], srow[:, :], srow[:, :], op=ALU.mult)
                v.drain()
                v.tensor_tensor(qrow[:, :], qrow[:, :], rowb[0:1, 0:1024], op=ALU.subtract)
                v.drain()
                v.tensor_scalar(qrow[:, :], qrow[:, :], EPS, None, op0=ALU.add)
                v.drain()
                v.reciprocal(qrow[:, :], qrow[:, :]).then_inc(sDV.h, 1); sDV.v += 1

            @blk.scalar
            def _(s):
                s.wait_ge(sDV.h, sDV.v)
                s.activation(adst_a[:, :], qrow[:, :], AF.Sqrt).then_inc(sAC.h, 1); sAC.v += 1

            @blk.vector
            def _(v):
                v.wait_ge(sAC.h, sAC.v)
                v.tensor_tensor(adst_b[:, :], srow[:, :], adst_a[:, :], op=ALU.mult)
                v.drain()
                v.tensor_scalar(adst_b[:, :], adst_b[:, :], -1.0, None,
                                op0=ALU.mult).then_inc(sDV.h, 1); sDV.v += 1

            @blk.tensor
            def _(t):
                t.wait_ge(sDV.h, sDV.v)
                for hh in range(2):
                    t.matmul(pbk[hh][:, :], onescr[:], adst_a[:, hh * 512:(hh + 1) * 512],
                             start=True, stop=True)
                    mm = t.matmul(pbk[2 + hh][:, :], onescr[:], adst_b[:, hh * 512:(hh + 1) * 512],
                                  start=True, stop=True)
                mm.then_inc(sPE.h, 1); sPE.v += 1

            @blk.vector
            def _(v):
                v.wait_ge(sPE.h, sPE.v)
                for hh in range(2):
                    v.tensor_copy(f32v(AR4, dst_bc + hh * 512, 512), pbk[hh][:, :])
                    cp = v.tensor_copy(f32v(AR4, dst_bc + 1024 + hh * 512, 512), pbk[2 + hh][:, :])
                cp.then_inc(sDV.h, 1); sDV.v += 1

    emit_ln_rows(arow, brow, 4096, True)
    fence()

    def _p2_chunk_src(i):
        if i < 64:
            rb, cb = i // 4, i % 4
            return wgT[rb * 128:(rb + 1) * 128, cb * 2048:(cb + 1) * 2048]
        if i < 128:
            j = i - 64
            rb, cb = j // 4, j % 4
            return wuT[rb * 128:(rb + 1) * 128, cb * 2048:(cb + 1) * 2048]
        j = i - 128
        return wdT[j * 128:(j + 1) * 128, :]

    p2_state = {"i": 0, "dv": [None, None]}

    def emit_p2_sprinkle(count):
        toks = []
        base = p2_state["i"]
        dv_b2 = sDV.v
        with nc.Block() as blk:
            @blk.sync
            def _(q):
                for n in range(count):
                    i = base + n
                    bi = i % 2
                    if n >= 2:
                        waits = [(sDV.h, dv_b2 + n - 1)]
                    elif p2_state["dv"][bi] is not None:
                        waits = [(sDV.h, p2_state["dv"][bi])]
                    else:
                        waits = []
                    toks.append(PL.load(q, f32v(AR4, 8192 + bi * 2048, 2048),
                                        _p2_chunk_src(i), waits=waits))

            @blk.vector
            def _(v):
                for n in range(count):
                    i = base + n
                    bi = i % 2
                    v.wait_ge(*toks[n])
                    v.tensor_reduce(redp[:, i:i + 1], f32v(AR4, 8192 + bi * 2048, 2048),
                                    axis=AX.X, op=ALU.add,
                                    apply_absolute_value=True).then_inc(sDV.h, 1)
                    sDV.v += 1
                    p2_state["dv"][bi] = sDV.v
        p2_state["i"] += count


    def emit_h1(half):
        dv0 = sDV.v
        h_toks = []
        with nc.Block() as blk:
            @blk.sync
            def _(q):
                for ci in range(16):
                    waits = [(sDV.h, dv0 + ci - 1)] if ci >= 2 else []
                    h_toks.append(PL.load(q, f32v(AR4, (ci % 2) * 512, 512),
                                          xT[ci * 128:(ci + 1) * 128, half * 512:(half + 1) * 512],
                                          waits=waits))

            @blk.vector
            def _(v):
                for ci in range(16):
                    v.wait_ge(*h_toks[ci])
                    xc = f32v(AR4, (ci % 2) * 512, 512)
                    t1 = f32v(AR4, 1024, 512)
                    v.tensor_tensor(t1, xc, f32v(AR4, 4096 + half * 512, 512), op=ALU.mult)
                    v.drain()
                    v.tensor_tensor(t1, t1, f32v(AR4, 5120 + half * 512, 512), op=ALU.add)
                    v.drain()
                    v.tensor_scalar(t1, t1, mvec[:, 16 + ci:17 + ci], mvec[:, ci:ci + 1],
                                    op0=ALU.mult, op1=ALU.add)
                    v.drain()
                    v.tensor_copy(bav(0, 8192)[:, ci * 512:(ci + 1) * 512], t1)
                    v.drain()
                    v.scalar_tensor_tensor(bav(8192, 8192)[:, ci * 512:(ci + 1) * 512],
                                           t1, 1.0, bav(0, 8192)[:, ci * 512:(ci + 1) * 512],
                                           op0=ALU.bypass,
                                           op1=ALU.subtract).then_inc(sDV.h, 1); sDV.v += 1
                    v.drain()

    tms_spills = [None, None]

    def emit_proj(half, wsrc, dst_kind):
        h1H = bav(0, 8192)
        h1L = bav(8192, 8192)
        scr_hi = {'q': qhscr, 'k': khscr}.get(dst_kind)
        scr_lo = {'q': qlscr, 'k': klscr}.get(dst_kind)
        spills = tms_spills
        pe_b, dv_b, pe_prev = sPE.v, sDV.v, sPE.v

        sched = [('S', 0), ('S', 1)]
        for fc in range(16):
            sched.append(('E1', fc)); sched.append(('E2', fc))
            if fc + 2 < 16:
                sched.append(('S', fc + 2))
        split_dv, ev1_dv, ev2_dv = {}, {}, {}
        c = dv_b
        for k, i in sched:
            c += 1
            {'S': split_dv, 'E1': ev1_dv, 'E2': ev2_dv}[k][i] = c
        sDV.v = c
        pe_mm = lambda fc: pe_b + (1 if fc == 0 else 2 * fc)
        pe_tr = lambda fc: pe_b + (2 * fc + 3 if fc < 15 else 32)
        ld_toks = []
        spill_slots = [list(spills)] + [None] * 16

        with nc.Block() as blk:
            @blk.sync
            def _(q):
                def emit_load(fc):
                    bi = fc % 2
                    w32 = f32v(AR1, bi * 2048, 2048)
                    waits = [(sDV.h, split_dv[fc - 2])] if fc >= 2 else []
                    ld_toks.append(PL.load(
                        q, w32.rearrange("p (ci f) -> p ci f", f=128),
                        wsrc[:, fc * 128:(fc + 1) * 128].rearrange("(ci p) f -> p ci f", p=128),
                        waits=waits))

                def emit_spill(fc):
                    if dst_kind == 'v':
                        t0 = PL.load(
                            q,
                            vscr.rearrange("p (g c) -> p g c", c=2048)[:, :, fc * 128:(fc + 1) * 128],
                            f32v(AR1, 4608, 512).rearrange("p (g c) -> p g c", c=128),
                            waits=[(sDV.h, ev2_dv[fc])])
                        spill_slots[fc + 1] = [t0, None]
                    else:
                        t0 = PL.load(
                            q,
                            scr_hi.rearrange("p (g c) -> p g c", c=2048)[:, :, fc * 128:(fc + 1) * 128],
                            bav(25600, 512).rearrange("p (g c) -> p g c", c=128),
                            waits=[(sDV.h, ev2_dv[fc])])
                        t1 = PL.load(
                            q,
                            scr_lo.rearrange("p (g c) -> p g c", c=2048)[:, :, fc * 128:(fc + 1) * 128],
                            bav(26112, 512).rearrange("p (g c) -> p g c", c=128))
                        spill_slots[fc + 1] = [t0, t1]

                emit_load(0)
                emit_load(1)
                for fc in range(16):
                    if fc + 2 < 16:
                        emit_load(fc + 2)
                    emit_spill(fc)

            @blk.vector
            def _(v):
                for kind, fc in sched:
                    bi = fc % 2
                    if kind == 'S':
                        w32 = f32v(AR1, bi * 2048, 2048)
                        wsH = bav(16384 + bi * 4096, 2048)
                        wsL = bav(18432 + bi * 4096, 2048)
                        v.wait_ge(*ld_toks[fc])
                        if fc >= 2:
                            v.wait_ge(sPE.h, pe_mm(fc - 2))
                        v.tensor_copy(wsH, w32)
                        v.drain()
                        v.scalar_tensor_tensor(wsL, w32, 1.0, wsH, op0=ALU.bypass,
                                               op1=ALU.subtract).then_inc(sDV.h, 1)
                        v.drain()
                    elif kind == 'E1':
                        v.wait_ge(sPE.h, pe_mm(fc))
                        bank = pbk[bi]
                        if dst_kind == 'v':
                            for sp in spill_slots[fc]:
                                if sp is not None:
                                    v.wait_ge(*sp)
                            v.tensor_copy(f32v(AR1, 4096, 512), bank[:, :]).then_inc(sDV.h, 1)
                            v.drain()
                        else:
                            v.tensor_copy(bav(24576, 512), bank[:, :])
                            v.drain()
                            v.scalar_tensor_tensor(bav(25088, 512), bank[:, :], 1.0,
                                                   bav(24576, 512), op0=ALU.bypass,
                                                   op1=ALU.subtract).then_inc(sDV.h, 1)
                            v.drain()
                    else:
                        v.wait_ge(sPE.h, pe_tr(fc))
                        if dst_kind == 'v':
                            for sp in spill_slots[fc]:
                                if sp is not None:
                                    v.wait_ge(*sp)
                            for tb in range(4):
                                cp = v.tensor_copy(
                                    f32v(AR1, 4608, 512)[:, tb * 128:(tb + 1) * 128],
                                    pbk[2 + tb % 2][:, (tb // 2) * 128:(tb // 2) * 128 + 128])
                            cp.then_inc(sDV.h, 1)
                            v.drain()
                        else:
                            for sp in spill_slots[fc]:
                                if sp is not None:
                                    v.wait_ge(*sp)
                            v.tensor_copy(bav(25600, 512), pbkB[0][:, 0:512])
                            v.tensor_copy(bav(26112, 512), pbkB[1][:, 0:512]).then_inc(sDV.h, 1)
                            v.drain()

            @blk.tensor
            def _(t):
                def emit_mm(fc):
                    bi = fc % 2
                    wsH = bav(16384 + bi * 4096, 2048)
                    wsL = bav(18432 + bi * 4096, 2048)
                    t.wait_ge(sDV.h, split_dv[fc])
                    if fc >= 2:
                        t.wait_ge(sDV.h, ev1_dv[fc - 2])
                    bank = pbk[bi]
                    first = True
                    for wof, mv in [(wsH, h1H), (wsH, h1L), (wsL, h1H)]:
                        for ci in range(16):
                            mm = t.matmul(bank[:, :], wof[:, ci * 128:(ci + 1) * 128],
                                          mv[:, ci * 512:(ci + 1) * 512],
                                          start=first, stop=(wof is wsL and ci == 15))
                            first = False
                    mm.then_inc(sPE.h, 1)

                def emit_tr(fc):
                    t.wait_ge(sDV.h, ev1_dv[fc])
                    if dst_kind == 'v':
                        for tb in range(4):
                            mm = t.transpose(pbk[2 + tb % 2][:, (tb // 2) * 128:(tb // 2) * 128 + 128],
                                             f32v(AR1, 4096 + tb * 128, 128), ident[:])
                    else:
                        for part, src in [(0, bav(24576, 512)), (1, bav(25088, 512))]:
                            for tb in range(4):
                                mm = t.transpose(pbkB[part][:, tb * 128:(tb + 1) * 128],
                                                 src[:, tb * 128:(tb + 1) * 128], identb[:])
                    mm.then_inc(sPE.h, 1)

                emit_mm(0)
                for fc in range(1, 16):
                    emit_mm(fc)
                    emit_tr(fc - 1)
                emit_tr(15)
        sPE.v = pe_b + 32
        spills[0] = spill_slots[16][0]
        spills[1] = spill_slots[16][1]

    def emit_group(half, g):
        fence()
        qmh, qml = bav(0, 2048), bav(2048, 2048)
        kmh, kml = bav(4096, 2048), bav(6144, 2048)
        qgh, qgl = bav(8192, 2048), bav(10240, 2048)
        kgh, kgl = bav(12288, 2048), bav(14336, 2048)
        vG = f32v(AR4, 0, 2048)
        vm, vm2 = f32v(AR1, 0, 2048), f32v(AR1, 2048, 2048)
        attsb, atts2 = f32v(AR1, 4096, 1024), f32v(AR1, 5120, 1024)
        ddup, dT, dT2 = f32v(AR1, 6144, 1024), f32v(AR1, 7168, 1024), f32v(AR1, 8192, 1024)
        AM = f32v(AR1, 9216, 2048)
        AMsq = f32v(AR1, 4096, 2048)
        rmx, rsm = f32v(AR1, 11264, 32), f32v(AR1, 11296, 32)
        s2a, s2b = f32v(AR1, 11328, 32), f32v(AR1, 11360, 32)

        toks = []
        with nc.Block() as blk:
            @blk.sync
            def _(q):
                toks.append(PL.load(q, vG, vscr[:, g * 2048:(g + 1) * 2048]))
                toks.append(PL.load(q, qgh[:, :], qhscr[:, g * 2048:(g + 1) * 2048]))
                toks.append(PL.load(q, qgl[:, :], qlscr[:, g * 2048:(g + 1) * 2048]))
                toks.append(PL.load(q, kgh[:, :], khscr[:, g * 2048:(g + 1) * 2048]))
                toks.append(PL.load(q, kgl[:, :], klscr[:, g * 2048:(g + 1) * 2048]))

            @blk.vector
            def _(v):
                for tk in toks:
                    v.wait_ge(*tk)
                v.transpose(qmh, qgh)
                v.transpose(qml, qgl)
                v.transpose(kmh, kgh)
                v.transpose(kml, kgl)
                v.transpose(vm, vG)
                v.drain()
                v.stream_shuffle(vm2, vm, [(p + 16) % 32 for p in range(32)]).then_inc(sDV.h, 1)
                sDV.v += 1

        dv_min = sDV.v
        with nc.Block() as blk:
            @blk.tensor
            def _(t):
                t.wait_ge(sDV.h, dv_min)
                for tt in range(128):
                    pq = tt // 32
                    t32 = tt % 32
                    bank = pbk[4] if t32 < 16 else pbk[5]
                    col = (t32 % 16) * 32
                    idx = 0
                    for qm, km in [(qmh, kmh), (qmh, kml), (qml, kmh)]:
                        for dh in range(2):
                            st = qm[32 * pq:32 * pq + 32, :].rearrange(
                                "p (i c) -> p i c", c=64)[:, :, 32 * dh + t32]
                            mv = km[32 * pq:32 * pq + 32, :].rearrange(
                                "p (i c) -> p i c", c=64)[:, :, 32 * dh + t32]
                            mm = t.matmul(bank[32 * pq:32 * pq + 32, col:col + 32], st, mv,
                                          start=(idx == 0), stop=(idx == 5),
                                          tile_position=(32 * pq, 32 * pq))
                            idx += 1
                mm.then_inc(sPE.h, 1); sPE.v += 1

        pe_gram = sPE.v
        with nc.Block() as blk:
            @blk.vector
            def _(v):
                v.wait_ge(sPE.h, pe_gram)
                v.tensor_scalar(attsb[:, 0:512], pbk[4][:, :], SCALE, None, op0=ALU.mult)
                v.tensor_scalar(attsb[:, 512:1024], pbk[5][:, :], SCALE, None, op0=ALU.mult)
                v.drain()
                v.tensor_reduce(rmx, attsb.rearrange("p (s j) -> p s j", j=32),
                                axis=AX.X, op=ALU.max)
                v.drain()
                v.tensor_tensor(atts2.rearrange("p (s j) -> p s j", j=32),
                                attsb.rearrange("p (s j) -> p s j", j=32),
                                rmx.unsqueeze(2).broadcast_to((128, 32, 32)),
                                op=ALU.subtract).then_inc(sDV.h, 1); sDV.v += 1

            @blk.scalar
            def _(s):
                s.wait_ge(sDV.h, sDV.v)
                s.activation(attsb[:, :], atts2[:, :], AF.Exp).then_inc(sAC.h, 1); sAC.v += 1

            @blk.vector
            def _(v):
                v.wait_ge(sAC.h, sAC.v)
                v.tensor_reduce(rsm, attsb.rearrange("p (s j) -> p s j", j=32),
                                axis=AX.X, op=ALU.add)
                v.drain()
                v.reciprocal(rsm, rsm)
                v.drain()
                v.tensor_tensor(atts2.rearrange("p (s j) -> p s j", j=32),
                                attsb.rearrange("p (s j) -> p s j", j=32),
                                rsm.unsqueeze(2).broadcast_to((128, 32, 32)), op=ALU.mult)
                v.drain()
                v.stream_shuffle(attsb, atts2, [(p + 16) % 32 for p in range(32)])
                v.drain()
                for pq in range(4):
                    for dcp in range(2):
                        v.scalar_tensor_tensor(
                            ddup[32 * pq:32 * pq + 16, :].rearrange(
                                "p (s j) -> p s j", j=32)[:, :, dcp * 16:(dcp + 1) * 16],
                            attsb[32 * pq:32 * pq + 16, :].rearrange(
                                "p (s j) -> p s j", j=32)[:, :, 16:32],
                            ppt[32 * pq:32 * pq + 16, 0:1],
                            atts2[32 * pq:32 * pq + 16, :].rearrange(
                                "p (s j) -> p s j", j=32)[:, :, 0:16],
                            op0=ALU.mult, op1=ALU.add)
                        v.drain()
                v.transpose(dT, ddup)
                v.drain()
                v.tensor_copy(dT2.rearrange("p (s j) -> p s j", j=32)[:, :, 16:32],
                              dT.rearrange("p (s j) -> p s j", j=32)[:, :, 0:16]).then_inc(sDV.h, 1)
                sDV.v += 1

        dv_diff = sDV.v
        pe_dv0 = sPE.v
        dv_ev0 = sDV.v
        with nc.Block() as blk:
            @blk.tensor
            def _(t):
                t.wait_ge(sDV.h, dv_diff)
                for batch in range(4):
                    if batch >= 2:
                        t.wait_ge(sDV.h, dv_ev0 + batch - 1)
                    for w in range(8):
                        s = batch * 8 + w
                        for pq in range(4):
                            be = pbk[2] if batch % 2 == 0 else pbk[3]
                            st_e = dT[32 * pq:32 * pq + 16, s * 32:s * 32 + 32]
                            st_o = dT2[32 * pq:32 * pq + 16, s * 32:s * 32 + 32]
                            mv_e = vm[32 * pq:32 * pq + 16, :].rearrange(
                                "p (u c) -> p u c", c=32)[:, :, s]
                            mv_o = vm2[32 * pq:32 * pq + 16, :].rearrange(
                                "p (u c) -> p u c", c=32)[:, :, s]
                            t.matmul(be[32 * pq:32 * pq + 32, w * 64:w * 64 + 64], st_e, mv_e,
                                     start=True, stop=False, tile_position=(32 * pq, 32 * pq))
                            mm = t.matmul(be[32 * pq:32 * pq + 32, w * 64:w * 64 + 64], st_o, mv_o,
                                          start=False, stop=True, tile_position=(32 * pq, 32 * pq))
                    mm.then_inc(sPE.h, 1); sPE.v += 1

            @blk.vector
            def _(v):
                for batch in range(4):
                    v.wait_ge(sPE.h, pe_dv0 + batch + 1)
                    be = pbk[2] if batch % 2 == 0 else pbk[3]
                    v.tensor_copy(
                        AM.rearrange("p (u c) -> p u c", c=32)[:, :, batch * 8:batch * 8 + 8],
                        be[:, :].rearrange("p (w u) -> p u w", u=64)).then_inc(sDV.h, 1)
                    sDV.v += 1

        fence()
        with nc.Block() as blk:
            @blk.scalar
            def _(s):
                s.activation(AMsq, AM, AF.Square).then_inc(sAC.h, 1); sAC.v += 1

            @blk.vector
            def _(v):
                v.wait_ge(sAC.h, sAC.v)
                v.tensor_reduce(s2a, AMsq.rearrange("p (u c) -> p c u", c=32),
                                axis=AX.X, op=ALU.add)
                v.drain()
                v.stream_shuffle(s2b, s2a, [(p + 16) % 32 for p in range(32)])
                v.drain()
                v.tensor_tensor(s2a, s2a, s2b, op=ALU.add)
                v.drain()
                v.tensor_scalar(s2a, s2a, 1.0 / 128.0, EPS, op0=ALU.mult, op1=ALU.add)
                v.drain()
                v.reciprocal(s2a, s2a).then_inc(sDV.h, 1); sDV.v += 1

            @blk.scalar
            def _(s):
                s.wait_ge(sDV.h, sDV.v)
                s.activation(s2a, s2a, AF.Sqrt,
                             scale=(1.0 - LAM_INIT) ** 2).then_inc(sAC.h, 1); sAC.v += 1

            @blk.vector
            def _(v):
                v.wait_ge(sAC.h, sAC.v)
                v.tensor_tensor(AM.rearrange("p (u c) -> p u c", c=32),
                                AM.rearrange("p (u c) -> p u c", c=32),
                                s2a.unsqueeze(1).broadcast_to((128, 64, 32)), op=ALU.mult)
                v.drain()
                v.tensor_copy(bav(16384, 2048), AM)
                v.drain()
                v.scalar_tensor_tensor(bav(18432, 2048), AM, 1.0, bav(16384, 2048),
                                       op0=ALU.bypass, op1=ALU.subtract)
                v.drain()
                v.transpose(bav(16384, 2048), bav(16384, 2048))
                v.drain()
                v.transpose(bav(18432, 2048), bav(18432, 2048)).then_inc(sDV.h, 1); sDV.v += 1

            dv_t32 = sDV.v
            @blk.tensor
            def _(t):
                t.wait_ge(sDV.h, dv_t32)
                i = 0
                for part, c0 in [(0, 16384), (1, 18432)]:
                    for fq in range(4):
                        if i >= 2:
                            t.wait_ge(sDV.h, dv_t32 + i - 1)
                        bank = pbkB[fq % 2]
                        for sub in range(4):
                            fc = fq * 4 + sub
                            mm = t.transpose(bank[:, sub * 128:(sub + 1) * 128],
                                             bav(c0, 2048)[:, fc * 128:(fc + 1) * 128],
                                             identb[:])
                        mm.then_inc(sPE.h, 1); sPE.v += 1
                        i += 1

            @blk.vector
            def _(v):
                cnt = sPE.v - 8
                for part, arof in [(0, 20480), (1, 28672)]:
                    for fq in range(4):
                        cnt += 1
                        v.wait_ge(sPE.h, cnt)
                        bank = pbkB[fq % 2]
                        dst = bav(arof, 8192)
                        v.tensor_copy(
                            dst.rearrange("p (fc t) -> p fc t", t=512)[:, fq * 4:fq * 4 + 4,
                                                                       g * 128:(g + 1) * 128],
                            bank[:, 0:512].rearrange("p (fc t) -> p fc t", t=128)).then_inc(sDV.h, 1)
                        sDV.v += 1

    def emit_wo(half):
        AFMH = bav(20480, 8192)
        AFML = bav(28672, 8192)
        pe_b, dv_b, ac_b = sPE.v, sDV.v, sAC.v

        sched = [('S', 0), ('S', 1)]
        for fc in range(16):
            sched.append(('R', fc))
            if fc + 2 < 16:
                sched.append(('S', fc + 2))
        split_dv, res_dv = {}, {}
        c = dv_b
        for k, i in sched:
            c += 1
            {'S': split_dv, 'R': res_dv}[k][i] = c
        sDV.v = c
        pe_mm = lambda fc: pe_b + (1 if fc == 0 else 2 * fc)
        pe_st = lambda fc: pe_b + (2 * fc + 3 if fc < 15 else 32)
        ac_bias = lambda fc: ac_b + 2 * fc + 1
        ac_sq = lambda fc: ac_b + 2 * fc + 2
        sAC.v = ac_b + 32

        def wsbuf(bi):
            if bi == 0:
                return bav(16384, 2048), bav(18432, 2048)
            return bav(0, 2048), bav(2048, 2048)

        wld, xld, spl = [], [], [None] * 17
        with nc.Block() as blk:
            @blk.sync
            def _(q):
                def emit_load(fc):
                    bi = fc % 2
                    w32 = f32v(AR1, bi * 2048, 2048)
                    waits = [(sDV.h, split_dv[fc - 2])] if fc >= 2 else []
                    wld.append(PL.load(
                        q, w32.rearrange("p (ci f) -> p ci f", f=128),
                        woT[:, fc * 128:(fc + 1) * 128].rearrange("(ci p) f -> p ci f", p=128),
                        waits=waits))
                    xw = [(sDV.h, res_dv[fc - 2])] if fc >= 2 else []
                    xld.append(PL.load(
                        q, f32v(AR4, bi * 512, 512),
                        xT[fc * 128:(fc + 1) * 128, half * 512:(half + 1) * 512],
                        waits=xw))

                emit_load(0)
                emit_load(1)
                for fc in range(16):
                    if fc + 2 < 16:
                        emit_load(fc + 2)
                    spl[fc + 1] = PL.load(
                        q, x2scr[fc * 128:(fc + 1) * 128, half * 512:(half + 1) * 512],
                        f32v(AR4, 2048, 512), waits=[(sPE.h, pe_st(fc))])

            @blk.vector
            def _(v):
                for kind, fc in sched:
                    bi = fc % 2
                    if kind == 'S':
                        w32 = f32v(AR1, bi * 2048, 2048)
                        wsH, wsL = wsbuf(bi)
                        v.wait_ge(*wld[fc])
                        if fc >= 2:
                            v.wait_ge(sPE.h, pe_mm(fc - 2))
                        v.tensor_copy(wsH, w32)
                        v.drain()
                        v.scalar_tensor_tensor(wsL, w32, 1.0, wsH, op0=ALU.bypass,
                                               op1=ALU.subtract).then_inc(sDV.h, 1)
                        v.drain()
                    else:
                        v.wait_ge(sAC.h, ac_bias(fc))
                        if spl[fc] is not None:
                            v.wait_ge(*spl[fc])
                        v.scalar_tensor_tensor(f32v(AR4, 2048, 512), f32v(AR4, 1024, 512),
                                               mvec[:, 32 + fc:33 + fc], f32v(AR4, bi * 512, 512),
                                               op0=ALU.mult, op1=ALU.add).then_inc(sDV.h, 1)
                        v.drain()

            @blk.scalar
            def _(s):
                for fc in range(16):
                    bi = fc % 2
                    s.wait_ge(sPE.h, pe_mm(fc))
                    s.wait_ge(*xld[fc])
                    if fc >= 1:
                        s.wait_ge(sDV.h, res_dv[fc - 1])
                    s.activation(f32v(AR4, 1024, 512), pbk[bi][:, :], AF.Identity,
                                 bias=bofm[:, fc:fc + 1], scale=1.0).then_inc(sAC.h, 1)
                    s.wait_ge(sDV.h, res_dv[fc])
                    s.activation(f32v(AR4, 3072, 512), f32v(AR4, 2048, 512),
                                 AF.Square).then_inc(sAC.h, 1)

            @blk.tensor
            def _(t):
                def emit_mm(fc):
                    bi = fc % 2
                    wsH, wsL = wsbuf(bi)
                    t.wait_ge(sDV.h, split_dv[fc])
                    if fc >= 2:
                        t.wait_ge(sAC.h, ac_bias(fc - 2))
                    bank = pbk[bi]
                    first = True
                    for wof, mv in [(wsH, AFMH), (wsH, AFML), (wsL, AFMH)]:
                        for ci in range(16):
                            mm = t.matmul(bank[:, :], wof[:, ci * 128:(ci + 1) * 128],
                                          mv[:, ci * 512:(ci + 1) * 512],
                                          start=first, stop=(wof is wsL and ci == 15))
                            first = False
                    mm.then_inc(sPE.h, 1)

                def emit_st(fc):
                    t.wait_ge(sAC.h, ac_sq(fc))
                    t.matmul(pbk[4][0:1, :], onespc[:], f32v(AR4, 2048, 512),
                             start=(fc == 0), stop=(fc == 15))
                    t.matmul(pbk[5][0:1, :], onespc[:], f32v(AR4, 3072, 512),
                             start=(fc == 0), stop=(fc == 15)).then_inc(sPE.h, 1)

                emit_mm(0)
                for fc in range(1, 16):
                    emit_mm(fc)
                    emit_st(fc - 1)
                emit_st(15)
        sPE.v = pe_b + 32
        fence()
        with nc.Block() as blk:
            @blk.vector
            def _(v):
                v.tensor_scalar(srow[:, half * 512:(half + 1) * 512], pbk[4][0:1, :],
                                1.0 / DIM, None, op0=ALU.mult)
                v.tensor_scalar(qrow[:, half * 512:(half + 1) * 512], pbk[5][0:1, :],
                                1.0 / DIM, None, op0=ALU.mult).then_inc(sDV.h, 1); sDV.v += 1

    for half in range(2):
        emit_h1(half)
        fence()
        emit_proj(half, wqT, 'q')
        emit_p2_sprinkle(24)
        emit_proj(half, wkT, 'k')
        emit_p2_sprinkle(24)
        emit_proj(half, wvT, 'v')
        emit_p2_sprinkle(24)
        fence()
        with nc.Block() as blk:
            @blk.vector
            def _(v):
                v.memset(f32v(AR1, 6144, 1024), 0.0)
                v.memset(f32v(AR1, 8192, 1024), 0.0).then_inc(sDV.h, 1); sDV.v += 1
        for g in range(4):
            emit_group(half, g)
        fence()
        emit_wo(half)
        emit_p2_sprinkle(24)
        fence()

    emit_ln_rows(a2row, b2row, 4096, False)
    fence()

    with nc.Block() as blk:
        @blk.vector
        def _(v):
            v.tensor_reduce(redp[:, 192:195],
                            redp[:, 0:192].rearrange("p (g k) -> p g k", k=64),
                            axis=AX.X, op=ALU.add).then_inc(sDV.h, 1); sDV.v += 1

        @blk.tensor
        def _(t):
            t.wait_ge(sDV.h, sDV.v)
            for j in range(3):
                mm = t.matmul(pbk[4][0:1, j:j + 1], onespc[:], redp[:, 192 + j:193 + j],
                              start=True, stop=True)
            mm.then_inc(sPE.h, 1); sPE.v += 1

        @blk.vector
        def _(v):
            v.wait_ge(sPE.h, sPE.v)
            v.tensor_scalar(lam1[:, 5:8], pbk[4][0:1, 0:3], 1.0 / (DIM * DI), EPS,
                            op0=ALU.mult, op1=ALU.max)
            v.drain()
            v.tensor_scalar(lam1[:, 8:11], lam1[:, 5:8], 0.5, None, op0=ALU.mult)
            v.drain()
            v.tensor_scalar(lam1[:, 11:14], lam1[:, 8:11], -1.0, None,
                            op0=ALU.mult).then_inc(sDV.h, 1); sDV.v += 1

        @blk.tensor
        def _(t):
            t.wait_ge(sDV.h, sDV.v)
            t.matmul(pbk[3][:, 0:9], onescr[:], lam1[0:1, 5:14], start=True,
                     stop=True).then_inc(sPE.h, 1); sPE.v += 1

        @blk.vector
        def _(v):
            v.wait_ge(sPE.h, sPE.v)
            v.tensor_copy(ppt[:, 1:10], pbk[3][:, 0:9]).then_inc(sDV.h, 1); sDV.v += 1
    fence()

    def utile(k):
        return f32v(AR1, k * QT, QT)

    def zqv(k):
        return bav(20480 + k * QT, QT)

    h2q_all = bav(0, 16384)
    u_all = f32v(AR1, 0, 16384)

    def emit_pmax_bcast(srcdst, width, extra_waits=()):
        nb = width // 128
        with nc.Block() as blk:
            @blk.tensor
            def _(t):
                t.wait_ge(sDV.h, sDV.v)
                for tb in range(nb):
                    mm = t.transpose(pbk[3][:, tb * 128:(tb + 1) * 128],
                                     srcdst[:, tb * 128:(tb + 1) * 128], ident[:])
                mm.then_inc(sPE.h, 1); sPE.v += 1

            @blk.vector
            def _(v):
                v.wait_ge(sPE.h, sPE.v)
                for h, vv in extra_waits:
                    v.wait_ge(h, vv)
                v.tensor_copy(qt1[:, 0:width], pbk[3][:, 0:width])
                v.drain()
                v.tensor_reduce(qt2[:, 0:nb],
                                qt1[:, 0:width].rearrange("p (tb c) -> p tb c", c=128),
                                axis=AX.X, op=ALU.max).then_inc(sDV.h, 1); sDV.v += 1

            @blk.tensor
            def _(t):
                t.wait_ge(sDV.h, sDV.v)
                for tb in range(nb):
                    mm = t.transpose(pbk[4][0:1, tb * 128:(tb + 1) * 128],
                                     qt2[:, tb:tb + 1], ident[:])
                mm.then_inc(sPE.h, 1); sPE.v += 1

            @blk.vector
            def _(v):
                v.wait_ge(sPE.h, sPE.v)
                v.tensor_copy(qt1[0:1, 0:width], pbk[4][0:1, 0:width]).then_inc(sDV.h, 1)
                sDV.v += 1

            @blk.tensor
            def _(t):
                t.wait_ge(sDV.h, sDV.v)
                t.matmul(pbk[3][:, 0:width], onescr[:], qt1[0:1, 0:width], start=True,
                         stop=True).then_inc(sPE.h, 1); sPE.v += 1

            @blk.vector
            def _(v):
                v.wait_ge(sPE.h, sPE.v)
                v.tensor_copy(srcdst, pbk[3][:, 0:width]).then_inc(sDV.h, 1); sDV.v += 1

    def emit_quant_slabs(src_slab, seff_row, seg, nslab, dst_of, extra_waits=()):
        SW = 2048
        slab_dv = []
        with nc.Block() as blk:
            @blk.vector
            def _(v):
                v.wait_ge(sDV.h, sDV.v)
                for h, vv in extra_waits:
                    v.wait_ge(h, vv)
                for s in range(nslab):
                    U = src_slab(s)
                    B = f32v(AR4, 8192, SW)
                    J = qt3[:, 0:SW]
                    nrep = SW // seg
                    v.tensor_tensor(U.rearrange("p (r t) -> p r t", t=seg),
                                    U.rearrange("p (r t) -> p r t", t=seg),
                                    seff_row.unsqueeze(1).broadcast_to((128, nrep, seg)),
                                    op=ALU.mult)
                    v.drain()
                    v.tensor_scalar(U, U, 127.0, 384.5, op0=ALU.min, op1=ALU.add)
                    v.drain()
                    v.tensor_copy(J, U)
                    v.drain()
                    v.tensor_scalar(B, J, -385.0, None, op0=ALU.add)
                    v.drain()
                    v.tensor_tensor(U, U, B, op=ALU.subtract)
                    v.drain()
                    v.scalar_tensor_tensor(B, U, 386.0, B, op0=ALU.is_ge, op1=ALU.add)
                    v.drain()
                    v.scalar_tensor_tensor(bav(dst_of + s * SW, SW), U, 385.0, B,
                                           op0=ALU.is_ge, op1=ALU.add).then_inc(sDV.h, 1)
                    sDV.v += 1
                    slab_dv.append(sDV.v)
                    v.drain()
        return slab_dv

    def emit_h2q():
        dv0 = sDV.v
        h_toks = []
        with nc.Block() as blk:
            @blk.sync
            def _(q):
                for ci in range(16):
                    waits = [(sDV.h, dv0 + ci - 1)] if ci >= 2 else []
                    h_toks.append(PL.load(
                        q, f32v(AR4, (ci % 2) * 1024, 1024),
                        x2scr[ci * 128:(ci + 1) * 128, :], waits=waits))

            @blk.vector
            def _(v):
                for ci in range(16):
                    v.wait_ge(*h_toks[ci])
                    xc = f32v(AR4, (ci % 2) * 1024, 1024)
                    t1 = f32v(AR4, 8192, 1024)
                    v.tensor_tensor(t1, xc, f32v(AR4, 4096, 1024), op=ALU.mult)
                    v.drain()
                    v.tensor_tensor(t1, t1, f32v(AR4, 5120, 1024), op=ALU.add)
                    v.drain()
                    v.tensor_scalar(t1, t1, mvec[:, 64 + ci:65 + ci], mvec[:, 48 + ci:49 + ci],
                                    op0=ALU.mult, op1=ALU.add)
                    v.drain()
                    v.tensor_copy(f32v(AR1, ci * 1024, 1024), t1).then_inc(sDV.h, 1); sDV.v += 1
                    v.drain()

        fence()
        with nc.Block() as blk:
            @blk.vector
            def _(v):
                v.tensor_reduce(f32v(AR4, 2048, 1024),
                                u_all.rearrange("p (ci t) -> p t ci", t=1024),
                                axis=AX.X, op=ALU.max,
                                apply_absolute_value=True).then_inc(sDV.h, 1); sDV.v += 1
        emit_pmax_bcast(f32v(AR4, 2048, 512), 512)
        emit_pmax_bcast(f32v(AR4, 2560, 512), 512)
        with nc.Block() as blk:
            @blk.vector
            def _(v):
                v.wait_ge(sDV.h, sDV.v)
                v.tensor_scalar(f32v(AR4, 2048, 1024), f32v(AR4, 2048, 1024), EPS, None,
                                op0=ALU.max)
                v.drain()
                v.reciprocal(f32v(AR4, 3072, 1024), f32v(AR4, 2048, 1024))
                v.drain()
                v.tensor_scalar(f32v(AR4, 3072, 1024), f32v(AR4, 3072, 1024), QB, None,
                                op0=ALU.mult)
                v.drain()
                v.tensor_scalar(f32v(AR4, 0, 1024), f32v(AR4, 2048, 1024), ppt[:, 1:2],
                                1.0 / QB, op0=ALU.mult, op1=ALU.mult)
                v.tensor_scalar(f32v(AR4, 1024, 1024), f32v(AR4, 2048, 1024), ppt[:, 2:3],
                                1.0 / QB, op0=ALU.mult, op1=ALU.mult).then_inc(sDV.h, 1)
                sDV.v += 1
        emit_quant_slabs(lambda s: f32v(AR1, s * 2048, 2048), f32v(AR4, 3072, 1024),
                         1024, 8, 0)

    def emit_tern(w32, dst_bf, kindcol, waits):
        with nc.Block() as blk:
            @blk.vector
            def _(v):
                for h, vv in waits:
                    v.wait_ge(h, vv)
                v.tensor_scalar(dst_bf, w32, ppt[:, 7 + kindcol:8 + kindcol], None, op0=ALU.is_gt)
                v.drain()
                v.scalar_tensor_tensor(dst_bf, w32, ppt[:, 4 + kindcol:5 + kindcol], dst_bf,
                                       op0=ALU.is_gt, op1=ALU.add)
                v.drain()
                v.tensor_scalar(dst_bf, dst_bf, -1.0, None, op0=ALU.add).then_inc(sDV.h, 1)
                sDV.v += 1
                v.drain()

    def emit_ffn_quarter(q4, prev):
        pe_b, dv_b, ac_b = sPE.v, sDV.v, sAC.v

        def tern_of(j):
            return 16384 if j % 2 == 0 else 18432

        def tbuf4(j):
            r = j % 4
            if r < 2:
                return bav(16384 + r * 2048, 2048)
            return TB2[:, (r - 2) * 2048:(r - 2) * 2048 + 2048]

        sched = []
        if q4 == 0:
            sched += [('T', 0), ('T', 1), ('T', 2), ('T', 3)]
        for p in range(32):
            if q4 == 0:
                for jj in range(4 * p + 4, min(4 * p + 8, 128)):
                    sched.append(('T', jj))
            sched += [('E1', p), ('E3', p)]
        tern_dv, ev1_dv, ev3_dv = {}, {}, {}
        c = dv_b
        for k, i in sched:
            c += 1
            {'T': tern_dv, 'E1': ev1_dv, 'E3': ev3_dv}[k][i] = c
        sDV.v = c
        sAC.v = ac_b + 32
        ld_toks = []
        spl_toks = {}
        with nc.Block() as blk:
            @blk.sync
            def _(q):
                if q4 != 0:
                    return
                for j in range(128):
                    oc, which = j // 2, j % 2
                    wsrcT = wgT if which == 0 else wuT
                    waits = [(sDV.h, tern_dv[j - 2])] if j >= 2 else []
                    ld_toks.append(PL.load(
                        q, f32v(AR4, 8192 + which * 2048, 2048).rearrange(
                            "p (ci f) -> p ci f", f=128),
                        wsrcT[:, oc * 128:(oc + 1) * 128].rearrange(
                            "(ci p) f -> p ci f", p=128),
                        waits=waits))

            @blk.gpsimd
            def _(g):
                for j in range(128):
                    oc, which = j // 2, j % 2
                    scr = tG if which == 0 else tU
                    if q4 == 0:
                        spl_toks[j] = PG.load(
                            g, scr[:, oc * 2048:(oc + 1) * 2048],
                            bav(tern_of(j), 2048), waits=[(sDV.h, tern_dv[j])])
                    else:
                        if j >= 4:
                            waits = [(sPE.h, pe_b + j - 3)]
                        elif prev is not None:
                            nb_p = prev['nbuf']
                            waits = []
                            if j < nb_p:
                                waits.append((sPE.h, prev['down_pe_end'] - nb_p + j + 1))
                            if j < 2 and prev['down_spl'][j] is not None:
                                waits.append(prev['down_spl'][j])
                        else:
                            waits = []
                        ld_toks.append(PG.load(
                            g, tbuf4(j),
                            scr[:, oc * 2048:(oc + 1) * 2048], waits=waits))

            @blk.vector
            def _(v):
                for kind, i in sched:
                    if kind == 'T':
                        j = i
                        wof = f32v(AR4, 8192 + (j % 2) * 2048, 2048)
                        dst = bav(tern_of(j), 2048)
                        kindcol = 0 if j % 2 == 0 else 1
                        v.wait_ge(*ld_toks[j])
                        if j >= 2:
                            v.wait_ge(sPE.h, pe_b + j - 1)
                            v.wait_ge(*spl_toks[j - 2])
                        v.tensor_scalar(dst, wof, ppt[:, 7 + kindcol:8 + kindcol], None,
                                        op0=ALU.is_gt)
                        v.drain()
                        v.scalar_tensor_tensor(dst, wof, ppt[:, 4 + kindcol:5 + kindcol],
                                               dst, op0=ALU.is_gt, op1=ALU.add)
                        v.drain()
                        v.tensor_scalar(dst, dst, -1.0, None, op0=ALU.add).then_inc(sDV.h, 1)
                        v.drain()
                    elif kind == 'E1':
                        p = i
                        gb_ = pbk[2 * (p % 2)]
                        v.wait_ge(sPE.h, pe_b + 4 * p + 4)
                        v.tensor_tensor(qt1[:, 0:512].rearrange("p (r t) -> p r t", t=QT),
                                        gb_[:, 0:512].rearrange("p (r t) -> p r t", t=QT),
                                        f32v(AR4, q4 * QT, QT).unsqueeze(1).broadcast_to(
                                            (128, 2, QT)),
                                        op=ALU.mult).then_inc(sDV.h, 1)
                        v.drain()
                    else:
                        p = i
                        ub_ = pbk[2 * (p % 2) + 1]
                        v.wait_ge(sAC.h, ac_b + p + 1)
                        v.tensor_tensor(f32v(AR1, 2 * p * QT, 512), qt1[:, 0:512],
                                        ub_[:, 0:512], op=ALU.mult).then_inc(sDV.h, 1)
                        v.drain()

            @blk.scalar
            def _(s):
                for p in range(32):
                    s.wait_ge(sDV.h, ev1_dv[p])
                    s.activation(qt1[:, 0:512], qt1[:, 0:512], AF.Silu).then_inc(sAC.h, 1)

            @blk.tensor
            def _(t):
                for j in range(128):
                    oc, which = j // 2, j % 2
                    p, slot = oc // 2, oc % 2
                    if q4 == 0:
                        t.wait_ge(sDV.h, tern_dv[j])
                    else:
                        t.wait_ge(*ld_toks[j])
                    if p >= 2 and slot == 0 and which == 0:
                        t.wait_ge(sDV.h, ev3_dv[p - 2])
                    if p == 0 and slot == 0 and which == 0 and prev is not None:
                        t.wait_ge(sDV.h, prev['down_ev15'])
                    bank = pbk[2 * (p % 2) + which]
                    tsrc = bav(tern_of(j), 2048) if q4 == 0 else tbuf4(j)
                    for ci in range(16):
                        mm = t.matmul(bank[:, slot * QT:slot * QT + QT],
                                      tsrc[:, ci * 128:(ci + 1) * 128],
                                      h2q_all[:, ci * 1024 + q4 * QT:ci * 1024 + q4 * QT + QT],
                                      start=(ci == 0), stop=(ci == 15))
                    mm.then_inc(sPE.h, 1)
        sPE.v = pe_b + 128

        pe0_ = sPE.v
        dv0_ = sDV.v
        with nc.Block() as blk:
            @blk.tensor
            def _(t):
                for k in range(64):
                    pair = k // 2
                    t.wait_ge(sDV.h, ev3_dv[k // 2])
                    if pair >= 2 and k % 2 == 0:
                        t.wait_ge(sDV.h, dv0_ + pair - 1)
                    bank = pbk[4 + pair % 2]
                    t.matmul(bank[:, (k % 2) * QT:(k % 2) * QT + QT], hadt[:], utile(k),
                             start=True, stop=True).then_inc(sPE.h, 1); sPE.v += 1

            @blk.vector
            def _(v):
                for pair in range(32):
                    v.wait_ge(sPE.h, pe0_ + 2 * pair + 2)
                    bank = pbk[4 + pair % 2]
                    v.tensor_copy(f32v(AR1, pair * 512, 512), bank[:, 0:512]).then_inc(sDV.h, 1)
                    sDV.v += 1

        with nc.Block() as blk:
            @blk.vector
            def _(v):
                for s in range(6):
                    d = 1 << s
                    rest = d * QT
                    av = u_all.rearrange("p (blk r2) -> p blk r2", r2=2 * rest)[:, :, 0:rest]
                    bv = u_all.rearrange("p (blk r2) -> p blk r2", r2=2 * rest)[:, :, rest:2 * rest]
                    v.tensor_tensor(av, av, bv, op=ALU.add)
                    v.drain()
                    last = v.scalar_tensor_tensor(bv, bv, -2.0, av, op0=ALU.mult, op1=ALU.add)
                    v.drain()
                last.then_inc(sDV.h, 1); sDV.v += 1

        with nc.Block() as blk:
            @blk.vector
            def _(v):
                v.tensor_reduce(f32v(AR4, 2048, QT),
                                u_all.rearrange("p (k t) -> p t k", t=QT),
                                axis=AX.X, op=ALU.max,
                                apply_absolute_value=True).then_inc(sDV.h, 1); sDV.v += 1

        emit_pmax_bcast(f32v(AR4, 2048, QT), QT,
                        extra_waits=([prev['y15']] if prev is not None else []))
        with nc.Block() as blk:
            @blk.vector
            def _(v):
                v.wait_ge(sDV.h, sDV.v)
                v.tensor_tensor(f32v(AR4, 2048, QT), f32v(AR4, 2048, QT),
                                f32v(AR4, 1024 + q4 * QT, QT), op=ALU.mult)
                v.drain()
                v.tensor_scalar(f32v(AR4, 2048, QT), f32v(AR4, 2048, QT), C_FW, EPS,
                                op0=ALU.mult, op1=ALU.max)
                v.drain()
                v.reciprocal(f32v(AR4, 2560, QT), f32v(AR4, 2048, QT))
                v.drain()
                v.tensor_tensor(f32v(AR4, 2560, QT), f32v(AR4, 2560, QT),
                                f32v(AR4, 1024 + q4 * QT, QT), op=ALU.mult)
                v.drain()
                v.tensor_scalar(f32v(AR4, 2560, QT), f32v(AR4, 2560, QT), QB * C_FW, None,
                                op0=ALU.mult)
                v.tensor_scalar(f32v(AR4, 2816, QT), f32v(AR4, 2048, QT), ppt[:, 3:4],
                                1.0 / QB, op0=ALU.mult, op1=ALU.mult).then_inc(sDV.h, 1)
                sDV.v += 1
        zq_waits = [(sPE.h, prev['down_pe_end'])] if prev is not None else []
        slab_dv = emit_quant_slabs(lambda s: f32v(AR1, s * 2048, 2048),
                                   f32v(AR4, 2560, QT), QT, 8, 20480,
                                   extra_waits=zq_waits)

        gup_tern_dv, gup_spl, gup_pe_end = tern_dv, spl_toks, pe_b + 128
        pe_b, dv_b = sPE.v, sDV.v
        sched = []
        for fo in range(16):
            if q4 == 0:
                sched += [('T', 4 * fo + s) for s in range(4)]
            sched += [('EV', fo)]
        tern_dv, ev_dv = {}, {}
        c = dv_b
        for k, i in sched:
            c += 1
            {'T': tern_dv, 'EV': ev_dv}[k][i] = c
        sDV.v = c
        ld_toks = []
        spl_toks = {}
        x2_toks = [None] * 16
        y_toks = [None] * 17
        with nc.Block() as blk:
            @blk.sync
            def _(q):
                for fo in range(16):
                    if fo >= 2:
                        xw = [(sDV.h, ev_dv[fo - 2])]
                    elif prev is not None:
                        xw = [(sDV.h, prev['down_ev14'] if fo == 0 else prev['down_ev15'])]
                    else:
                        xw = []
                    x2_toks[fo] = PL.load(
                        q, f32v(AR4, 3328 + (fo % 2) * 256, QT),
                        x2scr[fo * 128:(fo + 1) * 128, q4 * QT:(q4 + 1) * QT], waits=xw)
                    for sub in range(4):
                        j = 4 * fo + sub
                        if q4 == 0:
                            waits = ([(sDV.h, tern_dv[j - 2])] if j >= 2
                                     else [(sDV.h, gup_tern_dv[126 + j]),
                                           (sDV.h, slab_dv[7])])
                            ld_toks.append(PL.load(
                                q, f32v(AR4, 8192 + (j % 2) * 2048, 2048).rearrange(
                                    "p (ci f) -> p ci f", f=128),
                                wdT[sub * 2048:(sub + 1) * 2048,
                                    fo * 128:(fo + 1) * 128].rearrange(
                                        "(ci p) f -> p ci f", p=128),
                                waits=waits))
                    if fo >= 1:
                        y_toks[fo - 1] = PO.load(
                            q, y[(fo - 1) * 128:fo * 128, q4 * QT:(q4 + 1) * QT],
                            qt2[:, 0:QT], waits=[(sDV.h, ev_dv[fo - 1])])
                y_toks[15] = PO.load(
                    q, y[15 * 128:16 * 128, q4 * QT:(q4 + 1) * QT],
                    qt2[:, 0:QT], waits=[(sDV.h, ev_dv[15])])

            @blk.gpsimd
            def _(g):
                for j in range(64):
                    fo, sub = j // 4, j % 4
                    if q4 == 0:
                        spl_toks[j] = PG.load(
                            g, tD[:, j * 2048:(j + 1) * 2048],
                            bav(tern_of(j), 2048), waits=[(sDV.h, tern_dv[j])])
                    else:
                        waits = ([(sPE.h, pe_b + j - 3)] if j >= 4
                                 else [(sPE.h, gup_pe_end - 4 + j + 1)])
                        ld_toks.append(PG.load(
                            g, tbuf4(j),
                            tD[:, j * 2048:(j + 1) * 2048], waits=waits))

            @blk.vector
            def _(v):
                for kind, i in sched:
                    if kind == 'T':
                        j = i
                        wof = f32v(AR4, 8192 + (j % 2) * 2048, 2048)
                        dst = bav(tern_of(j), 2048)
                        v.wait_ge(*ld_toks[j])
                        if j >= 2:
                            v.wait_ge(sPE.h, pe_b + j - 1)
                            v.wait_ge(*spl_toks[j - 2])
                        else:
                            v.wait_ge(sPE.h, gup_pe_end - 1 + j)
                            v.wait_ge(*gup_spl[126 + j])
                        v.tensor_scalar(dst, wof, ppt[:, 9:10], None, op0=ALU.is_gt)
                        v.drain()
                        v.scalar_tensor_tensor(dst, wof, ppt[:, 6:7], dst,
                                               op0=ALU.is_gt, op1=ALU.add)
                        v.drain()
                        v.tensor_scalar(dst, dst, -1.0, None, op0=ALU.add).then_inc(sDV.h, 1)
                        v.drain()
                    else:
                        fo = i
                        v.wait_ge(sPE.h, pe_b + 4 * fo + 4)
                        v.wait_ge(*x2_toks[fo])
                        if fo >= 1 and y_toks[fo - 1] is not None:
                            v.wait_ge(*y_toks[fo - 1])
                        elif fo == 0 and prev is not None:
                            v.wait_ge(*prev['y15'])
                        v.tensor_tensor(qt1[:, 0:QT], pbk[fo % 2][:, 0:QT],
                                        f32v(AR4, 2816, QT), op=ALU.mult)
                        v.drain()
                        v.scalar_tensor_tensor(qt2[:, 0:QT], qt1[:, 0:QT],
                                               mvec[:, 80 + fo:81 + fo],
                                               f32v(AR4, 3328 + (fo % 2) * 256, QT),
                                               op0=ALU.mult, op1=ALU.add).then_inc(sDV.h, 1)
                        v.drain()

            @blk.tensor
            def _(t):
                for j in range(64):
                    fo, sub = j // 4, j % 4
                    if q4 == 0:
                        t.wait_ge(sDV.h, tern_dv[j])
                    else:
                        t.wait_ge(*ld_toks[j])
                    t.wait_ge(sDV.h, slab_dv[2 * sub + 1])
                    if sub == 0 and fo >= 2:
                        t.wait_ge(sDV.h, ev_dv[fo - 2])
                    tsrc = bav(tern_of(j), 2048) if q4 == 0 else tbuf4(j)
                    for ci in range(16):
                        k = sub * 16 + ci
                        mm = t.matmul(pbk[fo % 2][:, 0:QT],
                                      tsrc[:, ci * 128:(ci + 1) * 128],
                                      zqv(k), start=(k == 0), stop=(k == 63))
                    mm.then_inc(sPE.h, 1)
        sPE.v = pe_b + 64
        return {'down_ev14': ev_dv[14], 'down_ev15': ev_dv[15],
                'down_pe_end': pe_b + 64, 'y15': y_toks[15],
                'down_spl': (spl_toks.get(62), spl_toks.get(63)),
                'nbuf': 2 if q4 == 0 else 4}

    emit_h2q()
    fence()
    prev = None
    for q4 in range(4):
        prev = emit_ffn_quarter(q4, prev)
    fence()

    with nc.Block() as blk:
        @blk.sync
        def _(q):
            for h, v in PO.toks():
                q.wait_ge(h, v)

    return nc


def kernel(**inputs):
    x = np.asarray(inputs["L0_attn_x"])
    Wq = np.asarray(inputs["L0_attn_Wq"])
    Wk = np.asarray(inputs["L0_attn_Wk"])
    Wv = np.asarray(inputs["L0_attn_Wv"])
    Wo = np.asarray(inputs["L0_attn_Wo"])
    bo = np.asarray(inputs["L0_attn_bo"])
    lq1 = np.asarray(inputs["L0_attn_lq1"])
    lq2 = np.asarray(inputs["L0_attn_lq2"])
    lk1 = np.asarray(inputs["L0_attn_lk1"])
    lk2 = np.asarray(inputs["L0_attn_lk2"])
    wg = np.asarray(inputs["L0_ffn_w_gate"])
    wu = np.asarray(inputs["L0_ffn_w_up"])
    wd = np.asarray(inputs["L0_ffn_w_down"])
    c = np.asarray(inputs["L0_c"])
    wmod = np.asarray(inputs["L0_wmod"])

    pqk = _perm_qk()
    pv = _perm_v()
    pwo = _perm_wo_rows()

    wqT = np.ascontiguousarray(Wq.T[:, pqk])
    wkT = np.ascontiguousarray(Wk.T[:, pqk])
    wvT = np.ascontiguousarray(Wv.T[:, pv])
    woT = np.ascontiguousarray(Wo.T[pwo, :])
    wgT = np.ascontiguousarray(wg.T)
    wuT = np.ascontiguousarray(wu.T)
    wdT = np.ascontiguousarray(wd.T)

    shared = dict(
        wqT=wqT, wkT=wkT, wvT=wvT, woT=woT, wgT=wgT, wuT=wuT, wdT=wdT,
        wmod=np.ascontiguousarray(wmod),
        cvec=np.ascontiguousarray(c.reshape(1, DIM)),
        bo=np.ascontiguousarray(bo.reshape(1, DIM)),
        lq1=np.ascontiguousarray(lq1.reshape(1, D)),
        lq2=np.ascontiguousarray(lq2.reshape(1, D)),
        lk1=np.ascontiguousarray(lk1.reshape(1, D)),
        lk2=np.ascontiguousarray(lk2.reshape(1, D)),
        eye=np.eye(128, dtype=np.float32),
        had=_hadamard(128),
        onesc=np.ones((1, 128), dtype=np.float32),
        onesp=np.ones((128, 1), dtype=np.float32),
    )

    xt = x[0].T
    in_maps = []
    for i in range(NCORE):
        m = dict(shared)
        m["xT"] = np.ascontiguousarray(xt[:, i * TOKC:(i + 1) * TOKC])
        in_maps.append(m)

    nc = build_program()
    res = run_bass_kernel_spmd(nc, in_maps, core_ids=list(range(NCORE)))
    out = np.concatenate([np.asarray(r["y"]).T for r in res.results], axis=0)
    return np.ascontiguousarray(out.reshape(1, 8192, DIM))
